# version 07
# baseline (speedup 1.0000x reference)
import numpy as np
import ml_dtypes
from contextlib import ExitStack
import concourse.bass as bass
import concourse.mybir as mybir
from concourse.bass_utils import run_bass_kernel_spmd

F32 = mybir.dt.float32
BF16 = mybir.dt.bfloat16
F16 = mybir.dt.float16
I32 = mybir.dt.int32
AF = mybir.ActivationFunctionType
ALU = mybir.AluOpType
AX = mybir.AxisListType.X

NCORES = 8
T = 1024
D = 4096
DC = D // 128
EI = 7168
EIC = EI // 128
NEXP = 16
INTER = 448
BF = ml_dtypes.bfloat16
BIG = 1.0e30
ATT_SHIFT = 15.0



def wblocks_oi(Wm):
    O, I = Wm.shape
    W4 = np.asarray(Wm, np.float32).reshape(O // 128, 128, I // 128, 128)
    return np.ascontiguousarray(W4.transpose(0, 3, 2, 1)).reshape(O // 128, 128, I)


def wblocks_io(Wm):
    I, O = Wm.shape
    W4 = np.asarray(Wm, np.float32).reshape(I // 128, 128, O // 128, 128)
    return np.ascontiguousarray(W4.transpose(2, 1, 0, 3)).reshape(O // 128, 128, I)


def host_consts():
    return {
        "onesb_d": np.ones((128, 1), dtype=BF),
        "onesf_d": np.ones((1, 128), dtype=np.float32),
        "identf_d": np.eye(128, dtype=np.float32),
        "ones128_d": np.ones((128, 128), dtype=np.float32),
        "fconst_d": np.broadcast_to(
            np.array([1e-5, -15.0, 1e-6, 1.0], np.float32), (128, 4)).copy(),
    }



class Prog:
    def __init__(self):
        self.nc = bass.Bass("TRN2", target_bir_lowering=False, debug=False)
        self.ctx = ExitStack()
        self.c = {}
        self.sems = {}
        self.block = None
        self.H = {}

    def open(self):
        for s in ("gp", "sp", "pe", "act", "dve"):
            self.sems[s] = self.ctx.enter_context(self.nc.semaphore(s))
            self.c[s] = 0
        self.block = self.ctx.enter_context(self.nc.Block())
        return self

    def close(self):
        self.ctx.close()

    def dram_in(self, name, shape, dtype):
        self.H[name] = self.nc.dram_tensor(name, shape, dtype, kind="ExternalInput").ap()
        return self.H[name]

    def dram_out(self, name, shape, dtype):
        self.H[name] = self.nc.dram_tensor(name, shape, dtype, kind="ExternalOutput").ap()
        return self.H[name]

    def dram_int(self, name, shape, dtype):
        self.H[name] = self.nc.dram_tensor(name, shape, dtype, kind="Internal").ap()
        return self.H[name]

    def sb(self, name, shape, dtype, ctx=None):
        self._n = getattr(self, "_n", 0) + 1
        return (ctx or self.ctx).enter_context(
            self.nc.sbuf_tensor(f"{name}_{self._n}", shape, dtype))

    def scope(self):
        return ExitStack()

    def ps(self, name, shape, dtype=F32):
        return self.ctx.enter_context(self.nc.psum_tensor(name, shape, dtype))

    def inc(self, inst, name, n=1):
        inst.then_inc(self.sems[name], n)
        self.c[name] += n
        return self.c[name]

    def gp(self, f):
        self.block.gpsimd(f)

    def sp(self, f):
        self.block.sync(f)

    def pe(self, f):
        self.block.tensor(f)

    def sc(self, f):
        self.block.scalar(f)

    def ve(self, f):
        self.block.vector(f)


def W(p, eng, name, val):
    eng.wait_ge(p.sems[name], val)


def vchain(p, eng, inst):
    v = p.inc(inst, "dve")
    eng.wait_ge(p.sems["dve"], v)
    return v



def alloc_common(p):
    S = {}
    p.dram_in("onesb_d", [128, 1], BF16)
    p.dram_in("onesf_d", [1, 128], F32)
    p.dram_in("identf_d", [128, 128], F32)
    p.dram_in("fconst_d", [128, 4], F32)
    p.dram_in("ones128_d", [128, 128], F32)
    S["p"] = [p.ps(f"pb{i}", [128, 512], F32) for i in range(8)]
    S["xln"] = p.sb("xln", [128, DC * T], BF16)
    S["onesb"] = p.sb("onesb", [128, 1], BF16)
    S["onesf"] = p.sb("onesf", [1, 128], F32)
    S["identf"] = p.sb("identf", [128, 128], F32)
    S["sm1"] = p.sb("sm1", [128, 192], F32)
    S["fc"] = p.sb("fc", [128, 4], F32)
    S["ones128"] = p.sb("ones128", [128, 128], F32)
    S["brow"] = p.sb("brow", [128, 512], F32)
    S["rks"] = p.sb("rks", [1, 1], I32)

    def f(eng):
        p.inc(eng.dma_start(out=S["onesb"][:, :], in_=p.H["onesb_d"]), "sp", 16)
        p.inc(eng.dma_start(out=S["onesf"][:, :], in_=p.H["onesf_d"]), "sp", 16)
        p.inc(eng.dma_start(out=S["identf"][:, :], in_=p.H["identf_d"]), "sp", 16)
        p.inc(eng.dma_start(out=S["fc"][:, :], in_=p.H["fconst_d"]), "sp", 16)
        S["_c_ld"] = p.inc(eng.dma_start(out=S["ones128"][:, :], in_=p.H["ones128_d"]), "sp", 16)
        if "myrank" in p.H:
            S["_c_ld"] = p.inc(eng.dma_start(out=S["rks"][:, :], in_=p.H["myrank"]), "sp", 16)
    p.sp(f)

    def fz(eng):
        W(p, eng, "sp", S["_c_ld"])
        S["_броу"] = p.inc(eng.memset(S["brow"][:, :], 0.0), "dve")
    p.ve(fz)
    return S


def alloc_ln(p, S, ctx, gate=False):
    S["xst"] = p.sb("xst", [128, 2 * T], F32, ctx)
    S["row"] = p.sb("row", [1, 4 * T], F32, ctx)
    S["bca"] = p.sb("bca", [128, T], F32, ctx)
    S["bcb"] = p.sb("bcb", [128, T], F32, ctx)
    S["xbf"] = p.sb("xbf", [128, 2 * T], BF16, ctx)
    S["sq"] = p.sb("sq", [128, 2 * T], BF16, ctx)
    S["ts1"] = p.sb("ts1", [128, 2 * T], F32, ctx)
    if gate:
        S["xlo"] = p.sb("xlo", [128, 2 * T], BF16, ctx)
        S["gwf"] = p.sb("gwf", [128, DC * NEXP], F32, ctx)
        S["gwhi"] = p.sb("gwhi", [128, DC * NEXP], BF16, ctx)
        S["gwlo"] = p.sb("gwlo", [128, DC * NEXP], BF16, ctx)
        S["lg"] = p.sb("lg", [16, T], F32, ctx)
        S["lt"] = p.sb("lt", [128, 8 * 16], F32, ctx)
        S["wtm"] = p.sb("wtm", [128, 8 * 16], F32, ctx)



def emit_ln(p, S, x_src, perm_stride=False, gate=False, start_gate=()):
    P = S["p"]
    xst, xbf, sq, xln = S["xst"], S["xbf"], S["sq"], S["xln"]
    row, bca, bcb, ts1 = S["row"], S["bca"], S["bcb"], S["ts1"]
    ms = {}
    pe_done, sc_done, dma1, dma2, n_done, glog = {}, {}, {}, {}, {}, {}

    def xrow_ap(c):
        if not perm_stride:
            return x_src[c * 128:(c + 1) * 128, :]
        j, r = c // 4, c % 4
        off = (8 * 128 * r + j) * T
        return bass.AP(x_src.tensor, off, [[8 * T, 128], [1, T]])

    for c in range(DC):
        buf = c % 2

        def f_sp(eng, c=c):
            if c == 0:
                for sgn, sgv in start_gate:
                    W(p, eng, sgn, sgv() if callable(sgv) else sgv)
            if c >= 2:
                W(p, eng, "act", sc_done[c - 2])
            dma1[c] = p.inc(eng.dma_start(
                out=xst[:, (c % 2) * T:((c % 2) + 1) * T], in_=xrow_ap(c)), "sp", 16)
        p.sp(f_sp)

        def f_sc(eng, c=c, buf=buf):
            W(p, eng, "sp", dma1[c])
            if c >= 2:
                W(p, eng, "pe", pe_done[c - 2])
            src = xst[:, (c % 2) * T:((c % 2) + 1) * T]
            eng.activation(xbf[:, buf * T:(buf + 1) * T], src, AF.Copy)
            sc_done[c] = p.inc(eng.activation(sq[:, buf * T:(buf + 1) * T], src, AF.Square), "act")
        p.sc(f_sc)

        def f_pe(eng, c=c, buf=buf):
            W(p, eng, "act", sc_done[c])
            st, spp = (c == 0), (c == DC - 1)
            ob = S["onesb"][:, :]
            eng.matmul(P[0][0:1, :], ob, xbf[:, buf * T:buf * T + 512], start=st, stop=spp)
            eng.matmul(P[1][0:1, :], ob, xbf[:, buf * T + 512:(buf + 1) * T], start=st, stop=spp)
            eng.matmul(P[2][0:1, :], ob, sq[:, buf * T:buf * T + 512], start=st, stop=spp)
            i = eng.matmul(P[3][0:1, :], ob, sq[:, buf * T + 512:(buf + 1) * T], start=st, stop=spp)
            pe_done[c] = p.inc(i, "pe")
        p.pe(f_pe)

    def f_v1(eng):
        W(p, eng, "pe", pe_done[DC - 1])
        eng.tensor_scalar_mul(row[:, 0:512], P[0][0:1, :], 1.0 / D)
        eng.tensor_scalar_mul(row[:, 512:T], P[1][0:1, :], 1.0 / D)
        eng.tensor_scalar_mul(row[:, T:T + 512], P[2][0:1, :], 1.0 / D)
        vchain(p, eng, eng.tensor_scalar_mul(row[:, T + 512:2 * T], P[3][0:1, :], 1.0 / D))
        vchain(p, eng, eng.tensor_mul(row[:, 2 * T:3 * T], row[:, 0:T], row[:, 0:T]))
        ms["var"] = p.inc(eng.tensor_sub(row[:, T:2 * T], row[:, T:2 * T], row[:, 2 * T:3 * T]), "dve")
    p.ve(f_v1)

    def f_s1(eng):
        W(p, eng, "dve", ms["var"])
        ms["std"] = p.inc(eng.activation(row[:, 2 * T:3 * T], row[:, T:2 * T], AF.Sqrt,
                                         bias=S["fc"][0:1, 0:1]), "act")
    p.sc(f_s1)

    def f_v2(eng):
        W(p, eng, "act", ms["std"])
        vchain(p, eng, eng.reciprocal(row[:, 3 * T:4 * T], row[:, 2 * T:3 * T]))
        ms["mr"] = p.inc(eng.tensor_mul(row[:, 2 * T:3 * T], row[:, 0:T], row[:, 3 * T:4 * T]), "dve")
    p.ve(f_v2)

    srcs = [3 * T, 3 * T + 512, 2 * T, 2 * T + 512]
    dsts = [(bca, 0), (bca, 512), (bcb, 0), (bcb, 512)]
    for bi in range(4):
        def f_brow(eng, bi=bi):
            if bi == 0:
                W(p, eng, "dve", ms["mr"])
            else:
                W(p, eng, "pe", ms[("bcmm", bi - 1)])
            ms[("brow", bi)] = vchain(p, eng, eng.tensor_copy(
                S["brow"][0:1, :], row[0:1, srcs[bi]:srcs[bi] + 512]))
        p.ve(f_brow)

        def f_bc(eng, bi=bi):
            W(p, eng, "dve", ms[("brow", bi)])
            ms[("bcmm", bi)] = p.inc(eng.matmul(P[bi][:, :], S["ones128"][:, :],
                                                S["brow"][:, :], start=True, stop=True), "pe")
        p.pe(f_bc)

        def f_bcv(eng, bi=bi):
            W(p, eng, "pe", ms[("bcmm", bi)])
            dst, off = dsts[bi]
            v = p.inc(eng.tensor_copy(dst[:, off:off + 512], P[bi][:, :]), "dve")
            if bi == 3:
                ms["bcev"] = v
        p.ve(f_bcv)

    for c in range(DC):
        def f_sp2(eng, c=c):
            if c < 2:
                W(p, eng, "act", sc_done[DC - 2 + c])
            else:
                W(p, eng, "dve", n_done[c - 2])
            dma2[c] = p.inc(eng.dma_start(
                out=xst[:, (c % 2) * T:((c % 2) + 1) * T], in_=xrow_ap(c)), "sp", 16)
        p.sp(f_sp2)

        if not gate:
            def f_vn(eng, c=c):
                W(p, eng, "sp", dma2[c])
                if c == 0:
                    W(p, eng, "dve", ms["bcev"])
                src = xst[:, (c % 2) * T:((c % 2) + 1) * T]
                vchain(p, eng, eng.tensor_mul(ts1[:, 0:T], src, bca[:, 0:T]))
                n_done[c] = p.inc(eng.tensor_sub(xln[:, c * T:(c + 1) * T], ts1[:, 0:T], bcb[:, 0:T]), "dve")
            p.ve(f_vn)
        else:
            xlo = S["xlo"]

            def f_vng(eng, c=c):
                W(p, eng, "sp", dma2[c])
                if c == 0:
                    W(p, eng, "dve", ms["bcev"])
                if c >= 2:
                    W(p, eng, "pe", glog[c - 2])
                buf = c % 2
                src = xst[:, (c % 2) * T:((c % 2) + 1) * T]
                vchain(p, eng, eng.tensor_mul(ts1[:, 0:T], src, bca[:, 0:T]))
                vchain(p, eng, eng.tensor_sub(ts1[:, T:2 * T], ts1[:, 0:T], bcb[:, 0:T]))
                vchain(p, eng, eng.tensor_copy(xln[:, c * T:(c + 1) * T], ts1[:, T:2 * T]))
                n_done[c] = p.inc(eng.tensor_sub(xlo[:, buf * T:(buf + 1) * T],
                                                 ts1[:, T:2 * T], xln[:, c * T:(c + 1) * T]), "dve")
            p.ve(f_vng)

            def f_peg(eng, c=c):
                W(p, eng, "dve", n_done[c])
                buf = c % 2
                st, spp = (c == 0), (c == DC - 1)
                gh, gl = S["gwhi"], S["gwlo"]
                i = None
                for tb in range(2):
                    xh = xln[:, c * T + tb * 512:c * T + (tb + 1) * 512]
                    xl = xlo[:, buf * T + tb * 512:buf * T + (tb + 1) * 512]
                    eng.matmul(P[4 + tb][0:16, :], gh[:, c * 16:(c + 1) * 16], xh, start=st, stop=False)
                    eng.matmul(P[4 + tb][0:16, :], gl[:, c * 16:(c + 1) * 16], xh, start=False, stop=False)
                    i = eng.matmul(P[4 + tb][0:16, :], gh[:, c * 16:(c + 1) * 16], xl, start=False, stop=spp)
                glog[c] = p.inc(i, "pe")
            p.pe(f_peg)

    ms["xln"] = lambda: n_done[DC - 1]
    ms["glog"] = (lambda: glog[DC - 1]) if gate else None
    return ms



def emit_gating(p, S, ms):
    P = S["p"]
    lg, lt, wtm, wfm, sm1 = S["lg"], S["lt"], S["wtm"], S["wfm"], S["sm1"]
    g = {}

    def f_ev(eng):
        W(p, eng, "pe", ms["glog"]())
        eng.memset(wfm[:, :], 0.0)
        eng.tensor_copy(lg[:, 0:512], P[4][0:16, :])
        g["lg"] = p.inc(eng.tensor_copy(lg[:, 512:T], P[5][0:16, :]), "dve")
    p.ve(f_ev)

    tpmm, tpev = {}, {}
    for tc in range(8):
        def f_tp(eng, tc=tc):
            if tc == 0:
                W(p, eng, "dve", g["lg"])
            bank = P[6] if tc % 2 == 0 else P[7]
            if tc >= 2:
                W(p, eng, "dve", tpev[tc - 2])
            tpmm[tc] = p.inc(eng.transpose(bank[:, 0:16], lg[0:16, tc * 128:(tc + 1) * 128],
                                           S["identf"][0:16, 0:16]), "pe")
        p.pe(f_tp)

        def f_tpv(eng, tc=tc):
            W(p, eng, "pe", tpmm[tc])
            bank = P[6] if tc % 2 == 0 else P[7]
            tpev[tc] = p.inc(eng.tensor_copy(lt[:, tc * 16:(tc + 1) * 16], bank[:, 0:16]), "dve")
        p.ve(f_tpv)

    sexp = {}
    for tc in range(8):
        def f_tk(eng, tc=tc):
            W(p, eng, "dve", tpev[tc])
            L = lt[:, tc * 16:(tc + 1) * 16]
            m1, nm1, m2 = sm1[:, 0:1], sm1[:, 1:2], sm1[:, 2:3]
            eq, lm, keep = sm1[:, 16:32], sm1[:, 32:48], sm1[:, 48:64]
            eng.tensor_reduce(m1, L, AX, ALU.max)
            vchain(p, eng, eng.tensor_reduce(nm1, L, AX, ALU.max, negate=True))
            vchain(p, eng, eng.tensor_single_scalar(eq, L, m1, ALU.is_equal))
            vchain(p, eng, eng.scalar_tensor_tensor(lm, eq, -BIG, L, ALU.mult, ALU.add))
            vchain(p, eng, eng.tensor_reduce(m2, lm, AX, ALU.max))
            g[("keep", tc)] = p.inc(eng.tensor_single_scalar(keep, L, m2, ALU.is_ge), "dve")
        p.ve(f_tk)

        def f_te(eng, tc=tc):
            W(p, eng, "dve", g[("keep", tc)])
            sexp[tc] = p.inc(eng.activation(sm1[:, 80:96], lt[:, tc * 16:(tc + 1) * 16],
                                            AF.Exp, bias=sm1[:, 1:2]), "act")
        p.sc(f_te)

        def f_tk2(eng, tc=tc):
            W(p, eng, "act", sexp[tc])
            keep, wk = sm1[:, 48:64], sm1[:, 64:80]
            vchain(p, eng, eng.tensor_mul(wk, sm1[:, 80:96], keep))
            vchain(p, eng, eng.tensor_reduce(sm1[:, 3:4], wk, AX, ALU.add))
            vchain(p, eng, eng.reciprocal(sm1[:, 4:5], sm1[:, 3:4]))
            g[("w", tc)] = p.inc(eng.tensor_scalar_mul(wtm[:, tc * 16:(tc + 1) * 16], wk, sm1[:, 4:5]), "dve")
        p.ve(f_tk2)

    for tc in range(8):
        def f_wt(eng, tc=tc):
            W(p, eng, "dve", g[("w", tc)])
            bank = P[6] if tc % 2 == 0 else P[7]
            if tc >= 2:
                W(p, eng, "dve", g[("wf", tc - 2)])
            g[("wt", tc)] = p.inc(eng.transpose(bank[0:16, 0:128], wtm[:, tc * 16:(tc + 1) * 16],
                                                S["identf"][:, :]), "pe")
        p.pe(f_wt)

        def f_wv(eng, tc=tc):
            W(p, eng, "pe", g[("wt", tc)])
            bank = P[6] if tc % 2 == 0 else P[7]
            g[("wf", tc)] = p.inc(eng.tensor_copy(wfm[0:16, tc * 128:(tc + 1) * 128],
                                                  bank[0:16, 0:128]), "dve")
        p.ve(f_wv)

    return {"wfm": lambda: g[("wf", 7)], "sexp": lambda: sexp[7],
            "wt": lambda: g[("wt", 7)]}



def alloc_ffn0_outer(p, S, ctx):
    S["hsl"] = p.sb("hsl", [128, EIC * 512], BF16, ctx)
    S["wfm"] = p.sb("wfm", [128, T], BF16, ctx)


def alloc_ffn0_body(p, S, ctx):
    S["h1st"] = p.sb("h1st", [128, 2 * 512], BF16, ctx)
    S["wexp"] = p.sb("wexp", [128, 2 * T], BF16, ctx)
    S["expTs"] = p.sb("expTs", [128, EI], BF16, ctx)
    S["sg"] = p.sb("sg", [128, 2 * T], F32, ctx)
    S["hv"] = p.sb("hv", [128, T], F32, ctx)


def alloc_ffn0_ug(p, S, ctx):
    S["wu"] = p.sb("wu", [128, 2 * D], BF16, ctx)
    S["wg"] = p.sb("wg", [128, 2 * D], BF16, ctx)


def alloc_ffn0_down(p, S, ctx):
    S["wd"] = p.sb("wd", [128, 2 * EI], BF16, ctx)
    S["eps"] = p.sb("eps", [128, 2 * 512], F32, ctx)
    S["xep"] = p.sb("xep", [128, 2 * 512], F32, ctx)
    S["xo"] = p.sb("xo", [128, 2 * 512], F32, ctx)


def decl_ffn0_inputs(p):
    p.dram_in("u_blk", [EIC, 128, D], F32)
    p.dram_in("g_blk", [EIC, 128, D], F32)
    p.dram_in("d_blk", [DC, 128, EI], F32)
    p.dram_in("gwT_blk", [128, DC * NEXP], F32)
    p.dram_in("expT", [128, EI], BF16)
    p.dram_int("h1s", [EIC, 128, 512], BF16)


def emit_ffn0(p, S, x_src, x_dst, dbg=False):
    P = S["p"]
    u_blk, g_blk, d_blk = p.H["u_blk"], p.H["g_blk"], p.H["d_blk"]
    h1s = p.H["h1s"]

    outer_ctx = p.scope()
    alloc_ffn0_outer(p, S, outer_ctx)
    ln_ctx = p.scope()
    alloc_ln(p, S, ln_ctx, gate=True)

    gwld = {}

    def f_gw_ld(eng):
        gwld["ld"] = p.inc(eng.dma_start(out=S["gwf"][:, :], in_=p.H["gwT_blk"]), "sp", 16)
    p.sp(f_gw_ld)

    def f_gw_v(eng):
        W(p, eng, "sp", gwld["ld"])
        vchain(p, eng, eng.tensor_copy(S["gwhi"][:, :], S["gwf"][:, :]))
        gwld["hl"] = p.inc(eng.tensor_sub(S["gwlo"][:, :], S["gwf"][:, :], S["gwhi"][:, :]), "dve")
    p.ve(f_gw_v)

    ms = emit_ln(p, S, x_src, gate=True)
    gat = emit_gating(p, S, ms)
    wfm_done = gat["wfm"]
    if dbg:
        def f_dbg1(eng):
            W(p, eng, "dve", wfm_done())
            p.inc(eng.dma_start(out=p.H["xln_dbg"], in_=S["xln"][:, :]), "sp", 16)
            p.inc(eng.dma_start(out=p.H["wfm_dbg"], in_=S["wfm"][:, :]), "sp", 16)
        p.sp(f_dbg1)
    ln_ctx.close()
    body_ctx = p.scope()
    alloc_ffn0_body(p, S, body_ctx)
    ug_ctx = p.scope()
    alloc_ffn0_ug(p, S, ug_ctx)

    wu, wg, wexp, sg, hsl, h1st, hv = (S["wu"], S["wg"], S["wexp"], S["sg"],
                                       S["hsl"], S["h1st"], S["hv"])
    wld, ug_done, sil_done, hv_done, h1_store = {}, {}, {}, {}, {}
    wexp_ms = {}

    for e in range(EIC):
        buf = e % 2
        par = e % 2

        def f_gp(eng, e=e, buf=buf):
            if e == 0:
                W(p, eng, "dve", wfm_done())
                W(p, eng, "act", gat["sexp"]())
                W(p, eng, "pe", gat["wt"]())
                p.inc(eng.dma_start(out=S["expTs"][:, :], in_=p.H["expT"]), "gp", 16)
            if e >= 2:
                W(p, eng, "pe", ug_done[e - 2])
            p.inc(eng.dma_start(out=wu[:, buf * D:(buf + 1) * D], in_=u_blk[e, :, :]), "gp", 16)
            wld[e] = p.inc(eng.dma_start(out=wg[:, buf * D:(buf + 1) * D], in_=g_blk[e, :, :]), "gp", 16)
        p.gp(f_gp)

        def f_pe_a(eng, e=e, buf=buf, par=par):
            W(p, eng, "gp", wld[e])
            pu = [P[par * 4 + 0], P[par * 4 + 1]]
            pg = [P[par * 4 + 2], P[par * 4 + 3]]
            if e >= 2:
                W(p, eng, "act", sil_done[e - 2])
                W(p, eng, "dve", hv_done[e - 2])
            if e == 0:
                W(p, eng, "dve", wfm_done())
            i = None
            for tb in range(2):
                i = eng.matmul(pu[tb][:, :], S["expTs"][:, e * 128:(e + 1) * 128],
                               S["wfm"][:, tb * 512:(tb + 1) * 512], start=True, stop=True)
            wexp_ms[("mm", e)] = p.inc(i, "pe")
            for tb in range(2):
                for c in range(DC):
                    eng.matmul(pg[tb][:, :], wg[:, buf * D + c * 128:buf * D + (c + 1) * 128],
                               S["xln"][:, c * T + tb * 512:c * T + (tb + 1) * 512],
                               start=(c == 0), stop=(c == DC - 1))
        p.pe(f_pe_a)

        def f_vwx(eng, e=e, par=par):
            W(p, eng, "pe", wexp_ms[("mm", e)])
            pu = [P[par * 4 + 0], P[par * 4 + 1]]
            eng.tensor_copy(wexp[:, par * T:par * T + 512], pu[0][:, :])
            wexp_ms[("ev", e)] = p.inc(
                eng.tensor_copy(wexp[:, par * T + 512:(par + 1) * T], pu[1][:, :]), "dve")
        p.ve(f_vwx)

        def f_pe_b(eng, e=e, buf=buf, par=par):
            W(p, eng, "dve", wexp_ms[("ev", e)])
            pu = [P[par * 4 + 0], P[par * 4 + 1]]
            i = None
            for tb in range(2):
                for c in range(DC):
                    i = eng.matmul(pu[tb][:, :], wu[:, buf * D + c * 128:buf * D + (c + 1) * 128],
                                   S["xln"][:, c * T + tb * 512:c * T + (tb + 1) * 512],
                                   start=(c == 0), stop=(c == DC - 1))
            ug_done[e] = p.inc(i, "pe")
        p.pe(f_pe_b)

        def f_sil(eng, e=e, par=par):
            W(p, eng, "pe", ug_done[e])
            pg = [P[par * 4 + 2], P[par * 4 + 3]]
            eng.activation(sg[:, par * T:par * T + 512], pg[0][:, :], AF.Silu)
            sil_done[e] = p.inc(
                eng.activation(sg[:, par * T + 512:(par + 1) * T], pg[1][:, :], AF.Silu), "act")
        p.sc(f_sil)

        def f_hv(eng, e=e, par=par):
            W(p, eng, "act", sil_done[e])
            if e >= 2:
                W(p, eng, "sp", h1_store[e - 2])
            pu = [P[par * 4 + 0], P[par * 4 + 1]]
            eng.tensor_mul(hv[:, 0:512], sg[:, par * T:par * T + 512], pu[0][:, :])
            vchain(p, eng, eng.tensor_mul(hv[:, 512:T], sg[:, par * T + 512:(par + 1) * T], pu[1][:, :]))
            eng.tensor_mul(hsl[:, e * 512:(e + 1) * 512], hv[:, 0:512], wexp[:, par * T:par * T + 512])
            hv_done[e] = p.inc(
                eng.tensor_mul(h1st[:, par * 512:(par + 1) * 512], hv[:, 512:T],
                               wexp[:, par * T + 512:(par + 1) * T]), "dve")
        p.ve(f_hv)

        def f_h1(eng, e=e, par=par):
            W(p, eng, "dve", hv_done[e])
            h1_store[e] = p.inc(eng.dma_start(out=h1s[e, :, :],
                                              in_=h1st[:, par * 512:(par + 1) * 512]), "sp", 16)
        p.sp(f_h1)

    if dbg:
        def f_dbg2(eng):
            W(p, eng, "dve", hv_done[EIC - 1])
            p.inc(eng.dma_start(out=p.H["hsl_dbg"], in_=S["hsl"][:, :]), "sp", 16)
        p.sp(f_dbg2)
    ug_ctx.close()
    down_ctx = p.scope()
    alloc_ffn0_down(p, S, down_ctx)
    wd, xln = S["wd"], S["xln"]
    h1b = {}

    def f_h1back(eng):
        W(p, eng, "dve", hv_done[EIC - 1])
        W(p, eng, "sp", h1_store[EIC - 1])
        h1b["v"] = p.inc(eng.dma_start(
            out=xln[:, 0:EIC * 512].rearrange("p (e m) -> p e m", e=EIC),
            in_=h1s.rearrange("e p m -> p e m")), "sp", 16)
    p.sp(f_h1back)

    dwld, dn_done, ep_s, ep_v, ep_x, ep_st = {}, {}, {}, {}, {}, {}
    for o in range(DC):
        buf = o % 2

        def f_gp2(eng, o=o, buf=buf):
            if o < 2:
                W(p, eng, "pe", ug_done[EIC - 1])
            else:
                W(p, eng, "pe", dn_done[(o - 2, 1)])
            dwld[o] = p.inc(eng.dma_start(out=wd[:, buf * EI:(buf + 1) * EI], in_=d_blk[o, :, :]), "gp", 16)
        p.gp(f_gp2)

        for tb in range(2):
            idx = o * 2 + tb
            bank = idx % 4

            def f_spx(eng, idx=idx, o=o, tb=tb):
                if idx >= 2:
                    W(p, eng, "dve", ep_v[idx - 2])
                ep_x[idx] = p.inc(eng.dma_start(
                    out=S["xep"][:, (idx % 2) * 512:((idx % 2) + 1) * 512],
                    in_=x_src[o * 128:(o + 1) * 128, tb * 512:(tb + 1) * 512]), "sp", 16)
            p.sp(f_spx)

            def f_pe2(eng, o=o, tb=tb, idx=idx, bank=bank, buf=buf):
                W(p, eng, "gp", dwld[o])
                if idx == 0:
                    W(p, eng, "sp", h1b["v"])
                if idx >= 4:
                    W(p, eng, "act", ep_s[idx - 4])
                i = None
                for e in range(EIC):
                    rhs = (hsl[:, e * 512:(e + 1) * 512] if tb == 0
                           else xln[:, e * 512:(e + 1) * 512])
                    i = eng.matmul(P[bank][:, :], wd[:, buf * EI + e * 128:buf * EI + (e + 1) * 128],
                                   rhs, start=(e == 0), stop=(e == EIC - 1))
                dn_done[(o, tb)] = p.inc(i, "pe")
            p.pe(f_pe2)

            def f_eps(eng, idx=idx, bank=bank):
                W(p, eng, "pe", dn_done[(idx // 2, idx % 2)])
                if idx >= 2:
                    W(p, eng, "dve", ep_v[idx - 2])
                ep_s[idx] = p.inc(eng.activation(S["eps"][:, (idx % 2) * 512:((idx % 2) + 1) * 512],
                                                 P[bank][:, :], AF.Copy, scale=0.5), "act")
            p.sc(f_eps)

            def f_epv(eng, idx=idx):
                W(p, eng, "act", ep_s[idx])
                W(p, eng, "sp", ep_x[idx])
                if idx >= 2:
                    W(p, eng, "sp", ep_st[idx - 2])
                ep_v[idx] = p.inc(eng.tensor_add(
                    S["xo"][:, (idx % 2) * 512:((idx % 2) + 1) * 512],
                    S["eps"][:, (idx % 2) * 512:((idx % 2) + 1) * 512],
                    S["xep"][:, (idx % 2) * 512:((idx % 2) + 1) * 512]), "dve")
            p.ve(f_epv)

            def f_epst(eng, idx=idx, o=o, tb=tb):
                W(p, eng, "dve", ep_v[idx])
                ep_st[idx] = p.inc(eng.dma_start(
                    out=x_dst[o * 128:(o + 1) * 128, tb * 512:(tb + 1) * 512],
                    in_=S["xo"][:, (idx % 2) * 512:((idx % 2) + 1) * 512]), "sp", 16)
            p.sp(f_epst)

    down_ctx.close()
    body_ctx.close()
    outer_ctx.close()
    return lambda: ep_st[DC * 2 - 1]


def emit_barrier(p):
    snap = dict(p.c)
    for secname in ("gp", "sp", "pe", "sc", "ve"):
        def f(eng, snap=snap):
            for nm in ("gp", "sp", "pe", "act", "dve"):
                eng.wait_ge(p.sems[nm], snap[nm])
        getattr(p, secname)(f)


def final_wait(p):
    def f(eng):
        eng.wait_ge(p.sems["sp"], p.c["sp"])
        eng.wait_ge(p.sems["gp"], p.c["gp"])
    p.sp(f)

    def f2(eng):
        eng.wait_ge(p.sems["sp"], p.c["sp"])
    p.gp(f2)



def build_ffn0_prog(dbg=False):
    p = Prog().open()
    x_in = p.dram_in("x_in", [D, T], F32)
    x_out = p.dram_out("x_out", [D, T], F32)
    if dbg:
        p.dram_out("xln_dbg", [128, DC * T], BF16)
        p.dram_out("wfm_dbg", [16, T], BF16)
        p.dram_out("hsl_dbg", [128, EIC * 512], BF16)
    decl_ffn0_inputs(p)
    S = alloc_common(p)
    emit_ffn0(p, S, x_in, x_out, dbg=dbg)
    final_wait(p)
    return p


def prep_ffn0_inputs(inp):
    gw = np.asarray(inp["L0_ffn_gate_w"], np.float32)
    gwT_blk = np.ascontiguousarray(
        gw.reshape(16, DC, 128).transpose(2, 1, 0)).reshape(128, DC * 16)
    expT = np.zeros((128, EI), dtype=BF)
    for e2 in range(NEXP):
        expT[e2, e2 * INTER:(e2 + 1) * INTER] = 1
    return {
        "u_blk": wblocks_oi(inp["L0_ffn_u"]),
        "g_blk": wblocks_oi(inp["L0_ffn_g"]),
        "d_blk": wblocks_oi(inp["L0_ffn_d"]),
        "gwT_blk": gwT_blk,
        "expT": expT,
    }


def run_ffn0(x_full, inp, trace=False, dbg=False):
    pr = build_ffn0_prog(dbg=dbg)
    wmaps = prep_ffn0_inputs(inp)
    consts = host_consts()
    in_maps = []
    for c in range(NCORES):
        shard = np.ascontiguousarray(np.asarray(x_full, np.float32)[c * T:(c + 1) * T, :].T)
        m = {"x_in": shard}
        m.update(wmaps)
        m.update(consts)
        in_maps.append(m)
    res = run_bass_kernel_spmd(pr.nc, in_maps, core_ids=list(range(NCORES)), trace=trace)
    pr.close()
    out = np.empty((NCORES * T, D), np.float32)
    for c in range(NCORES):
        out[c * T:(c + 1) * T, :] = res.results[c]["x_out"].T
    return out, res





def decl_attn0_inputs(p):
    p.dram_in("wqsel_blk", [8, 128, D], F32)
    p.dram_in("bq_sel", [128, 8], F32)
    p.dram_in("m_in", [8, 128, 128], F32)
    p.dram_in("z_in", [128, 8], F32)
    p.dram_in("wo_blk", [DC, 128, D], F32)


def alloc_attn0_outer(p, S, ctx):
    S["rfm"] = p.sb("rfm", [128, 8 * T], BF16, ctx)


def alloc_attn0_qr(p, S, ctx):
    S["wq"] = p.sb("wq", [128, 2 * D], BF16, ctx)
    S["sigq"] = p.sb("sigq", [128, 8 * T], BF16, ctx)
    S["bqs"] = p.sb("bqs", [128, 8], F32, ctx)
    S["sia"] = p.sb("sia", [128, 2 * 512], F32, ctx)
    S["sib"] = p.sb("sib", [128, 2 * 512], F32, ctx)
    S["sic"] = p.sb("sic", [128, 2 * 512], F32, ctx)
    S["maug"] = p.sb("maug", [128, 8 * 132], BF16, ctx)
    S["mst"] = p.sb("mst", [128, 128], F32, ctx)
    S["zst"] = p.sb("zst", [128, 8], F32, ctx)
    S["rtm"] = p.sb("rtm", [128, 2 * 128], F32, ctx)


def alloc_attn0_o(p, S, ctx):
    S["wo"] = p.sb("wo", [128, 2 * D], BF16, ctx)
    S["eps"] = p.sb("eps", [128, 2 * 512], F32, ctx)
    S["xep"] = p.sb("xep", [128, 2 * 512], F32, ctx)
    S["xo"] = p.sb("xo", [128, 2 * 512], F32, ctx)


def emit_attn0(p, S, x_src, x_dst):
    P = S["p"]
    outer_ctx = p.scope()
    alloc_attn0_outer(p, S, outer_ctx)
    ln_ctx = p.scope()
    alloc_ln(p, S, ln_ctx, gate=False)
    ms = emit_ln(p, S, x_src, gate=False)
    xln_done = ms["xln"]
    ln_ctx.close()

    q_ctx = p.scope()
    alloc_attn0_qr(p, S, q_ctx)
    wq_blk, wo_blk = p.H["wqsel_blk"], p.H["wo_blk"]
    sigq, wq = S["sigq"], S["wq"]
    qwld, qmm, qv1, qs, qsig = {}, {}, {}, {}, {}

    for h in range(8):
        buf = h % 2

        def f_gp(eng, h=h, buf=buf):
            if h == 0:
                W(p, eng, "dve", xln_done())
                p.inc(eng.dma_start(out=S["bqs"][:, :], in_=p.H["bq_sel"]), "gp", 16)
            if h >= 2:
                W(p, eng, "pe", qmm[(h - 2, 1)])
            qwld[h] = p.inc(eng.dma_start(out=wq[:, buf * D:(buf + 1) * D],
                                          in_=wq_blk[h, :, :]), "gp", 16)
        p.gp(f_gp)

        for tb in range(2):
            idx = h * 2 + tb
            bank = idx % 4

            def f_pe(eng, h=h, tb=tb, idx=idx, bank=bank, buf=buf):
                W(p, eng, "gp", qwld[h])
                if idx >= 4:
                    W(p, eng, "dve", qv1[idx - 4])
                i = None
                for c in range(DC):
                    i = eng.matmul(P[bank][:, :], wq[:, buf * D + c * 128:buf * D + (c + 1) * 128],
                                   S["xln"][:, c * T + tb * 512:c * T + (tb + 1) * 512],
                                   start=(c == 0), stop=(c == DC - 1))
                qmm[(h, tb)] = p.inc(i, "pe")
            p.pe(f_pe)

            def f_v1(eng, h=h, tb=tb, idx=idx, bank=bank):
                W(p, eng, "pe", qmm[(h, tb)])
                if idx >= 2:
                    W(p, eng, "dve", qsig[idx - 2])
                par = idx % 2
                bq = S["bqs"][:, h:h + 1]
                eng.tensor_scalar(S["sia"][:, par * 512:(par + 1) * 512], P[bank][:, :],
                                  bq, 0.0, ALU.add, ALU.min)
                qv1[idx] = p.inc(eng.tensor_scalar(S["sib"][:, par * 512:(par + 1) * 512],
                                                   P[bank][:, :], bq, 0.0, ALU.add, ALU.max), "dve")
            p.ve(f_v1)

            def f_s(eng, idx=idx):
                W(p, eng, "dve", qv1[idx])
                par = idx % 2
                qs[idx] = p.inc(eng.activation(S["sic"][:, par * 512:(par + 1) * 512],
                                               S["sia"][:, par * 512:(par + 1) * 512], AF.Exp), "act")
            p.sc(f_s)

            def f_v2(eng, h=h, tb=tb, idx=idx):
                W(p, eng, "act", qs[idx])
                par = idx % 2
                qsig[idx] = p.inc(eng.tensor_add(
                    sigq[:, h * T + tb * 512:h * T + (tb + 1) * 512],
                    S["sic"][:, par * 512:(par + 1) * 512],
                    S["sib"][:, par * 512:(par + 1) * 512]), "dve")
            p.ve(f_v2)

    q_last = qsig[15]
    maug, rfm, rtm = S["maug"], S["rfm"], S["rtm"]
    mld, mcast = {}, {}

    def f_mld(eng):
        mld[0] = p.inc(eng.dma_start(out=S["zst"][:, :], in_=p.H["z_in"]), "gp", 16)
    p.gp(f_mld)

    for h in range(8):
        def f_mh(eng, h=h):
            if h >= 1:
                W(p, eng, "act", mcast[h - 1])
            mld[h + 1] = p.inc(eng.dma_start(out=S["mst"][:, :], in_=p.H["m_in"][h, :, :]), "gp", 16)
        p.gp(f_mh)

        def f_mc(eng, h=h):
            W(p, eng, "gp", mld[h + 1])
            if h == 0:
                W(p, eng, "gp", mld[0])
            eng.activation(maug[:, h * 132:h * 132 + 128], S["mst"][:, :], AF.Copy)
            mcast[h] = p.inc(eng.activation(maug[:, h * 132 + 128:h * 132 + 129],
                                            S["zst"][:, h:h + 1], AF.Copy), "act")
        p.sc(f_mc)

    rmm, rv, rtp, rfv = {}, {}, {}, {}
    for h in range(8):
        for tc in range(8):
            idx = h * 8 + tc
            bank = 4 + (idx % 2)

            def f_rmm(eng, h=h, tc=tc, idx=idx, bank=bank):
                if idx == 0:
                    W(p, eng, "act", mcast[7])
                    W(p, eng, "dve", q_last)
                if idx >= 2:
                    W(p, eng, "dve", rv[idx - 2])
                rmm[idx] = p.inc(eng.matmul(
                    P[bank][:, 0:129],
                    sigq[:, h * T + tc * 128:h * T + (tc + 1) * 128],
                    maug[:, h * 132:h * 132 + 129], start=True, stop=True), "pe")
            p.pe(f_rmm)

            def f_rv(eng, idx=idx, bank=bank):
                W(p, eng, "pe", rmm[idx])
                par = idx % 2
                den = S["sm1"][:, 128:129]
                vchain(p, eng, eng.tensor_scalar_add(den, P[bank][:, 128:129], 1e-6))
                vchain(p, eng, eng.reciprocal(S["sm1"][:, 129:130], den))
                rv[idx] = p.inc(eng.tensor_scalar_mul(
                    rtm[:, par * 128:(par + 1) * 128], P[bank][:, 0:128],
                    S["sm1"][:, 129:130]), "dve")
            p.ve(f_rv)

            def f_rtp(eng, idx=idx, h=h, tc=tc):
                W(p, eng, "dve", rv[idx])
                par = idx % 2
                bank2 = P[6] if idx % 2 == 0 else P[7]
                if idx >= 2:
                    W(p, eng, "dve", rfv[idx - 2])
                rtp[idx] = p.inc(eng.transpose(bank2[:, 0:128], rtm[:, par * 128:(par + 1) * 128],
                                               S["identf"][:, :]), "pe")
            p.pe(f_rtp)

            def f_rfv(eng, idx=idx, h=h, tc=tc):
                W(p, eng, "pe", rtp[idx])
                bank2 = P[6] if idx % 2 == 0 else P[7]
                rfv[idx] = p.inc(eng.tensor_copy(
                    rfm[:, h * T + tc * 128:h * T + (tc + 1) * 128], bank2[:, 0:128]), "dve")
            p.ve(f_rfv)

    r_last = lambda: rfv[63]

    q_ctx.close()
    o_ctx = p.scope()
    alloc_attn0_o(p, S, o_ctx)
    wo = S["wo"]
    owld, omm, ep_s, ep_v, ep_x, ep_st = {}, {}, {}, {}, {}, {}

    for o in range(DC):
        buf = o % 2

        def f_gpo(eng, o=o, buf=buf):
            if o == 0:
                W(p, eng, "dve", r_last())
            if o >= 2:
                W(p, eng, "pe", omm[(o - 2, 1)])
            owld[o] = p.inc(eng.dma_start(out=wo[:, buf * D:(buf + 1) * D],
                                          in_=wo_blk[o, :, :]), "gp", 16)
        p.gp(f_gpo)

        for tb in range(2):
            idx = o * 2 + tb
            bank = idx % 4

            def f_spx(eng, idx=idx, o=o, tb=tb):
                if idx == 0:
                    W(p, eng, "dve", r_last())
                if idx >= 2:
                    W(p, eng, "dve", ep_v[idx - 2])
                ep_x[idx] = p.inc(eng.dma_start(
                    out=S["xep"][:, (idx % 2) * 512:((idx % 2) + 1) * 512],
                    in_=x_src[o * 128:(o + 1) * 128, tb * 512:(tb + 1) * 512]), "sp", 16)
            p.sp(f_spx)

            def f_peo(eng, o=o, tb=tb, idx=idx, bank=bank, buf=buf):
                W(p, eng, "gp", owld[o])
                if idx < 2:
                    W(p, eng, "dve", r_last())
                if idx >= 4:
                    W(p, eng, "act", ep_s[idx - 4])
                i = None
                for cc in range(DC):
                    i = eng.matmul(P[bank][:, :], wo[:, buf * D + cc * 128:buf * D + (cc + 1) * 128],
                                   rfm[:, (cc // 4) * T + tb * 512:(cc // 4) * T + (tb + 1) * 512],
                                   start=(cc == 0), stop=(cc == DC - 1))
                omm[(o, tb)] = p.inc(i, "pe")
            p.pe(f_peo)

            def f_eps(eng, idx=idx, bank=bank):
                W(p, eng, "pe", omm[(idx // 2, idx % 2)])
                if idx >= 2:
                    W(p, eng, "dve", ep_v[idx - 2])
                ep_s[idx] = p.inc(eng.activation(S["eps"][:, (idx % 2) * 512:((idx % 2) + 1) * 512],
                                                 P[bank][:, :], AF.Copy), "act")
            p.sc(f_eps)

            def f_epv(eng, idx=idx):
                W(p, eng, "act", ep_s[idx])
                W(p, eng, "sp", ep_x[idx])
                if idx >= 2:
                    W(p, eng, "sp", ep_st[idx - 2])
                ep_v[idx] = p.inc(eng.tensor_add(
                    S["xo"][:, (idx % 2) * 512:((idx % 2) + 1) * 512],
                    S["eps"][:, (idx % 2) * 512:((idx % 2) + 1) * 512],
                    S["xep"][:, (idx % 2) * 512:((idx % 2) + 1) * 512]), "dve")
            p.ve(f_epv)

            def f_epst(eng, idx=idx, o=o, tb=tb):
                W(p, eng, "dve", ep_v[idx])
                ep_st[idx] = p.inc(eng.dma_start(
                    out=x_dst[o * 128:(o + 1) * 128, tb * 512:(tb + 1) * 512],
                    in_=S["xo"][:, (idx % 2) * 512:((idx % 2) + 1) * 512]), "sp", 16)
            p.sp(f_epst)

    o_ctx.close()
    outer_ctx.close()
    return lambda: ep_st[DC * 2 - 1]


def build_attn0_prog():
    p = Prog().open()
    x_in = p.dram_in("x_in", [D, T], F32)
    x_out = p.dram_out("x_out", [D, T], F32)
    decl_attn0_inputs(p)
    S = alloc_common(p)
    emit_attn0(p, S, x_in, x_out)
    final_wait(p)
    return p


def prep_attn0_inputs(inp):
    Wq = np.asarray(inp["L0_attn_Wq"], np.float32)
    bq = np.asarray(inp["L0_attn_bq"], np.float32)
    sel = np.concatenate([np.arange(4 * k * 128, (4 * k + 1) * 128) for k in range(8)])
    Wq_sel = Wq[sel]
    bq_sel = bq[sel]
    return {
        "wqsel_blk": wblocks_oi(Wq_sel),
        "bq_sel": np.ascontiguousarray(bq_sel.reshape(8, 128).T),
        "m_in": np.asarray(inp["L0_attn_M"], np.float32),
        "z_in": np.ascontiguousarray(np.asarray(inp["L0_attn_z"], np.float32).T),
        "wo_blk": wblocks_oi(np.asarray(inp["L0_attn_Wo"], np.float32)),
    }


def run_attn0(x_full, inp, trace=False):
    pr = build_attn0_prog()
    wmaps = prep_attn0_inputs(inp)
    consts = host_consts()
    in_maps = []
    for c in range(NCORES):
        shard = np.ascontiguousarray(np.asarray(x_full, np.float32)[c * T:(c + 1) * T, :].T)
        m = {"x_in": shard}
        m.update(wmaps)
        m.update(consts)
        in_maps.append(m)
    res = run_bass_kernel_spmd(pr.nc, in_maps, core_ids=list(range(NCORES)), trace=trace)
    pr.close()
    out = np.empty((NCORES * T, D), np.float32)
    for c in range(NCORES):
        out[c * T:(c + 1) * T, :] = res.results[c]["x_out"].T
    return out, res





def decl_ffn1_inputs(p):
    p.dram_in("qw_in", [512, D], I32)
    p.dram_in("qz_in", [32, 512], I32)
    p.dram_in("sc_in", [32, D], F16)
    p.dram_in("gidx_p32", [32, D], F32)
    p.dram_in("gidx_fm", [D, 32], F32)
    p.dram_in("iota32", [32, 1], F32)
    p.dram_in("iotaf", [128, 32], F32)
    p.dram_in("bias_h", [128, 32], F16)
    p.dram_int("wdq", [32, 128, DC * 128], BF16)


def alloc_ffn1_outer(p, S, ctx):
    S["negB"] = p.sb("negB", [128, D], BF16, ctx)
    S["onehT"] = p.sb("onehT", [128, D], BF16, ctx)
    S["onehF"] = p.sb("onehF", [128, DC * 32], BF16, ctx)
    S["xgs"] = p.sb("xgs", [128, T], BF16, ctx)
    S["bh"] = p.sb("bh", [128, 32], F32, ctx)


def alloc_ffn1_deq(p, S, ctx):
    S["scb"] = p.sb("scb", [128, D], BF16, ctx)


def alloc_ffn1_deq_prep(p, S, ctx):
    S["scf"] = p.sb("scf", [32, D], F32, ctx)
    S["zq"] = p.sb("zq", [32, D], I32, ctx)
    S["zqf"] = p.sb("zqf", [32, D], F32, ctx)
    S["qzs"] = p.sb("qzs", [32, 512], I32, ctx)
    S["zsh"] = p.sb("zsh", [32, 512], I32, ctx)
    S["bhf"] = p.sb("bhf", [128, 32], F16, ctx)


def alloc_ffn1_deq_main(p, S, ctx):
    S["qw"] = p.sb("qw", [128, D], I32, ctx)
    S["qn"] = p.sb("qn", [128, D], I32, ctx)
    S["qn2"] = p.sb("qn2", [128, D], I32, ctx)
    S["qfb"] = p.sb("qfb", [128, 2 * 512], BF16, ctx)
    S["wst"] = p.sb("wst", [128, D], BF16, ctx)


def alloc_ffn1_main(p, S, ctx):
    S["wn"] = p.sb("wn", [128, 2 * D], BF16, ctx)
    S["eps"] = p.sb("eps", [128, 2 * 512], F32, ctx)
    S["xep"] = p.sb("xep", [128, 2 * 512], F32, ctx)
    S["xo"] = p.sb("xo", [128, 2 * 512], F32, ctx)


def emit_ffn1(p, S, x_src, x_dst, dequant):
    P = S["p"]
    outer_ctx = p.scope()
    alloc_ffn1_outer(p, S, outer_ctx)

    ohh = {}
    deq_ctx = p.scope()
    if dequant:
        alloc_ffn1_deq(p, S, deq_ctx)
    oh_ctx = p.scope()
    S["gfm"] = p.sb("gfm", [128, DC * 32], F32, oh_ctx)
    S["scf2"] = p.sb("scf2", [32, D], F32, oh_ctx)
    S["io32"] = p.sb("io32", [32, 1], F32, oh_ctx)
    S["iof"] = p.sb("iof", [128, 32], F32, oh_ctx)

    def f_ohld(eng):
        p.inc(eng.dma_start(out=S["gfm"][:, :].rearrange("p (c e) -> p c e", c=DC),
                            in_=p.H["gidx_fm"].rearrange("(c p) e -> p c e", p=128)), "gp", 16)
        p.inc(eng.dma_start(out=S["scf2"][:, :], in_=p.H["gidx_p32"]), "gp", 16)
        p.inc(eng.dma_start(out=S["io32"][:, :], in_=p.H["iota32"]), "gp", 16)
        ohh["ld"] = p.inc(eng.dma_start(out=S["iof"][:, :], in_=p.H["iotaf"]), "gp", 16)
    p.gp(f_ohld)

    def f_oh(eng):
        W(p, eng, "gp", ohh["ld"])
        vchain(p, eng, eng.memset(S["onehT"][:, :], 0.0))
        vchain(p, eng, eng.memset(S["negB"][:, :], 0.0))
        vchain(p, eng, eng.memset(S["xgs"][:, :], 0.0))
        vchain(p, eng, eng.tensor_single_scalar(S["onehT"][0:32, :], S["scf2"][:, :],
                                               S["io32"][:, :], ALU.is_equal))
        for c in range(DC):
            eng.tensor_tensor(S["onehF"][:, c * 32:(c + 1) * 32],
                              S["gfm"][:, c * 32:(c + 1) * 32], S["iof"][:, :], ALU.is_equal)
        ohh["oh"] = p.inc(eng.engine_nop(), "dve")
    p.ve(f_oh)

    if dequant:
        dq, deq_main_ctx = emit_dequant2(p, S, ohh, oh_ctx)
    else:
        dq = {"laststore": lambda: None, "lastev": lambda: None}
        deq_main_ctx = None
        prep_ctx = p.scope()
        alloc_ffn1_deq_prep(p, S, prep_ctx)
        dd = {}

        def f_ld2(eng):
            p.inc(eng.dma_start(out=S["qzs"][:, :], in_=p.H["qz_in"]), "gp", 16)
            p.inc(eng.dma_start(out=S["scf"][:, :], in_=p.H["sc_in"]), "gp", 16)
            dd["ld"] = p.inc(eng.dma_start(out=S["bhf"][:, :], in_=p.H["bias_h"]), "gp", 16)
        p.gp(f_ld2)

        def f_prep2(eng):
            W(p, eng, "gp", dd["ld"])
            for jj in range(8):
                vchain(p, eng, eng.tensor_single_scalar(S["zsh"][:, :], S["qzs"][:, :],
                                                        4 * jj, ALU.logical_shift_right))
                zq_ap = bass.AP(S["zq"], jj, [[S["zq"].ap().ap[0][0], 32], [8, 512]])
                vchain(p, eng, eng.tensor_single_scalar(zq_ap, S["zsh"][:, :], 15, ALU.bitwise_and))
            vchain(p, eng, eng.tensor_copy(S["zqf"][:, :], S["zq"][:, :]))
            vchain(p, eng, eng.tensor_scalar_add(S["zqf"][:, :], S["zqf"][:, :], 1.0))
            vchain(p, eng, eng.tensor_mul(S["zqf"][:, :], S["zqf"][:, :], S["scf"][:, :]))
            vchain(p, eng, eng.tensor_scalar_mul(S["negB"][0:32, :], S["zqf"][:, :], -1.0))
            vchain(p, eng, eng.tensor_copy(S["bh"][:, :], S["bhf"][:, :]))
            dd["prep"] = vchain(p, eng, eng.tensor_scalar_mul(S["bh"][:, :], S["bh"][:, :], 0.5))
        p.ve(f_prep2)
        dq["negb"] = lambda: dd["prep"]
        prep_ctx.close()
        oh_ctx.close()

    ln_ctx = p.scope()
    alloc_ln(p, S, ln_ctx, gate=False)
    ms = emit_ln(p, S, x_src, perm_stride=True, gate=False,
                 start_gate=(("dve", dq["negb"]),) if not dequant else ())
    xln_done = ms["xln"]
    ln_ctx.close()
    if deq_main_ctx is not None:
        deq_main_ctx.close()
    deq_ctx.close()

    xg = {}

    def f_xg(eng):
        W(p, eng, "dve", xln_done())
        W(p, eng, "dve", ohh["oh"])
        lev = dq["lastev"]()
        if lev is not None:
            W(p, eng, "dve", lev)
        i = None
        for tb in range(2):
            for c in range(DC):
                i = eng.matmul(P[6 + tb][0:32, :], S["onehF"][:, c * 32:(c + 1) * 32],
                               S["xln"][:, c * T + tb * 512:c * T + (tb + 1) * 512],
                               start=(c == 0), stop=(c == DC - 1))
        xg["mm"] = p.inc(i, "pe")
    p.pe(f_xg)

    def f_xgv(eng):
        W(p, eng, "pe", xg["mm"])
        eng.tensor_copy(S["xgs"][0:32, 0:512], P[6][0:32, :])
        xg["ev"] = p.inc(eng.tensor_copy(S["xgs"][0:32, 512:T], P[7][0:32, :]), "dve")
    p.ve(f_xgv)

    main_ctx = p.scope()
    alloc_ffn1_main(p, S, main_ctx)
    wn = S["wn"]
    wld, mm_done, ep_s, ep_v, ep_x, ep_st = {}, {}, {}, {}, {}, {}

    for n in range(DC):
        buf = n % 2

        def f_wld(eng, n=n, buf=buf):
            if n == 0:
                ls = dq["laststore"]()
                if ls is not None:
                    W(p, eng, "sp", ls)
                W(p, eng, "dve", xln_done())
            if n >= 2:
                W(p, eng, "pe", mm_done[(n - 2, 1)])
            wld[n] = p.inc(eng.dma_start(out=wn[:, buf * D:(buf + 1) * D],
                                         in_=p.H["wdq"][n, :, :]), "gp", 16)
        p.gp(f_wld)

        for tb in range(2):
            idx = n * 2 + tb
            bank = idx % 4

            def f_spx(eng, idx=idx, n=n, tb=tb):
                if idx == 0:
                    W(p, eng, "dve", xln_done())
                if idx >= 2:
                    W(p, eng, "dve", ep_v[idx - 2])
                ep_x[idx] = p.inc(eng.dma_start(
                    out=S["xep"][:, (idx % 2) * 512:((idx % 2) + 1) * 512],
                    in_=x_src[n * 128:(n + 1) * 128, tb * 512:(tb + 1) * 512]), "sp", 16)
            p.sp(f_spx)

            def f_pem(eng, n=n, tb=tb, idx=idx, bank=bank, buf=buf):
                W(p, eng, "gp", wld[n])
                if idx == 0:
                    W(p, eng, "dve", xg["ev"])
                    W(p, eng, "dve", dq["negb"]())
                if idx >= 4:
                    W(p, eng, "dve", ep_s[idx - 4])
                for k in range(DC):
                    eng.matmul(P[bank][:, :], wn[:, buf * D + k * 128:buf * D + (k + 1) * 128],
                               S["xln"][:, k * T + tb * 512:k * T + (tb + 1) * 512],
                               start=(k == 0), stop=False)
                i = eng.matmul(P[bank][:, :], S["negB"][:, n * 128:(n + 1) * 128],
                               S["xgs"][:, tb * 512:(tb + 1) * 512],
                               start=False, stop=True)
                mm_done[(n, tb)] = p.inc(i, "pe")
            p.pe(f_pem)

            def f_eps(eng, idx=idx, bank=bank, n=n):
                W(p, eng, "pe", mm_done[(n, idx % 2)])
                if idx >= 2:
                    W(p, eng, "dve", ep_v[idx - 2])
                ep_s[idx] = p.inc(eng.tensor_scalar(S["eps"][:, (idx % 2) * 512:((idx % 2) + 1) * 512],
                                                    P[bank][:, :], 0.5, S["bh"][:, n:n + 1],
                                                    ALU.mult, ALU.add), "dve")
            p.ve(f_eps)

            def f_epv(eng, idx=idx):
                W(p, eng, "dve", ep_s[idx])
                W(p, eng, "sp", ep_x[idx])
                if idx >= 2:
                    W(p, eng, "sp", ep_st[idx - 2])
                ep_v[idx] = p.inc(eng.tensor_add(
                    S["xo"][:, (idx % 2) * 512:((idx % 2) + 1) * 512],
                    S["eps"][:, (idx % 2) * 512:((idx % 2) + 1) * 512],
                    S["xep"][:, (idx % 2) * 512:((idx % 2) + 1) * 512]), "dve")
            p.ve(f_epv)

            def f_epst(eng, idx=idx, n=n, tb=tb):
                W(p, eng, "dve", ep_v[idx])
                ep_st[idx] = p.inc(eng.dma_start(
                    out=x_dst[n * 128:(n + 1) * 128, tb * 512:(tb + 1) * 512],
                    in_=S["xo"][:, (idx % 2) * 512:((idx % 2) + 1) * 512]), "sp", 16)
            p.sp(f_epst)

    main_ctx.close()
    outer_ctx.close()
    return lambda: ep_st[DC * 2 - 1]


def emit_dequant2(p, S, ohh, oh_ctx):
    P = S["p"]
    d = {}
    prep_ctx = p.scope()
    alloc_ffn1_deq_prep(p, S, prep_ctx)

    def f_ld(eng):
        p.inc(eng.dma_start(out=S["qzs"][:, :], in_=p.H["qz_in"]), "gp", 16)
        p.inc(eng.dma_start(out=S["scf"][:, :], in_=p.H["sc_in"]), "gp", 16)
        d["ld"] = p.inc(eng.dma_start(out=S["bhf"][:, :], in_=p.H["bias_h"]), "gp", 16)
    p.gp(f_ld)

    def f_prep(eng):
        W(p, eng, "gp", d["ld"])
        vchain(p, eng, eng.memset(S["scb"][:, :], 0.0))
        vchain(p, eng, eng.tensor_copy(S["scb"][0:32, :], S["scf"][:, :]))
        for jj in range(8):
            vchain(p, eng, eng.tensor_single_scalar(S["zsh"][:, :], S["qzs"][:, :],
                                                    4 * jj, ALU.logical_shift_right))
            zq_ap = bass.AP(S["zq"], jj, [[S["zq"].ap().ap[0][0], 32], [8, 512]])
            vchain(p, eng, eng.tensor_single_scalar(zq_ap, S["zsh"][:, :], 15, ALU.bitwise_and))
        vchain(p, eng, eng.tensor_copy(S["zqf"][:, :], S["zq"][:, :]))
        vchain(p, eng, eng.tensor_scalar_add(S["zqf"][:, :], S["zqf"][:, :], 1.0))
        vchain(p, eng, eng.tensor_mul(S["zqf"][:, :], S["zqf"][:, :], S["scf"][:, :]))
        vchain(p, eng, eng.tensor_scalar_mul(S["negB"][0:32, :], S["zqf"][:, :], -1.0))
        vchain(p, eng, eng.tensor_copy(S["bh"][:, :], S["bhf"][:, :]))
        d["prep"] = vchain(p, eng, eng.tensor_scalar_mul(S["bh"][:, :], S["bh"][:, :], 0.5))
    p.ve(f_prep)

    prep_ctx.close()
    oh_ctx.close()
    main_ctx = p.scope()
    alloc_ffn1_deq_main(p, S, main_ctx)
    qld, qsh, wmm, wev, wstd = {}, {}, {}, {}, {}
    for c in range(DC):
        j, r = c // 4, c % 4

        def f_qld(eng, c=c, r=r):
            if c == 0:
                W(p, eng, "dve", d["prep"])
            if c >= 1:
                W(p, eng, "dve", qsh[c - 1])
            qld[c] = p.inc(eng.dma_start(out=S["qw"][:, :],
                                         in_=p.H["qw_in"][128 * r:128 * (r + 1), :]), "sp", 16)
        p.sp(f_qld)

        def f_qsh(eng, c=c, j=j):
            W(p, eng, "sp", qld[c])
            if c >= 1:
                W(p, eng, "dve", wev[(c - 1, 7)])
            vchain(p, eng, eng.tensor_single_scalar(S["qn"][:, :], S["qw"][:, :],
                                                    4 * j, ALU.logical_shift_right))
            qsh[c] = p.inc(eng.tensor_single_scalar(S["qn2"][:, :], S["qn"][:, :], 15,
                                                    ALU.bitwise_and), "dve")
        p.ve(f_qsh)

        for ns in range(8):
            def f_amm(eng, c=c, ns=ns):
                if c == 0 and ns == 0:
                    W(p, eng, "dve", d["prep"])
                prev = wev.get((c, ns - 2)) if ns >= 2 else (wev.get((c - 1, 6 + ns)) if c >= 1 else None)
                if prev is not None:
                    W(p, eng, "dve", prev)
                bank = P[6] if ns % 2 == 0 else P[7]
                wmm[(c, ns)] = p.inc(eng.matmul(bank[:, :], S["onehT"][:, c * 128:(c + 1) * 128],
                                                S["scb"][:, ns * 512:(ns + 1) * 512],
                                                start=True, stop=True), "pe")
            p.pe(f_amm)

            def f_wev(eng, c=c, ns=ns):
                W(p, eng, "pe", wmm[(c, ns)])
                W(p, eng, "dve", qsh[c])
                if ns == 0 and c >= 1:
                    W(p, eng, "sp", wstd[c - 1])
                bank = P[6] if ns % 2 == 0 else P[7]
                par = ns % 2
                vchain(p, eng, eng.tensor_copy(S["qfb"][:, par * 512:(par + 1) * 512],
                                               S["qn2"][:, ns * 512:(ns + 1) * 512]))
                wev[(c, ns)] = p.inc(eng.tensor_mul(S["wst"][:, ns * 512:(ns + 1) * 512],
                                                    S["qfb"][:, par * 512:(par + 1) * 512],
                                                    bank[:, :]), "dve")
            p.ve(f_wev)

        def f_wst(eng, c=c):
            W(p, eng, "dve", wev[(c, 7)])
            for gq in range(8):
                i = eng.dma_start(
                    out=p.H["wdq"][gq * 4:(gq + 1) * 4, :, c * 128:(c + 1) * 128].rearrange("n p m -> p n m"),
                    in_=S["wst"][:, gq * 512:(gq + 1) * 512].rearrange("p (n m) -> p n m", n=4))
                wstd[c] = p.inc(i, "sp", 16)
        p.sp(f_wst)

    d["laststore"] = lambda: wstd[DC - 1]
    d["lastev"] = lambda: wev[(DC - 1, 7)]
    d["negb"] = lambda: d["prep"]
    return d, main_ctx


def build_ffn1_prog():
    p = Prog().open()
    x_in = p.dram_in("x_in", [D, T], F32)
    x_out = p.dram_out("x_out", [D, T], F32)
    decl_ffn1_inputs(p)
    S = alloc_common(p)
    emit_ffn1(p, S, x_in, x_out, dequant=True)
    final_wait(p)
    return p


def perm_kprime():
    idx = np.empty(D, np.int64)
    for c in range(DC):
        j, r = c // 4, c % 4
        pvec = np.arange(128)
        idx[c * 128:(c + 1) * 128] = 8 * (128 * r + pvec) + j
    return idx


def prep_ffn1_inputs(inp):
    gidx = np.asarray(inp["L1_ffn_g_idx"], np.int64)
    kp = perm_kprime()
    gperm = gidx[kp].astype(np.float32)
    bias = np.asarray(inp["L1_ffn_bias"])
    return {
        "qw_in": np.asarray(inp["L1_ffn_qweight"], np.int32),
        "qz_in": np.asarray(inp["L1_ffn_qzeros"], np.int32),
        "sc_in": np.asarray(inp["L1_ffn_scales"], np.float16),
        "gidx_p32": np.ascontiguousarray(np.broadcast_to(gperm, (32, D))),
        "gidx_fm": np.ascontiguousarray(np.broadcast_to(gperm[:, None], (D, 32))),
        "iota32": np.arange(32, dtype=np.float32).reshape(32, 1),
        "iotaf": np.ascontiguousarray(np.broadcast_to(np.arange(32, dtype=np.float32), (128, 32))),
        "bias_h": np.ascontiguousarray(np.asarray(bias, np.float16).reshape(32, 128).T),
    }


def run_ffn1(x_full, inp, trace=False):
    pr = build_ffn1_prog()
    wmaps = prep_ffn1_inputs(inp)
    consts = host_consts()
    in_maps = []
    for c in range(NCORES):
        shard = np.ascontiguousarray(np.asarray(x_full, np.float32)[c * T:(c + 1) * T, :].T)
        m = {"x_in": shard}
        m.update(wmaps)
        m.update(consts)
        in_maps.append(m)
    res = run_bass_kernel_spmd(pr.nc, in_maps, core_ids=list(range(NCORES)), trace=trace)
    pr.close()
    out = np.empty((NCORES * T, D), np.float32)
    for c in range(NCORES):
        out[c * T:(c + 1) * T, :] = res.results[c]["x_out"].T
    return out, res





def decl_qkv1_inputs(p):
    p.dram_in("wq1_blk", [DC, 128, D], F32)
    p.dram_in("wk1_blk", [8, 128, D], F32)
    p.dram_in("wv1_blk", [8, 128, D], F32)
    p.dram_in("cs_in", [128, T], F32)
    p.dram_int("qkvb", [6144, T], BF16)


def alloc_qkv1(p, S, ctx):
    S["wqb"] = p.sb("wqb", [128, 2 * D], BF16, ctx)
    S["css"] = p.sb("css", [128, T], F32, ctx)
    S["qst"] = p.sb("qst", [128, 4 * T], BF16, ctx)
    S["rta"] = p.sb("rta", [128, 512], F32, ctx)
    S["rtb"] = p.sb("rtb", [128, 512], F32, ctx)


def emit_qkv1(p, S, x_src):
    P = S["p"]
    ln_ctx = p.scope()
    alloc_ln(p, S, ln_ctx, gate=False)
    ms = emit_ln(p, S, x_src, gate=False)
    xln_done = ms["xln"]
    ln_ctx.close()
    q_ctx = p.scope()
    alloc_qkv1(p, S, q_ctx)
    qkvb = p.H["qkvb"]
    wld, mmd, rpd, std, sld = {}, {}, {}, {}, {}

    def f_cs(eng):
        W(p, eng, "dve", xln_done())
        sld["cs"] = p.inc(eng.dma_start(out=S["css"][:, :], in_=p.H["cs_in"]), "gp", 16)
    p.gp(f_cs)

    for oc in range(48):
        buf = oc % 2

        def f_w(eng, oc=oc, buf=buf):
            if oc >= 2:
                W(p, eng, "pe", mmd[(oc - 2, 1)])
            if oc < 32:
                src = p.H["wq1_blk"][oc, :, :]
            elif oc < 40:
                src = p.H["wk1_blk"][oc - 32, :, :]
            else:
                src = p.H["wv1_blk"][oc - 40, :, :]
            wld[oc] = p.inc(eng.dma_start(out=S["wqb"][:, buf * D:(buf + 1) * D], in_=src), "gp", 16)
        p.gp(f_w)

        for tb in range(2):
            idx = oc * 2 + tb
            bank = idx % 4

            def f_mm(eng, oc=oc, tb=tb, idx=idx, bank=bank, buf=buf):
                W(p, eng, "gp", wld[oc])
                if idx >= 4:
                    W(p, eng, "dve", rpd[idx - 4])
                i = None
                for c in range(DC):
                    i = eng.matmul(P[bank][:, :], S["wqb"][:, buf * D + c * 128:buf * D + (c + 1) * 128],
                                   S["xln"][:, c * T + tb * 512:c * T + (tb + 1) * 512],
                                   start=(c == 0), stop=(c == DC - 1))
                mmd[(oc, tb)] = p.inc(i, "pe")
            p.pe(f_mm)

            if oc < 40:
                def f_rope(eng, oc=oc, tb=tb, idx=idx, bank=bank):
                    W(p, eng, "pe", mmd[(oc, tb)])
                    if idx == 0:
                        W(p, eng, "gp", sld["cs"])
                    if idx >= 2:
                        W(p, eng, "sp", std[idx - 2])
                    par = idx % 2
                    cos = S["css"][0:64, tb * 512:(tb + 1) * 512]
                    sin = S["css"][64:128, tb * 512:(tb + 1) * 512]
                    dst = S["qst"][:, par * T:par * T + 512]
                    eng.tensor_tensor(S["rta"][0:64, :], P[bank][0:64, :], cos, ALU.mult)
                    vchain(p, eng, eng.tensor_tensor(S["rtb"][0:64, :], P[bank][64:128, :], sin, ALU.mult))
                    eng.tensor_sub(dst[0:64, :], S["rta"][0:64, :], S["rtb"][0:64, :])
                    eng.tensor_tensor(S["rta"][0:64, :], P[bank][0:64, :], sin, ALU.mult)
                    vchain(p, eng, eng.tensor_tensor(S["rtb"][0:64, :], P[bank][64:128, :], cos, ALU.mult))
                    rpd[idx] = p.inc(eng.tensor_add(dst[64:128, :], S["rta"][0:64, :], S["rtb"][0:64, :]), "dve")
                p.ve(f_rope)
            else:
                def f_vcp(eng, oc=oc, tb=tb, idx=idx, bank=bank):
                    W(p, eng, "pe", mmd[(oc, tb)])
                    if idx >= 2:
                        W(p, eng, "sp", std[idx - 2])
                    par = idx % 2
                    rpd[idx] = p.inc(eng.tensor_copy(S["qst"][:, par * T:par * T + 512],
                                                     P[bank][:, :]), "dve")
                p.ve(f_vcp)

            def f_st(eng, oc=oc, tb=tb, idx=idx):
                W(p, eng, "dve", rpd[idx])
                par = idx % 2
                if oc < 32:
                    row0 = oc * 128
                elif oc < 40:
                    row0 = 4096 + (oc - 32) * 128
                else:
                    row0 = 5120 + (oc - 40) * 128
                std[idx] = p.inc(eng.dma_start(
                    out=qkvb[row0:row0 + 128, tb * 512:(tb + 1) * 512],
                    in_=S["qst"][:, par * T:par * T + 512]), "sp", 16)
            p.sp(f_st)

    q_ctx.close()
    return lambda: std[95]




def decl_attn1core_inputs(p):
    p.dram_in("q_in", [512, 8 * T], BF16)
    p.dram_in("k_in", [128, 8 * T], BF16)
    p.dram_in("v_in", [128, 8 * T], BF16)
    p.dram_in("mask_d", [128, 4 * 512], BF16)
    p.dram_out("attn_out", [512, 8 * T], BF16)


def alloc_attn1core(p, S, ctx):
    S["ksl"] = p.sb("ksl", [128, 64 * 128], BF16, ctx)
    S["vfm"] = p.sb("vfm", [128, 64 * 128], BF16, ctx)
    S["vtf"] = p.sb("vtf", [128, 2 * 128], F32, ctx)
    S["vsl"] = p.sb("vsl", [128, 64 * 128], BF16, ctx)
    S["qtl"] = p.sb("qtl", [128, 2 * 512], BF16, ctx)
    S["psc"] = p.sb("psc", [128, 2 * 512], BF16, ctx)
    S["mask"] = p.sb("mask", [128, 4 * 512], BF16, ctx)
    S["aow"] = p.sb("aow", [128, 2 * 512], BF16, ctx)
    S["rbs"] = p.sb("rbs", [128, 2 * 512], F32, ctx)


def emit_attn1core(p, S, gate=(), fused=False):
    P = S["p"]
    core_ctx = p.scope()
    alloc_attn1core(p, S, core_ctx)
    SC = 1.0 / np.sqrt(128.0)
    sld, qld, scd, exd, mkd, pvd, tl = {}, {}, {}, {}, {}, {}, {}

    if fused:
        qall = p.H["qkv_all"]

        def f_slab(eng):
            for sgn, sgv in gate:
                W(p, eng, sgn, sgv() if callable(sgv) else sgv)
            with eng.register("ra") as ra, eng.register("rb") as rb:
                eng.reg_load(ra, S["rks"][0:1, 0:1])
                eng.reg_alu(ra, ra, 128 * T, ALU.mult)
                eng.reg_add(rb, ra, 4096 * T)
                p.inc(eng.dma_start(
                    out=S["ksl"][:, :].rearrange("p (r t) -> p r t", r=NCORES),
                    in_=bass.AP(qall.tensor, rb, [[T, 128], [6144 * T, NCORES], [1, T]])), "gp", 16)
                eng.reg_add(rb, ra, 5120 * T)
                p.inc(eng.dma_start(
                    out=S["vfm"][:, :].rearrange("p (r t) -> p r t", r=NCORES),
                    in_=bass.AP(qall.tensor, rb, [[T, 128], [6144 * T, NCORES], [1, T]])), "gp", 16)
                eng.reg_alu(ra, ra, 4, ALU.mult)
                for j in range(4):
                    eng.reg_add(rb, ra, j * 128 * T)
                    p.inc(eng.dma_start(
                        out=p.H["q_mine"][j * 128:(j + 1) * 128, :].rearrange("p (r t) -> p r t", r=NCORES),
                        in_=bass.AP(qall.tensor, rb, [[T, 128], [6144 * T, NCORES], [1, T]])), "gp", 16)
            sld["s"] = p.inc(eng.dma_start(out=S["mask"][:, :], in_=p.H["mask_d"]), "gp", 16)
        p.gp(f_slab)
        p.H["q_in"] = p.H["q_mine"]
    else:
        def f_slab(eng):
            for sgn, sgv in gate:
                W(p, eng, sgn, sgv() if callable(sgv) else sgv)
            p.inc(eng.dma_start(out=S["ksl"][:, :], in_=p.H["k_in"]), "gp", 16)
            p.inc(eng.dma_start(out=S["vfm"][:, :], in_=p.H["v_in"]), "gp", 16)
            sld["s"] = p.inc(eng.dma_start(out=S["mask"][:, :], in_=p.H["mask_d"]), "gp", 16)
        p.gp(f_slab)

    vt = {}
    for sc64 in range(64):
        def f_vt1(eng, sc64=sc64):
            if sc64 == 0:
                W(p, eng, "gp", sld["s"])
            if sc64 >= 2:
                W(p, eng, "pe", vt[("t", sc64 - 2)])
            par = sc64 % 2
            vt[("c", sc64)] = p.inc(eng.tensor_copy(S["vtf"][:, par * 128:(par + 1) * 128],
                                                    S["vfm"][:, sc64 * 128:(sc64 + 1) * 128]), "dve")
        p.ve(f_vt1)

        def f_vt2(eng, sc64=sc64):
            W(p, eng, "dve", vt[("c", sc64)])
            bank = P[6] if sc64 % 2 == 0 else P[7]
            if sc64 >= 2:
                W(p, eng, "dve", vt[("e", sc64 - 2)])
            par = sc64 % 2
            vt[("t", sc64)] = p.inc(eng.transpose(bank[:, 0:128], S["vtf"][:, par * 128:(par + 1) * 128],
                                                  S["identf"][:, :]), "pe")
        p.pe(f_vt2)

        def f_vt3(eng, sc64=sc64):
            W(p, eng, "pe", vt[("t", sc64)])
            bank = P[6] if sc64 % 2 == 0 else P[7]
            vt[("e", sc64)] = p.inc(eng.tensor_copy(S["vsl"][:, sc64 * 128:(sc64 + 1) * 128],
                                                    bank[:, 0:128]), "dve")
        p.ve(f_vt3)

    for j in range(4):
        for qb in range(16):
            jqi = j * 16 + qb
            nsc = 4 * qb + 4
            par2 = jqi % 2
            Po = P[par2]
            Pl = P[2 + par2]

            qsem = "sp"

            def f_q(eng, j=j, qb=qb, jqi=jqi, par2=par2):
                if jqi == 0 and fused:
                    W(p, eng, "gp", sld["s"])
                if jqi >= 2:
                    W(p, eng, "pe", scd[(jqi - 2, "last")])
                src = p.H["q_in"][j * 128:(j + 1) * 128, qb * 512:(qb + 1) * 512]
                qld[jqi] = p.inc(eng.dma_start(
                    out=S["qtl"][:, par2 * 512:(par2 + 1) * 512], in_=src), "sp", 16)
            p.sp(f_q)

            for sc in range(nsc):
                spar = sc % 2

                def f_sc(eng, j=j, qb=qb, jqi=jqi, sc=sc, spar=spar, par2=par2, nsc=nsc, qsem=qsem):
                    if jqi == 0 and sc == 0:
                        W(p, eng, "gp", sld["s"])
                    W(p, eng, qsem, qld[jqi])
                    if sc >= 2:
                        W(p, eng, "act", exd[(jqi, sc - 2)])
                    elif jqi >= 1:
                        W(p, eng, "act", exd[(jqi - 1, "last")])
                    if sc == 0:
                        if jqi >= 2:
                            W(p, eng, "dve", tl[(jqi - 2, "ev")])
                    i = eng.matmul(P[4 + spar][:, :], S["ksl"][:, sc * 128:(sc + 1) * 128],
                                   S["qtl"][:, par2 * 512:(par2 + 1) * 512], start=True, stop=True)
                    scd[(jqi, sc)] = p.inc(i, "pe")
                    if sc == nsc - 1:
                        scd[(jqi, "last")] = scd[(jqi, sc)]
                p.pe(f_sc)

                def f_exp(eng, jqi=jqi, sc=sc, spar=spar, nsc=nsc):
                    W(p, eng, "pe", scd[(jqi, sc)])
                    if (jqi, sc - 2) in pvd:
                        W(p, eng, "pe", pvd[(jqi, sc - 2)])
                    elif sc < 2 and jqi >= 1:
                        W(p, eng, "pe", pvd[(jqi - 1, "last")])
                    i = eng.activation(S["psc"][:, spar * 512:(spar + 1) * 512],
                                       P[4 + spar][:, :], AF.Exp, bias=S["fc"][0:128, 1:2], scale=SC)
                    exd[(jqi, sc)] = p.inc(i, "act")
                    if sc == nsc - 1:
                        exd[(jqi, "last")] = exd[(jqi, sc)]
                p.sc(f_exp)

                if sc >= 4 * qb:
                    m = sc - 4 * qb

                    def f_mk(eng, jqi=jqi, sc=sc, spar=spar, m=m):
                        W(p, eng, "act", exd[(jqi, sc)])
                        mkd[(jqi, sc)] = p.inc(eng.tensor_tensor(
                            S["psc"][:, spar * 512:(spar + 1) * 512],
                            S["psc"][:, spar * 512:(spar + 1) * 512],
                            S["mask"][:, m * 512:(m + 1) * 512], ALU.mult), "dve")
                    p.ve(f_mk)

                def f_pv(eng, jqi=jqi, qb=qb, sc=sc, spar=spar, nsc=nsc, Po=Po, Pl=Pl):
                    if jqi == 0 and sc == 0:
                        W(p, eng, "dve", vt[("e", 63)])
                    if (jqi, sc) in mkd:
                        W(p, eng, "dve", mkd[(jqi, sc)])
                    else:
                        W(p, eng, "act", exd[(jqi, sc)])
                    st, spp = (sc == 0), (sc == nsc - 1)
                    eng.matmul(Po[:, :], S["vsl"][:, sc * 128:(sc + 1) * 128],
                               S["psc"][:, spar * 512:(spar + 1) * 512], start=st, stop=spp)
                    i = eng.matmul(Pl[0:1, :], S["onesb"][:, :],
                                   S["psc"][:, spar * 512:(spar + 1) * 512], start=st, stop=spp)
                    pvd[(jqi, sc)] = p.inc(i, "pe")
                    if spp:
                        pvd[(jqi, "last")] = pvd[(jqi, sc)]
                p.pe(f_pv)

            def f_tl1(eng, jqi=jqi, par2=par2, Pl=Pl):
                W(p, eng, "pe", pvd[(jqi, "last")])
                if jqi >= 1:
                    W(p, eng, "pe", tl[(jqi - 1, "rb")])
                vchain(p, eng, eng.reciprocal(S["rbs"][0:1, par2 * 512:par2 * 512 + 512], Pl[0:1, :]))
                tl[(jqi, "rl")] = p.inc(eng.tensor_copy(S["brow"][0:1, :],
                                                        S["rbs"][0:1, par2 * 512:par2 * 512 + 512]), "dve")
            p.ve(f_tl1)

            def f_tl2(eng, jqi=jqi):
                W(p, eng, "dve", tl[(jqi, "rl")])
                tl[(jqi, "rb")] = p.inc(eng.matmul(P[6][:, :], S["ones128"][:, :],
                                                   S["brow"][:, :], start=True, stop=True), "pe")
            p.pe(f_tl2)

            def f_tl3(eng, jqi=jqi, par2=par2, Po=Po):
                W(p, eng, "pe", tl[(jqi, "rb")])
                if jqi >= 2:
                    W(p, eng, "sp", tl[(jqi - 2, "st")])
                vchain(p, eng, eng.tensor_copy(S["rbs"][:, par2 * 512:(par2 + 1) * 512], P[6][:, :]))
                tl[(jqi, "ev")] = p.inc(eng.tensor_mul(
                    S["aow"][:, par2 * 512:(par2 + 1) * 512], Po[:, :],
                    S["rbs"][:, par2 * 512:(par2 + 1) * 512]), "dve")
            p.ve(f_tl3)

            def f_tl4(eng, j=j, qb=qb, jqi=jqi, par2=par2):
                W(p, eng, "dve", tl[(jqi, "ev")])
                dst = p.H["attnb" if fused else "attn_out"]
                tl[(jqi, "st")] = p.inc(eng.dma_start(
                    out=dst[j * 128:(j + 1) * 128, qb * 512:(qb + 1) * 512],
                    in_=S["aow"][:, par2 * 512:(par2 + 1) * 512]), "sp", 16)
            p.sp(f_tl4)

    core_ctx.close()
    return lambda: tl[(63, "st")]



def decl_outproj1_inputs(p):
    p.dram_in("attn_in", [D, T], BF16)
    p.dram_in("wo1_blk", [DC, 128, D], F32)


def alloc_outproj1(p, S, ctx):
    S["asl"] = p.sb("asl", [128, DC * T], BF16, ctx)
    S["wo"] = p.sb("wo", [128, 2 * D], BF16, ctx)
    S["eps"] = p.sb("eps", [128, 2 * 512], F32, ctx)
    S["xep"] = p.sb("xep", [128, 2 * 512], F32, ctx)
    S["xo"] = p.sb("xo", [128, 2 * 512], F32, ctx)


def emit_outproj1(p, S, x_src, x_dst, gate=(), fused=False):
    P = S["p"]
    o_ctx = p.scope()
    alloc_outproj1(p, S, o_ctx)
    ald, owld, omm, ep_s, ep_v, ep_x, ep_st = {}, {}, {}, {}, {}, {}, {}

    def f_asl(eng):
        for sgn, sgv in gate:
            W(p, eng, sgn, sgv() if callable(sgv) else sgv)
        if not fused:
            ald["a"] = p.inc(eng.dma_start(
                out=S["asl"][:, :].rearrange("p (c t) -> p c t", c=DC),
                in_=p.H["attn_in"].rearrange("(c p) t -> p c t", p=128)), "gp", 16)
        else:
            aall = p.H["attn_all"]
            with eng.register("rc") as rc:
                eng.reg_load(rc, S["rks"][0:1, 0:1])
                eng.reg_alu(rc, rc, T, ALU.mult)
                with eng.register("rd") as rd:
                    for g4 in range(4):
                        eng.reg_add(rd, rc, g4 * 8 * 128 * (8 * T))
                        ald["a"] = p.inc(eng.dma_start(
                            out=S["asl"][:, g4 * 8 * T:(g4 + 1) * 8 * T].rearrange(
                                "p (cc t) -> p cc t", cc=8),
                            in_=bass.AP(aall.tensor, rd, [[8 * T, 128], [128 * 8 * T, 8], [1, T]])), "gp", 16)
    p.gp(f_asl)

    for o in range(DC):
        buf = o % 2

        def f_gpo(eng, o=o, buf=buf):
            if o >= 2:
                W(p, eng, "pe", omm[(o - 2, 1)])
            owld[o] = p.inc(eng.dma_start(out=S["wo"][:, buf * D:(buf + 1) * D],
                                          in_=p.H["wo1_blk"][o, :, :]), "gp", 16)
        p.gp(f_gpo)

        for tb in range(2):
            idx = o * 2 + tb
            bank = idx % 4

            def f_spx(eng, idx=idx, o=o, tb=tb):
                if idx >= 2:
                    W(p, eng, "dve", ep_v[idx - 2])
                ep_x[idx] = p.inc(eng.dma_start(
                    out=S["xep"][:, (idx % 2) * 512:((idx % 2) + 1) * 512],
                    in_=x_src[o * 128:(o + 1) * 128, tb * 512:(tb + 1) * 512]), "sp", 16)
            p.sp(f_spx)

            def f_peo(eng, o=o, tb=tb, idx=idx, bank=bank, buf=buf):
                W(p, eng, "gp", owld[o])
                if idx == 0:
                    W(p, eng, "gp", ald["a"])
                if idx >= 4:
                    W(p, eng, "act", ep_s[idx - 4])
                i = None
                for cc in range(DC):
                    i = eng.matmul(P[bank][:, :], S["wo"][:, buf * D + cc * 128:buf * D + (cc + 1) * 128],
                                   S["asl"][:, cc * T + tb * 512:cc * T + (tb + 1) * 512],
                                   start=(cc == 0), stop=(cc == DC - 1))
                omm[(o, tb)] = p.inc(i, "pe")
            p.pe(f_peo)

            def f_eps(eng, idx=idx, bank=bank):
                W(p, eng, "pe", omm[(idx // 2, idx % 2)])
                if idx >= 2:
                    W(p, eng, "dve", ep_v[idx - 2])
                ep_s[idx] = p.inc(eng.activation(S["eps"][:, (idx % 2) * 512:((idx % 2) + 1) * 512],
                                                 P[bank][:, :], AF.Copy), "act")
            p.sc(f_eps)

            def f_epv(eng, idx=idx):
                W(p, eng, "act", ep_s[idx])
                W(p, eng, "sp", ep_x[idx])
                if idx >= 2:
                    W(p, eng, "sp", ep_st[idx - 2])
                ep_v[idx] = p.inc(eng.tensor_add(
                    S["xo"][:, (idx % 2) * 512:((idx % 2) + 1) * 512],
                    S["eps"][:, (idx % 2) * 512:((idx % 2) + 1) * 512],
                    S["xep"][:, (idx % 2) * 512:((idx % 2) + 1) * 512]), "dve")
            p.ve(f_epv)

            def f_epst(eng, idx=idx, o=o, tb=tb):
                W(p, eng, "dve", ep_v[idx])
                ep_st[idx] = p.inc(eng.dma_start(
                    out=x_dst[o * 128:(o + 1) * 128, tb * 512:(tb + 1) * 512],
                    in_=S["xo"][:, (idx % 2) * 512:((idx % 2) + 1) * 512]), "sp", 16)
            p.sp(f_epst)

    o_ctx.close()
    return lambda: ep_st[DC * 2 - 1]





def build_qkv1_prog():
    p = Prog().open()
    x_in = p.dram_in("x_in", [D, T], F32)
    p.dram_in("wq1_blk", [DC, 128, D], F32)
    p.dram_in("wk1_blk", [8, 128, D], F32)
    p.dram_in("wv1_blk", [8, 128, D], F32)
    p.dram_in("cs_in", [128, T], F32)
    p.dram_out("qkvb", [6144, T], BF16)
    S = alloc_common(p)
    emit_qkv1(p, S, x_in)
    final_wait(p)
    return p


def build_attn1core_prog():
    p = Prog().open()
    decl_attn1core_inputs(p)
    S = alloc_common(p)
    emit_attn1core(p, S)
    final_wait(p)
    return p


def build_outproj1_prog():
    p = Prog().open()
    x_in = p.dram_in("x_in", [D, T], F32)
    x_out = p.dram_out("x_out", [D, T], F32)
    decl_outproj1_inputs(p)
    S = alloc_common(p)
    emit_outproj1(p, S, x_in, x_out)
    final_wait(p)
    return p


def host_cos_sin(core):
    half = 64
    inv = 1.0 / (10000.0 ** (np.arange(half, dtype=np.float32) / half))
    pos = np.arange(core * T, (core + 1) * T, dtype=np.float32)
    ang = pos[:, None] * inv[None, :]
    return np.ascontiguousarray(
        np.concatenate([np.cos(ang).T, np.sin(ang).T], axis=0).astype(np.float32))


def host_masks():
    m = np.zeros((128, 4, 512), dtype=BF)
    for mm_ in range(4):
        for n in range(4):
            if n > mm_:
                m[:, mm_, n * 128:(n + 1) * 128] = 1
            elif n == mm_:
                tri = (np.arange(128)[:, None] <= np.arange(128)[None, :])
                m[:, mm_, n * 128:(n + 1) * 128] = tri.astype(BF)
    return np.ascontiguousarray(m.reshape(128, 4 * 512))


def prep_attn1_weights(inp):
    return {
        "wq1_blk": wblocks_io(np.asarray(inp["L1_attn_Wq"], np.float32)),
        "wk1_blk": wblocks_io(np.asarray(inp["L1_attn_Wk"], np.float32)),
        "wv1_blk": wblocks_io(np.asarray(inp["L1_attn_Wv"], np.float32)),
        "wo1_blk": wblocks_io(np.asarray(inp["L1_attn_Wo"], np.float32)),
    }


def run_attn1(x_full, inp, trace=False):
    consts = host_consts()
    wm = prep_attn1_weights(inp)
    pr = build_qkv1_prog()
    in_maps = []
    for c in range(NCORES):
        shard = np.ascontiguousarray(np.asarray(x_full, np.float32)[c * T:(c + 1) * T, :].T)
        m = {"x_in": shard, "wq1_blk": wm["wq1_blk"], "wk1_blk": wm["wk1_blk"],
             "wv1_blk": wm["wv1_blk"], "cs_in": host_cos_sin(c)}
        m.update(consts)
        in_maps.append(m)
    res = run_bass_kernel_spmd(pr.nc, in_maps, core_ids=list(range(NCORES)))
    pr.close()
    qkv_all = np.concatenate([res.results[c]["qkvb"] for c in range(NCORES)], axis=0)

    masks = host_masks()
    pr = build_attn1core_prog()
    in_maps = []
    for c in range(NCORES):
        q_in = np.empty((512, 8 * T), dtype=BF)
        k_in = np.empty((128, 8 * T), dtype=BF)
        v_in = np.empty((128, 8 * T), dtype=BF)
        for r in range(NCORES):
            blk = qkv_all[6144 * r:6144 * (r + 1)]
            for j in range(4):
                q_in[j * 128:(j + 1) * 128, r * T:(r + 1) * T] = blk[(4 * c + j) * 128:(4 * c + j + 1) * 128]
            k_in[:, r * T:(r + 1) * T] = blk[4096 + c * 128:4096 + (c + 1) * 128]
            v_in[:, r * T:(r + 1) * T] = blk[5120 + c * 128:5120 + (c + 1) * 128]
        m = {"q_in": q_in, "k_in": k_in, "v_in": v_in, "mask_d": masks}
        m.update(consts)
        in_maps.append(m)
    res = run_bass_kernel_spmd(pr.nc, in_maps, core_ids=list(range(NCORES)))
    pr.close()
    attn_all = np.concatenate([res.results[c]["attn_out"] for c in range(NCORES)], axis=0)

    pr = build_outproj1_prog()
    in_maps = []
    for c in range(NCORES):
        shard = np.ascontiguousarray(np.asarray(x_full, np.float32)[c * T:(c + 1) * T, :].T)
        m = {"x_in": shard, "attn_in": np.ascontiguousarray(attn_all[:, c * T:(c + 1) * T]),
             "wo1_blk": wm["wo1_blk"]}
        m.update(consts)
        in_maps.append(m)
    res = run_bass_kernel_spmd(pr.nc, in_maps, core_ids=list(range(NCORES)), trace=trace)
    pr.close()
    out = np.empty((NCORES * T, D), np.float32)
    for c in range(NCORES):
        out[c * T:(c + 1) * T, :] = res.results[c]["x_out"].T
    return out, res





def build_fused_prog():
    p = Prog().open()
    x_in = p.dram_in("x_in", [D, T], F32)
    x_out = p.dram_out("x_out", [D, T], F32)
    p.dram_in("myrank", [1, 1], I32)
    decl_ffn0_inputs(p)
    decl_attn0_inputs(p)
    decl_ffn1_inputs(p)
    decl_qkv1_inputs(p)
    p.dram_in("mask_d", [128, 4 * 512], BF16)
    p.dram_in("wo1_blk", [DC, 128, D], F32)
    xres = p.dram_int("xres", [D, T], F32)
    p.dram_int("qkv_all", [NCORES * 6144, T], BF16)
    p.H["attnb"] = p.dram_int("attnb", [512, 8 * T], BF16)
    p.dram_int("attn_all", [4096, 8 * T], BF16)
    p.dram_int("q_mine", [512, 8 * T], BF16)
    S = alloc_common(p)

    emit_ffn0(p, S, x_in, xres)
    emit_barrier(p)
    emit_attn0(p, S, xres, xres)
    emit_barrier(p)
    emit_ffn0(p, S, xres, xres)
    emit_barrier(p)
    emit_ffn1(p, S, xres, xres, dequant=True)
    emit_barrier(p)
    qkv_done = emit_qkv1(p, S, xres)
    ag = {}

    def f_ag1(eng):
        W(p, eng, "sp", qkv_done())
        ag["1"] = p.inc(eng.collective_compute(
            "AllGather", ALU.bypass, replica_groups=[list(range(NCORES))],
            ins=[p.H["qkvb"].opt()], outs=[p.H["qkv_all"].opt()]), "gp", 1)
    p.gp(f_ag1)
    emit_barrier(p)
    attn_done = emit_attn1core(p, S, gate=(("gp", lambda: ag["1"]),), fused=True)

    def f_ag2(eng):
        W(p, eng, "sp", attn_done())
        ag["2"] = p.inc(eng.collective_compute(
            "AllGather", ALU.bypass, replica_groups=[list(range(NCORES))],
            ins=[p.H["attnb"].opt()], outs=[p.H["attn_all"].opt()]), "gp", 1)
    p.gp(f_ag2)
    emit_barrier(p)
    emit_outproj1(p, S, xres, xres, gate=(("gp", lambda: ag["2"]),), fused=True)
    emit_barrier(p)
    emit_ffn1(p, S, xres, x_out, dequant=False)
    final_wait(p)
    return p


_FUSED_CACHE = {}


def kernel(**inputs):
    inp = {k: np.asarray(v) for k, v in inputs.items()}
    x = np.asarray(inp["L0_attn_hidden_states"], np.float32).reshape(8192, 4096)
    if "prog" not in _FUSED_CACHE:
        _FUSED_CACHE["prog"] = build_fused_prog()
    pr = _FUSED_CACHE["prog"]
    wm = {}
    wm.update(prep_ffn0_inputs(inp))
    wm.update(prep_attn0_inputs(inp))
    wm.update(prep_ffn1_inputs(inp))
    wm.update(prep_attn1_weights(inp))
    wm["mask_d"] = host_masks()
    wm.update(host_consts())
    in_maps = []
    for c in range(NCORES):
        m = dict(wm)
        m["x_in"] = np.ascontiguousarray(x[c * T:(c + 1) * T, :].T)
        m["myrank"] = np.array([[c]], np.int32)
        m["cs_in"] = host_cos_sin(c)
        in_maps.append(m)
    res = run_bass_kernel_spmd(pr.nc, in_maps, core_ids=list(range(NCORES)))
    out = np.empty((NCORES * T, D), np.float32)
    for c in range(NCORES):
        out[c * T:(c + 1) * T, :] = res.results[c]["x_out"].T
    return out.reshape(1, 8192, 4096)



# version 09
# speedup vs baseline: 1.4096x; 1.4096x over previous
import numpy as np
import ml_dtypes
from contextlib import ExitStack
import concourse.bass as bass
import concourse.mybir as mybir
from concourse.bass_utils import run_bass_kernel_spmd

F32 = mybir.dt.float32
BF16 = mybir.dt.bfloat16
F16 = mybir.dt.float16
I32 = mybir.dt.int32
AF = mybir.ActivationFunctionType
ALU = mybir.AluOpType
AX = mybir.AxisListType.X

NCORES = 8
T = 1024
D = 4096
DC = D // 128
EI = 7168
EIC = EI // 128
NEXP = 16
INTER = 448
BF = ml_dtypes.bfloat16
BIG = 1.0e30
ATT_SHIFT = 15.0



def wblocks_oi(Wm):
    O, I = Wm.shape
    W4 = np.asarray(Wm, np.float32).reshape(O // 128, 128, I // 128, 128)
    return np.ascontiguousarray(W4.transpose(0, 3, 2, 1)).reshape(O // 128, 128, I)


def wblocks_io(Wm):
    I, O = Wm.shape
    W4 = np.asarray(Wm, np.float32).reshape(I // 128, 128, O // 128, 128)
    return np.ascontiguousarray(W4.transpose(2, 1, 0, 3)).reshape(O // 128, 128, I)


def host_consts():
    return {
        "onesb_d": np.ones((128, 1), dtype=BF),
        "onesf_d": np.ones((1, 128), dtype=np.float32),
        "identf_d": np.eye(128, dtype=np.float32),
        "ones128_d": np.ones((128, 128), dtype=np.float32),
        "fconst_d": np.broadcast_to(
            np.array([1e-5, -15.0, 1e-6, 1.0], np.float32), (128, 4)).copy(),
    }



class Prog:
    def __init__(self):
        self.nc = bass.Bass("TRN2", target_bir_lowering=False, debug=False)
        self.ctx = ExitStack()
        self.c = {}
        self.sems = {}
        self.block = None
        self.H = {}

    def open(self):
        for s in ("gp", "sp", "pe", "act", "dve"):
            self.sems[s] = self.ctx.enter_context(self.nc.semaphore(s))
            self.c[s] = 0
        self.block = self.ctx.enter_context(self.nc.Block())
        return self

    def close(self):
        self.ctx.close()

    def dram_in(self, name, shape, dtype):
        self.H[name] = self.nc.dram_tensor(name, shape, dtype, kind="ExternalInput").ap()
        return self.H[name]

    def dram_out(self, name, shape, dtype):
        self.H[name] = self.nc.dram_tensor(name, shape, dtype, kind="ExternalOutput").ap()
        return self.H[name]

    def dram_int(self, name, shape, dtype):
        self.H[name] = self.nc.dram_tensor(name, shape, dtype, kind="Internal").ap()
        return self.H[name]

    def sb(self, name, shape, dtype, ctx=None):
        self._n = getattr(self, "_n", 0) + 1
        return (ctx or self.ctx).enter_context(
            self.nc.sbuf_tensor(f"{name}_{self._n}", shape, dtype))

    def scope(self):
        return ExitStack()

    def ps(self, name, shape, dtype=F32):
        return self.ctx.enter_context(self.nc.psum_tensor(name, shape, dtype))

    def inc(self, inst, name, n=1):
        inst.then_inc(self.sems[name], n)
        self.c[name] += n
        return self.c[name]

    def gp(self, f):
        self.block.gpsimd(f)

    def sp(self, f):
        self.block.sync(f)

    def pe(self, f):
        self.block.tensor(f)

    def sc(self, f):
        self.block.scalar(f)

    def ve(self, f):
        self.block.vector(f)


def W(p, eng, name, val):
    eng.wait_ge(p.sems[name], val)


def vchain(p, eng, inst):
    v = p.inc(inst, "dve")
    eng.wait_ge(p.sems["dve"], v)
    return v



def alloc_common(p):
    S = {}
    p.dram_in("onesb_d", [128, 1], BF16)
    p.dram_in("onesf_d", [1, 128], F32)
    p.dram_in("identf_d", [128, 128], F32)
    p.dram_in("fconst_d", [128, 4], F32)
    p.dram_in("ones128_d", [128, 128], F32)
    S["p"] = [p.ps(f"pb{i}", [128, 512], F32) for i in range(8)]
    S["xln"] = p.sb("xln", [128, DC * T], BF16)
    S["onesb"] = p.sb("onesb", [128, 1], BF16)
    S["onesf"] = p.sb("onesf", [1, 128], F32)
    S["identf"] = p.sb("identf", [128, 128], F32)
    S["sm1"] = p.sb("sm1", [128, 192], F32)
    S["fc"] = p.sb("fc", [128, 4], F32)
    S["ones128"] = p.sb("ones128", [128, 128], F32)
    S["brow"] = p.sb("brow", [128, 512], F32)
    S["rks"] = p.sb("rks", [1, 1], I32)

    def f(eng):
        p.inc(eng.dma_start(out=S["onesb"][:, :], in_=p.H["onesb_d"]), "sp", 16)
        p.inc(eng.dma_start(out=S["onesf"][:, :], in_=p.H["onesf_d"]), "sp", 16)
        p.inc(eng.dma_start(out=S["identf"][:, :], in_=p.H["identf_d"]), "sp", 16)
        p.inc(eng.dma_start(out=S["fc"][:, :], in_=p.H["fconst_d"]), "sp", 16)
        S["_c_ld"] = p.inc(eng.dma_start(out=S["ones128"][:, :], in_=p.H["ones128_d"]), "sp", 16)
        if "myrank" in p.H:
            S["_c_ld"] = p.inc(eng.dma_start(out=S["rks"][:, :], in_=p.H["myrank"]), "sp", 16)
    p.sp(f)

    def fz(eng):
        W(p, eng, "sp", S["_c_ld"])
        S["_броу"] = p.inc(eng.memset(S["brow"][:, :], 0.0), "dve")
    p.ve(fz)
    return S


def alloc_ln(p, S, ctx, gate=False):
    S["xst"] = p.sb("xst", [128, 2 * T], F32, ctx)
    S["row"] = p.sb("row", [1, 4 * T], F32, ctx)
    S["bca"] = p.sb("bca", [128, T], F32, ctx)
    S["bcb"] = p.sb("bcb", [128, T], F32, ctx)
    S["xbf"] = p.sb("xbf", [128, 2 * T], BF16, ctx)
    S["sq"] = p.sb("sq", [128, 2 * T], BF16, ctx)
    S["ts1"] = p.sb("ts1", [128, 2 * T], F32, ctx)
    if gate:
        S["xlo"] = p.sb("xlo", [128, 2 * T], BF16, ctx)
        S["gwf"] = p.sb("gwf", [128, DC * NEXP], F32, ctx)
        S["gwhi"] = p.sb("gwhi", [128, DC * NEXP], BF16, ctx)
        S["gwlo"] = p.sb("gwlo", [128, DC * NEXP], BF16, ctx)
        S["lg"] = p.sb("lg", [16, T], F32, ctx)
        S["lt"] = p.sb("lt", [128, 8 * 16], F32, ctx)
        S["wtm"] = p.sb("wtm", [128, 8 * 16], F32, ctx)



def emit_ln(p, S, x_src, perm_stride=False, gate=False, start_gate=()):
    P = S["p"]
    xst, xbf, sq, xln = S["xst"], S["xbf"], S["sq"], S["xln"]
    row, bca, bcb, ts1 = S["row"], S["bca"], S["bcb"], S["ts1"]
    ms = {}
    pe_done, sc_done, dma1, dma2, n_done, glog = {}, {}, {}, {}, {}, {}

    def xrow_ap(c):
        if not perm_stride:
            return x_src[c * 128:(c + 1) * 128, :]
        j, r = c // 4, c % 4
        off = (8 * 128 * r + j) * T
        return bass.AP(x_src.tensor, off, [[8 * T, 128], [1, T]])

    for c in range(DC):
        buf = c % 2

        def f_sp(eng, c=c):
            if c == 0:
                for sgn, sgv in start_gate:
                    W(p, eng, sgn, sgv() if callable(sgv) else sgv)
            if c >= 2:
                W(p, eng, "act", sc_done[c - 2])
            dma1[c] = p.inc(eng.dma_start(
                out=xst[:, (c % 2) * T:((c % 2) + 1) * T], in_=xrow_ap(c)), "sp", 16)
        p.sp(f_sp)

        def f_sc(eng, c=c, buf=buf):
            W(p, eng, "sp", dma1[c])
            if c >= 2:
                W(p, eng, "pe", pe_done[c - 2])
            src = xst[:, (c % 2) * T:((c % 2) + 1) * T]
            eng.activation(xbf[:, buf * T:(buf + 1) * T], src, AF.Copy)
            sc_done[c] = p.inc(eng.activation(sq[:, buf * T:(buf + 1) * T], src, AF.Square), "act")
        p.sc(f_sc)

        def f_pe(eng, c=c, buf=buf):
            W(p, eng, "act", sc_done[c])
            st, spp = (c == 0), (c == DC - 1)
            ob = S["onesb"][:, :]
            eng.matmul(P[0][0:1, :], ob, xbf[:, buf * T:buf * T + 512], start=st, stop=spp)
            eng.matmul(P[1][0:1, :], ob, xbf[:, buf * T + 512:(buf + 1) * T], start=st, stop=spp)
            eng.matmul(P[2][0:1, :], ob, sq[:, buf * T:buf * T + 512], start=st, stop=spp)
            i = eng.matmul(P[3][0:1, :], ob, sq[:, buf * T + 512:(buf + 1) * T], start=st, stop=spp)
            pe_done[c] = p.inc(i, "pe")
        p.pe(f_pe)

    def f_v1(eng):
        W(p, eng, "pe", pe_done[DC - 1])
        eng.tensor_scalar_mul(row[:, 0:512], P[0][0:1, :], 1.0 / D)
        eng.tensor_scalar_mul(row[:, 512:T], P[1][0:1, :], 1.0 / D)
        eng.tensor_scalar_mul(row[:, T:T + 512], P[2][0:1, :], 1.0 / D)
        vchain(p, eng, eng.tensor_scalar_mul(row[:, T + 512:2 * T], P[3][0:1, :], 1.0 / D))
        vchain(p, eng, eng.tensor_mul(row[:, 2 * T:3 * T], row[:, 0:T], row[:, 0:T]))
        ms["var"] = p.inc(eng.tensor_sub(row[:, T:2 * T], row[:, T:2 * T], row[:, 2 * T:3 * T]), "dve")
    p.ve(f_v1)

    def f_s1(eng):
        W(p, eng, "dve", ms["var"])
        ms["std"] = p.inc(eng.activation(row[:, 2 * T:3 * T], row[:, T:2 * T], AF.Sqrt,
                                         bias=S["fc"][0:1, 0:1]), "act")
    p.sc(f_s1)

    def f_v2(eng):
        W(p, eng, "act", ms["std"])
        vchain(p, eng, eng.reciprocal(row[:, 3 * T:4 * T], row[:, 2 * T:3 * T]))
        ms["mr"] = p.inc(eng.tensor_mul(row[:, 2 * T:3 * T], row[:, 0:T], row[:, 3 * T:4 * T]), "dve")
    p.ve(f_v2)

    srcs = [3 * T, 3 * T + 512, 2 * T, 2 * T + 512]
    dsts = [(bca, 0), (bca, 512), (bcb, 0), (bcb, 512)]
    for bi in range(4):
        def f_brow(eng, bi=bi):
            if bi == 0:
                W(p, eng, "dve", ms["mr"])
            else:
                W(p, eng, "pe", ms[("bcmm", bi - 1)])
            ms[("brow", bi)] = vchain(p, eng, eng.tensor_copy(
                S["brow"][0:1, :], row[0:1, srcs[bi]:srcs[bi] + 512]))
        p.ve(f_brow)

        def f_bc(eng, bi=bi):
            W(p, eng, "dve", ms[("brow", bi)])
            ms[("bcmm", bi)] = p.inc(eng.matmul(P[bi][:, :], S["ones128"][:, :],
                                                S["brow"][:, :], start=True, stop=True), "pe")
        p.pe(f_bc)

        def f_bcv(eng, bi=bi):
            W(p, eng, "pe", ms[("bcmm", bi)])
            dst, off = dsts[bi]
            v = p.inc(eng.tensor_copy(dst[:, off:off + 512], P[bi][:, :]), "dve")
            if bi == 3:
                ms["bcev"] = v
        p.ve(f_bcv)

    for c in range(DC):
        def f_sp2(eng, c=c):
            if c < 2:
                W(p, eng, "act", sc_done[DC - 2 + c])
            else:
                W(p, eng, "dve", n_done[c - 2])
            dma2[c] = p.inc(eng.dma_start(
                out=xst[:, (c % 2) * T:((c % 2) + 1) * T], in_=xrow_ap(c)), "sp", 16)
        p.sp(f_sp2)

        if not gate:
            def f_vn(eng, c=c):
                W(p, eng, "sp", dma2[c])
                if c == 0:
                    W(p, eng, "dve", ms["bcev"])
                src = xst[:, (c % 2) * T:((c % 2) + 1) * T]
                vchain(p, eng, eng.tensor_mul(ts1[:, 0:T], src, bca[:, 0:T]))
                n_done[c] = p.inc(eng.tensor_sub(xln[:, c * T:(c + 1) * T], ts1[:, 0:T], bcb[:, 0:T]), "dve")
            p.ve(f_vn)
        else:
            xlo = S["xlo"]

            def f_vng(eng, c=c):
                W(p, eng, "sp", dma2[c])
                if c == 0:
                    W(p, eng, "dve", ms["bcev"])
                if c >= 2:
                    W(p, eng, "pe", glog[c - 2])
                buf = c % 2
                src = xst[:, (c % 2) * T:((c % 2) + 1) * T]
                vchain(p, eng, eng.tensor_mul(ts1[:, 0:T], src, bca[:, 0:T]))
                vchain(p, eng, eng.tensor_sub(ts1[:, T:2 * T], ts1[:, 0:T], bcb[:, 0:T]))
                vchain(p, eng, eng.tensor_copy(xln[:, c * T:(c + 1) * T], ts1[:, T:2 * T]))
                n_done[c] = p.inc(eng.tensor_sub(xlo[:, buf * T:(buf + 1) * T],
                                                 ts1[:, T:2 * T], xln[:, c * T:(c + 1) * T]), "dve")
            p.ve(f_vng)

            def f_peg(eng, c=c):
                W(p, eng, "dve", n_done[c])
                buf = c % 2
                st, spp = (c == 0), (c == DC - 1)
                gh, gl = S["gwhi"], S["gwlo"]
                i = None
                for tb in range(2):
                    xh = xln[:, c * T + tb * 512:c * T + (tb + 1) * 512]
                    xl = xlo[:, buf * T + tb * 512:buf * T + (tb + 1) * 512]
                    eng.matmul(P[4 + tb][0:16, :], gh[:, c * 16:(c + 1) * 16], xh, start=st, stop=False)
                    eng.matmul(P[4 + tb][0:16, :], gl[:, c * 16:(c + 1) * 16], xh, start=False, stop=False)
                    i = eng.matmul(P[4 + tb][0:16, :], gh[:, c * 16:(c + 1) * 16], xl, start=False, stop=spp)
                glog[c] = p.inc(i, "pe")
            p.pe(f_peg)

    ms["xln"] = lambda: n_done[DC - 1]
    ms["glog"] = (lambda: glog[DC - 1]) if gate else None
    return ms



def emit_gating(p, S, ms):
    P = S["p"]
    lg, lt, wtm, wfm, sm1 = S["lg"], S["lt"], S["wtm"], S["wfm"], S["sm1"]
    g = {}

    def f_ev(eng):
        W(p, eng, "pe", ms["glog"]())
        eng.memset(wfm[:, :], 0.0)
        eng.tensor_copy(lg[:, 0:512], P[4][0:16, :])
        g["lg"] = p.inc(eng.tensor_copy(lg[:, 512:T], P[5][0:16, :]), "dve")
    p.ve(f_ev)

    tpmm, tpev = {}, {}
    for tc in range(8):
        def f_tp(eng, tc=tc):
            if tc == 0:
                W(p, eng, "dve", g["lg"])
            bank = P[6] if tc % 2 == 0 else P[7]
            if tc >= 2:
                W(p, eng, "dve", tpev[tc - 2])
            tpmm[tc] = p.inc(eng.transpose(bank[:, 0:16], lg[0:16, tc * 128:(tc + 1) * 128],
                                           S["identf"][0:16, 0:16]), "pe")
        p.pe(f_tp)

        def f_tpv(eng, tc=tc):
            W(p, eng, "pe", tpmm[tc])
            bank = P[6] if tc % 2 == 0 else P[7]
            tpev[tc] = p.inc(eng.tensor_copy(lt[:, tc * 16:(tc + 1) * 16], bank[:, 0:16]), "dve")
        p.ve(f_tpv)

    sexp = {}
    for tc in range(8):
        def f_tk(eng, tc=tc):
            W(p, eng, "dve", tpev[tc])
            L = lt[:, tc * 16:(tc + 1) * 16]
            m1, nm1, m2 = sm1[:, 0:1], sm1[:, 1:2], sm1[:, 2:3]
            eq, lm, keep = sm1[:, 16:32], sm1[:, 32:48], sm1[:, 48:64]
            eng.tensor_reduce(m1, L, AX, ALU.max)
            vchain(p, eng, eng.tensor_reduce(nm1, L, AX, ALU.max, negate=True))
            vchain(p, eng, eng.tensor_single_scalar(eq, L, m1, ALU.is_equal))
            vchain(p, eng, eng.scalar_tensor_tensor(lm, eq, -BIG, L, ALU.mult, ALU.add))
            vchain(p, eng, eng.tensor_reduce(m2, lm, AX, ALU.max))
            g[("keep", tc)] = p.inc(eng.tensor_single_scalar(keep, L, m2, ALU.is_ge), "dve")
        p.ve(f_tk)

        def f_te(eng, tc=tc):
            W(p, eng, "dve", g[("keep", tc)])
            sexp[tc] = p.inc(eng.activation(sm1[:, 80:96], lt[:, tc * 16:(tc + 1) * 16],
                                            AF.Exp, bias=sm1[:, 1:2]), "act")
        p.sc(f_te)

        def f_tk2(eng, tc=tc):
            W(p, eng, "act", sexp[tc])
            keep, wk = sm1[:, 48:64], sm1[:, 64:80]
            vchain(p, eng, eng.tensor_mul(wk, sm1[:, 80:96], keep))
            vchain(p, eng, eng.tensor_reduce(sm1[:, 3:4], wk, AX, ALU.add))
            vchain(p, eng, eng.reciprocal(sm1[:, 4:5], sm1[:, 3:4]))
            g[("w", tc)] = p.inc(eng.tensor_scalar_mul(wtm[:, tc * 16:(tc + 1) * 16], wk, sm1[:, 4:5]), "dve")
        p.ve(f_tk2)

    for tc in range(8):
        def f_wt(eng, tc=tc):
            W(p, eng, "dve", g[("w", tc)])
            bank = P[6] if tc % 2 == 0 else P[7]
            if tc >= 2:
                W(p, eng, "dve", g[("wf", tc - 2)])
            g[("wt", tc)] = p.inc(eng.transpose(bank[0:16, 0:128], wtm[:, tc * 16:(tc + 1) * 16],
                                                S["identf"][:, :]), "pe")
        p.pe(f_wt)

        def f_wv(eng, tc=tc):
            W(p, eng, "pe", g[("wt", tc)])
            bank = P[6] if tc % 2 == 0 else P[7]
            g[("wf", tc)] = p.inc(eng.tensor_copy(wfm[0:16, tc * 128:(tc + 1) * 128],
                                                  bank[0:16, 0:128]), "dve")
        p.ve(f_wv)

    return {"wfm": lambda: g[("wf", 7)], "sexp": lambda: sexp[7],
            "wt": lambda: g[("wt", 7)]}



def alloc_ffn0_outer(p, S, ctx):
    S["hsl"] = p.sb("hsl", [128, EIC * 512], BF16, ctx)
    S["wfm"] = p.sb("wfm", [128, T], BF16, ctx)


def alloc_ffn0_body(p, S, ctx):
    S["h1st"] = p.sb("h1st", [128, 2 * 512], BF16, ctx)
    S["wexp"] = p.sb("wexp", [128, 2 * T], BF16, ctx)
    S["expTs"] = p.sb("expTs", [128, EI], BF16, ctx)
    S["sg"] = p.sb("sg", [128, 2 * T], F32, ctx)
    S["hv"] = p.sb("hv", [128, T], F32, ctx)


def alloc_ffn0_ug(p, S, ctx):
    S["wu"] = p.sb("wu", [128, 2 * D], BF16, ctx)
    S["wg"] = p.sb("wg", [128, 2 * D], BF16, ctx)


def alloc_ffn0_down(p, S, ctx):
    S["wd"] = p.sb("wd", [128, 2 * EI], BF16, ctx)
    S["eps"] = p.sb("eps", [128, 2 * 512], F32, ctx)
    S["xep"] = p.sb("xep", [128, 2 * 512], F32, ctx)
    S["xo"] = p.sb("xo", [128, 2 * 512], F32, ctx)


def decl_ffn0_inputs(p):
    p.dram_in("u_blk", [EIC, 128, D], F32)
    p.dram_in("g_blk", [EIC, 128, D], F32)
    p.dram_in("d_blk", [DC, 128, EI], F32)
    p.dram_in("gwT_blk", [128, DC * NEXP], F32)
    p.dram_in("expT", [128, EI], BF16)
    p.dram_int("h1s", [EIC, 128, 512], BF16)


def emit_ffn0(p, S, x_src, x_dst, dbg=False):
    P = S["p"]
    u_blk, g_blk, d_blk = p.H["u_blk"], p.H["g_blk"], p.H["d_blk"]
    h1s = p.H["h1s"]

    outer_ctx = p.scope()
    alloc_ffn0_outer(p, S, outer_ctx)
    ln_ctx = p.scope()
    alloc_ln(p, S, ln_ctx, gate=True)

    gwld = {}

    def f_gw_ld(eng):
        gwld["ld"] = p.inc(eng.dma_start(out=S["gwf"][:, :], in_=p.H["gwT_blk"]), "sp", 16)
    p.sp(f_gw_ld)

    def f_gw_v(eng):
        W(p, eng, "sp", gwld["ld"])
        vchain(p, eng, eng.tensor_copy(S["gwhi"][:, :], S["gwf"][:, :]))
        gwld["hl"] = p.inc(eng.tensor_sub(S["gwlo"][:, :], S["gwf"][:, :], S["gwhi"][:, :]), "dve")
    p.ve(f_gw_v)

    ms = emit_ln(p, S, x_src, gate=True)
    gat = emit_gating(p, S, ms)
    wfm_done = gat["wfm"]
    if dbg:
        def f_dbg1(eng):
            W(p, eng, "dve", wfm_done())
            p.inc(eng.dma_start(out=p.H["xln_dbg"], in_=S["xln"][:, :]), "sp", 16)
            p.inc(eng.dma_start(out=p.H["wfm_dbg"], in_=S["wfm"][:, :]), "sp", 16)
        p.sp(f_dbg1)
    ln_ctx.close()
    body_ctx = p.scope()
    alloc_ffn0_body(p, S, body_ctx)
    ug_ctx = p.scope()
    alloc_ffn0_ug(p, S, ug_ctx)

    wu, wg, wexp, sg, hsl, h1st, hv = (S["wu"], S["wg"], S["wexp"], S["sg"],
                                       S["hsl"], S["h1st"], S["hv"])
    wld, ug_done, sil_done, hv_done, h1_store = {}, {}, {}, {}, {}
    wexp_ms = {}

    for e in range(EIC):
        buf = e % 2
        par = e % 2

        def f_gp(eng, e=e, buf=buf):
            if e == 0:
                W(p, eng, "dve", wfm_done())
                W(p, eng, "act", gat["sexp"]())
                W(p, eng, "pe", gat["wt"]())
                p.inc(eng.dma_start(out=S["expTs"][:, :], in_=p.H["expT"]), "gp", 16)
            if e >= 2:
                W(p, eng, "pe", ug_done[e - 2])
            p.inc(eng.dma_start(out=wu[:, buf * D:(buf + 1) * D], in_=u_blk[e, :, :]), "gp", 16)
            wld[e] = p.inc(eng.dma_start(out=wg[:, buf * D:(buf + 1) * D], in_=g_blk[e, :, :]), "gp", 16)
        p.gp(f_gp)

        def f_pe_a(eng, e=e, buf=buf, par=par):
            W(p, eng, "gp", wld[e])
            pu = [P[par * 4 + 0], P[par * 4 + 1]]
            pg = [P[par * 4 + 2], P[par * 4 + 3]]
            if e >= 2:
                W(p, eng, "act", sil_done[e - 2])
                W(p, eng, "dve", hv_done[e - 2])
            if e == 0:
                W(p, eng, "dve", wfm_done())
            i = None
            for tb in range(2):
                i = eng.matmul(pu[tb][:, :], S["expTs"][:, e * 128:(e + 1) * 128],
                               S["wfm"][:, tb * 512:(tb + 1) * 512], start=True, stop=True)
            wexp_ms[("mm", e)] = p.inc(i, "pe")
            for tb in range(2):
                for c in range(DC):
                    eng.matmul(pg[tb][:, :], wg[:, buf * D + c * 128:buf * D + (c + 1) * 128],
                               S["xln"][:, c * T + tb * 512:c * T + (tb + 1) * 512],
                               start=(c == 0), stop=(c == DC - 1))
        p.pe(f_pe_a)

        def f_vwx(eng, e=e, par=par):
            W(p, eng, "pe", wexp_ms[("mm", e)])
            pu = [P[par * 4 + 0], P[par * 4 + 1]]
            eng.tensor_copy(wexp[:, par * T:par * T + 512], pu[0][:, :])
            wexp_ms[("ev", e)] = p.inc(
                eng.tensor_copy(wexp[:, par * T + 512:(par + 1) * T], pu[1][:, :]), "dve")
        p.ve(f_vwx)

        def f_pe_b(eng, e=e, buf=buf, par=par):
            W(p, eng, "dve", wexp_ms[("ev", e)])
            pu = [P[par * 4 + 0], P[par * 4 + 1]]
            i = None
            for tb in range(2):
                for c in range(DC):
                    i = eng.matmul(pu[tb][:, :], wu[:, buf * D + c * 128:buf * D + (c + 1) * 128],
                                   S["xln"][:, c * T + tb * 512:c * T + (tb + 1) * 512],
                                   start=(c == 0), stop=(c == DC - 1))
            ug_done[e] = p.inc(i, "pe")
        p.pe(f_pe_b)

        def f_sil(eng, e=e, par=par):
            W(p, eng, "pe", ug_done[e])
            pg = [P[par * 4 + 2], P[par * 4 + 3]]
            eng.activation(sg[:, par * T:par * T + 512], pg[0][:, :], AF.Silu)
            sil_done[e] = p.inc(
                eng.activation(sg[:, par * T + 512:(par + 1) * T], pg[1][:, :], AF.Silu), "act")
        p.sc(f_sil)

        def f_hv(eng, e=e, par=par):
            W(p, eng, "act", sil_done[e])
            if e >= 2:
                W(p, eng, "sp", h1_store[e - 2])
            pu = [P[par * 4 + 0], P[par * 4 + 1]]
            eng.tensor_mul(hv[:, 0:512], sg[:, par * T:par * T + 512], pu[0][:, :])
            vchain(p, eng, eng.tensor_mul(hv[:, 512:T], sg[:, par * T + 512:(par + 1) * T], pu[1][:, :]))
            eng.tensor_mul(hsl[:, e * 512:(e + 1) * 512], hv[:, 0:512], wexp[:, par * T:par * T + 512])
            hv_done[e] = p.inc(
                eng.tensor_mul(h1st[:, par * 512:(par + 1) * 512], hv[:, 512:T],
                               wexp[:, par * T + 512:(par + 1) * T]), "dve")
        p.ve(f_hv)

        def f_h1(eng, e=e, par=par):
            W(p, eng, "dve", hv_done[e])
            h1_store[e] = p.inc(eng.dma_start(out=h1s[e, :, :],
                                              in_=h1st[:, par * 512:(par + 1) * 512]), "sp", 16)
        p.sp(f_h1)

    if dbg:
        def f_dbg2(eng):
            W(p, eng, "dve", hv_done[EIC - 1])
            p.inc(eng.dma_start(out=p.H["hsl_dbg"], in_=S["hsl"][:, :]), "sp", 16)
        p.sp(f_dbg2)
    ug_ctx.close()
    down_ctx = p.scope()
    alloc_ffn0_down(p, S, down_ctx)
    wd, xln = S["wd"], S["xln"]
    h1b = {}

    def f_h1back(eng):
        W(p, eng, "dve", hv_done[EIC - 1])
        W(p, eng, "sp", h1_store[EIC - 1])
        h1b["v"] = p.inc(eng.dma_start(
            out=xln[:, 0:EIC * 512].rearrange("p (e m) -> p e m", e=EIC),
            in_=h1s.rearrange("e p m -> p e m")), "sp", 16)
    p.sp(f_h1back)

    dwld, dn_done, ep_s, ep_v, ep_x, ep_st = {}, {}, {}, {}, {}, {}
    for o in range(DC):
        buf = o % 2

        def f_gp2(eng, o=o, buf=buf):
            if o < 2:
                W(p, eng, "pe", ug_done[EIC - 1])
            else:
                W(p, eng, "pe", dn_done[(o - 2, 1)])
            dwld[o] = p.inc(eng.dma_start(out=wd[:, buf * EI:(buf + 1) * EI], in_=d_blk[o, :, :]), "gp", 16)
        p.gp(f_gp2)

        for tb in range(2):
            idx = o * 2 + tb
            bank = idx % 4

            def f_spx(eng, idx=idx, o=o, tb=tb):
                if idx >= 2:
                    W(p, eng, "dve", ep_v[idx - 2])
                ep_x[idx] = p.inc(eng.dma_start(
                    out=S["xep"][:, (idx % 2) * 512:((idx % 2) + 1) * 512],
                    in_=x_src[o * 128:(o + 1) * 128, tb * 512:(tb + 1) * 512]), "sp", 16)
            p.sp(f_spx)

            def f_pe2(eng, o=o, tb=tb, idx=idx, bank=bank, buf=buf):
                W(p, eng, "gp", dwld[o])
                if idx == 0:
                    W(p, eng, "sp", h1b["v"])
                if idx >= 4:
                    W(p, eng, "act", ep_s[idx - 4])
                i = None
                for e in range(EIC):
                    rhs = (hsl[:, e * 512:(e + 1) * 512] if tb == 0
                           else xln[:, e * 512:(e + 1) * 512])
                    i = eng.matmul(P[bank][:, :], wd[:, buf * EI + e * 128:buf * EI + (e + 1) * 128],
                                   rhs, start=(e == 0), stop=(e == EIC - 1))
                dn_done[(o, tb)] = p.inc(i, "pe")
            p.pe(f_pe2)

            def f_eps(eng, idx=idx, bank=bank):
                W(p, eng, "pe", dn_done[(idx // 2, idx % 2)])
                if idx >= 2:
                    W(p, eng, "dve", ep_v[idx - 2])
                ep_s[idx] = p.inc(eng.activation(S["eps"][:, (idx % 2) * 512:((idx % 2) + 1) * 512],
                                                 P[bank][:, :], AF.Copy, scale=0.5), "act")
            p.sc(f_eps)

            def f_epv(eng, idx=idx):
                W(p, eng, "act", ep_s[idx])
                W(p, eng, "sp", ep_x[idx])
                if idx >= 2:
                    W(p, eng, "sp", ep_st[idx - 2])
                ep_v[idx] = p.inc(eng.tensor_add(
                    S["xo"][:, (idx % 2) * 512:((idx % 2) + 1) * 512],
                    S["eps"][:, (idx % 2) * 512:((idx % 2) + 1) * 512],
                    S["xep"][:, (idx % 2) * 512:((idx % 2) + 1) * 512]), "dve")
            p.ve(f_epv)

            def f_epst(eng, idx=idx, o=o, tb=tb):
                W(p, eng, "dve", ep_v[idx])
                ep_st[idx] = p.inc(eng.dma_start(
                    out=x_dst[o * 128:(o + 1) * 128, tb * 512:(tb + 1) * 512],
                    in_=S["xo"][:, (idx % 2) * 512:((idx % 2) + 1) * 512]), "sp", 16)
            p.sp(f_epst)

    down_ctx.close()
    body_ctx.close()
    outer_ctx.close()
    return lambda: ep_st[DC * 2 - 1]




class _FakeInst:
    def then_inc(self, sem, n=1):
        return self


class _FakeReg:
    def __enter__(self):
        return self

    def __exit__(self, *a):
        return False


class MD(dict):

    def __missing__(self, k):
        return 0


class _FakeEng:
    def wait_ge(self, *a, **k):
        return None

    def register(self, *a, **k):
        return _FakeReg()

    def __getattr__(self, name):
        if name in ("reg_load", "reg_alu", "reg_add", "reg_mov"):
            return lambda *a, **k: None
        return lambda *a, **k: _FakeInst()


def reg_off(eng, reg):
    return 0 if isinstance(eng, _FakeEng) else reg


def run_twopass(p, thunks, grp_size=500):
    c0 = dict(p.c)
    fake = _FakeEng()
    for _, fn in thunks:
        fn(fake)
    p.c.clear()
    p.c.update(c0)
    for engname in ("gp", "sp", "pe", "sc", "ve"):
        fns = [f for e, f in thunks if e == engname]
        for i in range(0, len(fns), grp_size):
            grp = fns[i:i + grp_size]

            def sec(eng, grp=grp):
                for f in grp:
                    f(eng)
            getattr(p, engname)(sec)


def emit_barrier(p):
    snap = dict(p.c)
    for secname in ("gp", "sp", "pe", "sc", "ve"):
        def f(eng, snap=snap):
            for nm in ("gp", "sp", "pe", "act", "dve"):
                eng.wait_ge(p.sems[nm], snap[nm])
        getattr(p, secname)(f)


def final_wait(p):
    def f(eng):
        eng.wait_ge(p.sems["sp"], p.c["sp"])
        eng.wait_ge(p.sems["gp"], p.c["gp"])
    p.sp(f)

    def f2(eng):
        eng.wait_ge(p.sems["sp"], p.c["sp"])
    p.gp(f2)



def build_ffn0_prog(dbg=False):
    p = Prog().open()
    x_in = p.dram_in("x_in", [D, T], F32)
    x_out = p.dram_out("x_out", [D, T], F32)
    if dbg:
        p.dram_out("xln_dbg", [128, DC * T], BF16)
        p.dram_out("wfm_dbg", [16, T], BF16)
        p.dram_out("hsl_dbg", [128, EIC * 512], BF16)
    decl_ffn0_inputs(p)
    S = alloc_common(p)
    emit_ffn0(p, S, x_in, x_out, dbg=dbg)
    final_wait(p)
    return p


def prep_ffn0_inputs(inp):
    gw = np.asarray(inp["L0_ffn_gate_w"], np.float32)
    gwT_blk = np.ascontiguousarray(
        gw.reshape(16, DC, 128).transpose(2, 1, 0)).reshape(128, DC * 16)
    expT = np.zeros((128, EI), dtype=BF)
    for e2 in range(NEXP):
        expT[e2, e2 * INTER:(e2 + 1) * INTER] = 1
    return {
        "u_blk": wblocks_oi(inp["L0_ffn_u"]),
        "g_blk": wblocks_oi(inp["L0_ffn_g"]),
        "d_blk": wblocks_oi(inp["L0_ffn_d"]),
        "gwT_blk": gwT_blk,
        "expT": expT,
    }


def run_ffn0(x_full, inp, trace=False, dbg=False):
    pr = build_ffn0_prog(dbg=dbg)
    wmaps = prep_ffn0_inputs(inp)
    consts = host_consts()
    in_maps = []
    for c in range(NCORES):
        shard = np.ascontiguousarray(np.asarray(x_full, np.float32)[c * T:(c + 1) * T, :].T)
        m = {"x_in": shard}
        m.update(wmaps)
        m.update(consts)
        in_maps.append(m)
    res = run_bass_kernel_spmd(pr.nc, in_maps, core_ids=list(range(NCORES)), trace=trace)
    pr.close()
    out = np.empty((NCORES * T, D), np.float32)
    for c in range(NCORES):
        out[c * T:(c + 1) * T, :] = res.results[c]["x_out"].T
    return out, res





def decl_attn0_inputs(p):
    p.dram_in("wqsel_blk", [8, 128, D], F32)
    p.dram_in("bq_sel", [128, 8], F32)
    p.dram_in("m_in", [8, 128, 128], F32)
    p.dram_in("z_in", [128, 8], F32)
    p.dram_in("wo_blk", [DC, 128, D], F32)


def alloc_attn0_outer(p, S, ctx):
    S["rfm"] = p.sb("rfm", [128, 8 * T], BF16, ctx)


def alloc_attn0_qr(p, S, ctx):
    S["wq"] = p.sb("wq", [128, 2 * D], BF16, ctx)
    S["sigq"] = p.sb("sigq", [128, 8 * T], BF16, ctx)
    S["bqs"] = p.sb("bqs", [128, 8], F32, ctx)
    S["sia"] = p.sb("sia", [128, 2 * 512], F32, ctx)
    S["sib"] = p.sb("sib", [128, 2 * 512], F32, ctx)
    S["sic"] = p.sb("sic", [128, 2 * 512], F32, ctx)
    S["maug"] = p.sb("maug", [128, 8 * 132], BF16, ctx)
    S["mst"] = p.sb("mst", [128, 128], F32, ctx)
    S["zst"] = p.sb("zst", [128, 8], F32, ctx)
    S["rtm"] = p.sb("rtm", [128, 2 * 128], F32, ctx)


def alloc_attn0_o(p, S, ctx):
    S["wo"] = p.sb("wo", [128, 2 * D], BF16, ctx)
    S["eps"] = p.sb("eps", [128, 2 * 512], F32, ctx)
    S["xep"] = p.sb("xep", [128, 2 * 512], F32, ctx)
    S["xo"] = p.sb("xo", [128, 2 * 512], F32, ctx)


def emit_attn0(p, S, x_src, x_dst):
    P = S["p"]
    outer_ctx = p.scope()
    alloc_attn0_outer(p, S, outer_ctx)
    ln_ctx = p.scope()
    alloc_ln(p, S, ln_ctx, gate=False)
    ms = emit_ln(p, S, x_src, gate=False)
    xln_done = ms["xln"]
    ln_ctx.close()

    q_ctx = p.scope()
    alloc_attn0_qr(p, S, q_ctx)
    wq_blk, wo_blk = p.H["wqsel_blk"], p.H["wo_blk"]
    sigq, wq = S["sigq"], S["wq"]
    qwld, qmm, qv1, qs, qsig = {}, {}, {}, {}, {}

    for h in range(8):
        buf = h % 2

        def f_gp(eng, h=h, buf=buf):
            if h == 0:
                W(p, eng, "dve", xln_done())
                p.inc(eng.dma_start(out=S["bqs"][:, :], in_=p.H["bq_sel"]), "gp", 16)
            if h >= 2:
                W(p, eng, "pe", qmm[(h - 2, 1)])
            qwld[h] = p.inc(eng.dma_start(out=wq[:, buf * D:(buf + 1) * D],
                                          in_=wq_blk[h, :, :]), "gp", 16)
        p.gp(f_gp)

        for tb in range(2):
            idx = h * 2 + tb
            bank = idx % 4

            def f_pe(eng, h=h, tb=tb, idx=idx, bank=bank, buf=buf):
                W(p, eng, "gp", qwld[h])
                if idx >= 4:
                    W(p, eng, "dve", qv1[idx - 4])
                i = None
                for c in range(DC):
                    i = eng.matmul(P[bank][:, :], wq[:, buf * D + c * 128:buf * D + (c + 1) * 128],
                                   S["xln"][:, c * T + tb * 512:c * T + (tb + 1) * 512],
                                   start=(c == 0), stop=(c == DC - 1))
                qmm[(h, tb)] = p.inc(i, "pe")
            p.pe(f_pe)

            def f_v1(eng, h=h, tb=tb, idx=idx, bank=bank):
                W(p, eng, "pe", qmm[(h, tb)])
                if idx >= 2:
                    W(p, eng, "dve", qsig[idx - 2])
                par = idx % 2
                bq = S["bqs"][:, h:h + 1]
                eng.tensor_scalar(S["sia"][:, par * 512:(par + 1) * 512], P[bank][:, :],
                                  bq, 0.0, ALU.add, ALU.min)
                qv1[idx] = p.inc(eng.tensor_scalar(S["sib"][:, par * 512:(par + 1) * 512],
                                                   P[bank][:, :], bq, 0.0, ALU.add, ALU.max), "dve")
            p.ve(f_v1)

            def f_s(eng, idx=idx):
                W(p, eng, "dve", qv1[idx])
                par = idx % 2
                qs[idx] = p.inc(eng.activation(S["sic"][:, par * 512:(par + 1) * 512],
                                               S["sia"][:, par * 512:(par + 1) * 512], AF.Exp), "act")
            p.sc(f_s)

            def f_v2(eng, h=h, tb=tb, idx=idx):
                W(p, eng, "act", qs[idx])
                par = idx % 2
                qsig[idx] = p.inc(eng.tensor_add(
                    sigq[:, h * T + tb * 512:h * T + (tb + 1) * 512],
                    S["sic"][:, par * 512:(par + 1) * 512],
                    S["sib"][:, par * 512:(par + 1) * 512]), "dve")
            p.ve(f_v2)

    q_last = qsig[15]
    maug, rfm, rtm = S["maug"], S["rfm"], S["rtm"]
    mld, mcast = {}, {}

    def f_mld(eng):
        mld[0] = p.inc(eng.dma_start(out=S["zst"][:, :], in_=p.H["z_in"]), "gp", 16)
    p.gp(f_mld)

    for h in range(8):
        def f_mh(eng, h=h):
            if h >= 1:
                W(p, eng, "act", mcast[h - 1])
            mld[h + 1] = p.inc(eng.dma_start(out=S["mst"][:, :], in_=p.H["m_in"][h, :, :]), "gp", 16)
        p.gp(f_mh)

        def f_mc(eng, h=h):
            W(p, eng, "gp", mld[h + 1])
            if h == 0:
                W(p, eng, "gp", mld[0])
            eng.activation(maug[:, h * 132:h * 132 + 128], S["mst"][:, :], AF.Copy)
            mcast[h] = p.inc(eng.activation(maug[:, h * 132 + 128:h * 132 + 129],
                                            S["zst"][:, h:h + 1], AF.Copy), "act")
        p.sc(f_mc)

    rmm, rv, rtp, rfv = {}, {}, {}, {}
    for h in range(8):
        for tc in range(8):
            idx = h * 8 + tc
            bank = 4 + (idx % 2)

            def f_rmm(eng, h=h, tc=tc, idx=idx, bank=bank):
                if idx == 0:
                    W(p, eng, "act", mcast[7])
                    W(p, eng, "dve", q_last)
                if idx >= 2:
                    W(p, eng, "dve", rv[idx - 2])
                rmm[idx] = p.inc(eng.matmul(
                    P[bank][:, 0:129],
                    sigq[:, h * T + tc * 128:h * T + (tc + 1) * 128],
                    maug[:, h * 132:h * 132 + 129], start=True, stop=True), "pe")
            p.pe(f_rmm)

            def f_rv(eng, idx=idx, bank=bank):
                W(p, eng, "pe", rmm[idx])
                par = idx % 2
                den = S["sm1"][:, 128:129]
                vchain(p, eng, eng.tensor_scalar_add(den, P[bank][:, 128:129], 1e-6))
                vchain(p, eng, eng.reciprocal(S["sm1"][:, 129:130], den))
                rv[idx] = p.inc(eng.tensor_scalar_mul(
                    rtm[:, par * 128:(par + 1) * 128], P[bank][:, 0:128],
                    S["sm1"][:, 129:130]), "dve")
            p.ve(f_rv)

            def f_rtp(eng, idx=idx, h=h, tc=tc):
                W(p, eng, "dve", rv[idx])
                par = idx % 2
                bank2 = P[6] if idx % 2 == 0 else P[7]
                if idx >= 2:
                    W(p, eng, "dve", rfv[idx - 2])
                rtp[idx] = p.inc(eng.transpose(bank2[:, 0:128], rtm[:, par * 128:(par + 1) * 128],
                                               S["identf"][:, :]), "pe")
            p.pe(f_rtp)

            def f_rfv(eng, idx=idx, h=h, tc=tc):
                W(p, eng, "pe", rtp[idx])
                bank2 = P[6] if idx % 2 == 0 else P[7]
                rfv[idx] = p.inc(eng.tensor_copy(
                    rfm[:, h * T + tc * 128:h * T + (tc + 1) * 128], bank2[:, 0:128]), "dve")
            p.ve(f_rfv)

    r_last = lambda: rfv[63]

    q_ctx.close()
    o_ctx = p.scope()
    alloc_attn0_o(p, S, o_ctx)
    wo = S["wo"]
    owld, omm, ep_s, ep_v, ep_x, ep_st = {}, {}, {}, {}, {}, {}

    for o in range(DC):
        buf = o % 2

        def f_gpo(eng, o=o, buf=buf):
            if o == 0:
                W(p, eng, "dve", r_last())
            if o >= 2:
                W(p, eng, "pe", omm[(o - 2, 1)])
            owld[o] = p.inc(eng.dma_start(out=wo[:, buf * D:(buf + 1) * D],
                                          in_=wo_blk[o, :, :]), "gp", 16)
        p.gp(f_gpo)

        for tb in range(2):
            idx = o * 2 + tb
            bank = idx % 4

            def f_spx(eng, idx=idx, o=o, tb=tb):
                if idx == 0:
                    W(p, eng, "dve", r_last())
                if idx >= 2:
                    W(p, eng, "dve", ep_v[idx - 2])
                ep_x[idx] = p.inc(eng.dma_start(
                    out=S["xep"][:, (idx % 2) * 512:((idx % 2) + 1) * 512],
                    in_=x_src[o * 128:(o + 1) * 128, tb * 512:(tb + 1) * 512]), "sp", 16)
            p.sp(f_spx)

            def f_peo(eng, o=o, tb=tb, idx=idx, bank=bank, buf=buf):
                W(p, eng, "gp", owld[o])
                if idx < 2:
                    W(p, eng, "dve", r_last())
                if idx >= 4:
                    W(p, eng, "act", ep_s[idx - 4])
                i = None
                for cc in range(DC):
                    i = eng.matmul(P[bank][:, :], wo[:, buf * D + cc * 128:buf * D + (cc + 1) * 128],
                                   rfm[:, (cc // 4) * T + tb * 512:(cc // 4) * T + (tb + 1) * 512],
                                   start=(cc == 0), stop=(cc == DC - 1))
                omm[(o, tb)] = p.inc(i, "pe")
            p.pe(f_peo)

            def f_eps(eng, idx=idx, bank=bank):
                W(p, eng, "pe", omm[(idx // 2, idx % 2)])
                if idx >= 2:
                    W(p, eng, "dve", ep_v[idx - 2])
                ep_s[idx] = p.inc(eng.activation(S["eps"][:, (idx % 2) * 512:((idx % 2) + 1) * 512],
                                                 P[bank][:, :], AF.Copy), "act")
            p.sc(f_eps)

            def f_epv(eng, idx=idx):
                W(p, eng, "act", ep_s[idx])
                W(p, eng, "sp", ep_x[idx])
                if idx >= 2:
                    W(p, eng, "sp", ep_st[idx - 2])
                ep_v[idx] = p.inc(eng.tensor_add(
                    S["xo"][:, (idx % 2) * 512:((idx % 2) + 1) * 512],
                    S["eps"][:, (idx % 2) * 512:((idx % 2) + 1) * 512],
                    S["xep"][:, (idx % 2) * 512:((idx % 2) + 1) * 512]), "dve")
            p.ve(f_epv)

            def f_epst(eng, idx=idx, o=o, tb=tb):
                W(p, eng, "dve", ep_v[idx])
                ep_st[idx] = p.inc(eng.dma_start(
                    out=x_dst[o * 128:(o + 1) * 128, tb * 512:(tb + 1) * 512],
                    in_=S["xo"][:, (idx % 2) * 512:((idx % 2) + 1) * 512]), "sp", 16)
            p.sp(f_epst)

    o_ctx.close()
    outer_ctx.close()
    return lambda: ep_st[DC * 2 - 1]


def build_attn0_prog():
    p = Prog().open()
    x_in = p.dram_in("x_in", [D, T], F32)
    x_out = p.dram_out("x_out", [D, T], F32)
    decl_attn0_inputs(p)
    S = alloc_common(p)
    emit_attn0(p, S, x_in, x_out)
    final_wait(p)
    return p


def prep_attn0_inputs(inp):
    Wq = np.asarray(inp["L0_attn_Wq"], np.float32)
    bq = np.asarray(inp["L0_attn_bq"], np.float32)
    sel = np.concatenate([np.arange(4 * k * 128, (4 * k + 1) * 128) for k in range(8)])
    Wq_sel = Wq[sel]
    bq_sel = bq[sel]
    return {
        "wqsel_blk": wblocks_oi(Wq_sel),
        "bq_sel": np.ascontiguousarray(bq_sel.reshape(8, 128).T),
        "m_in": np.asarray(inp["L0_attn_M"], np.float32),
        "z_in": np.ascontiguousarray(np.asarray(inp["L0_attn_z"], np.float32).T),
        "wo_blk": wblocks_oi(np.asarray(inp["L0_attn_Wo"], np.float32)),
    }


def run_attn0(x_full, inp, trace=False):
    pr = build_attn0_prog()
    wmaps = prep_attn0_inputs(inp)
    consts = host_consts()
    in_maps = []
    for c in range(NCORES):
        shard = np.ascontiguousarray(np.asarray(x_full, np.float32)[c * T:(c + 1) * T, :].T)
        m = {"x_in": shard}
        m.update(wmaps)
        m.update(consts)
        in_maps.append(m)
    res = run_bass_kernel_spmd(pr.nc, in_maps, core_ids=list(range(NCORES)), trace=trace)
    pr.close()
    out = np.empty((NCORES * T, D), np.float32)
    for c in range(NCORES):
        out[c * T:(c + 1) * T, :] = res.results[c]["x_out"].T
    return out, res





def decl_ffn1_inputs(p):
    p.dram_in("qw_in", [512, D], I32)
    p.dram_in("qz_in", [32, 512], I32)
    p.dram_in("sc_in", [32, D], F16)
    p.dram_in("gidx_p32", [32, D], F32)
    p.dram_in("gidx_fm", [D, 32], F32)
    p.dram_in("iota32", [32, 1], F32)
    p.dram_in("iotaf", [128, 32], F32)
    p.dram_in("bias_h", [128, 32], F16)
    p.dram_int("wdq", [32, 128, DC * 128], BF16)


def alloc_ffn1_outer(p, S, ctx):
    S["negB"] = p.sb("negB", [128, D], BF16, ctx)
    S["onehT"] = p.sb("onehT", [128, D], BF16, ctx)
    S["onehF"] = p.sb("onehF", [128, DC * 32], BF16, ctx)
    S["xgs"] = p.sb("xgs", [128, T], BF16, ctx)
    S["bh"] = p.sb("bh", [128, 32], F32, ctx)


def alloc_ffn1_deq(p, S, ctx):
    S["scb"] = p.sb("scb", [128, D], BF16, ctx)


def alloc_ffn1_deq_prep(p, S, ctx):
    S["scf"] = p.sb("scf", [32, D], F32, ctx)
    S["zq"] = p.sb("zq", [32, D], I32, ctx)
    S["zqf"] = p.sb("zqf", [32, D], F32, ctx)
    S["qzs"] = p.sb("qzs", [32, 512], I32, ctx)
    S["zsh"] = p.sb("zsh", [32, 512], I32, ctx)
    S["bhf"] = p.sb("bhf", [128, 32], F16, ctx)


def alloc_ffn1_deq_main(p, S, ctx):
    S["qw"] = p.sb("qw", [128, D], I32, ctx)
    S["qn"] = p.sb("qn", [128, D], I32, ctx)
    S["qn2"] = p.sb("qn2", [128, D], I32, ctx)
    S["qfb"] = p.sb("qfb", [128, 2 * 512], BF16, ctx)
    S["wst"] = p.sb("wst", [128, D], BF16, ctx)


def alloc_ffn1_main(p, S, ctx):
    S["wn"] = p.sb("wn", [128, 2 * D], BF16, ctx)
    S["eps"] = p.sb("eps", [128, 2 * 512], F32, ctx)
    S["xep"] = p.sb("xep", [128, 2 * 512], F32, ctx)
    S["xo"] = p.sb("xo", [128, 2 * 512], F32, ctx)


def emit_ffn1(p, S, x_src, x_dst, dequant):
    P = S["p"]
    outer_ctx = p.scope()
    alloc_ffn1_outer(p, S, outer_ctx)

    ohh = {}
    deq_ctx = p.scope()
    if dequant:
        alloc_ffn1_deq(p, S, deq_ctx)
    oh_ctx = p.scope()
    S["gfm"] = p.sb("gfm", [128, DC * 32], F32, oh_ctx)
    S["scf2"] = p.sb("scf2", [32, D], F32, oh_ctx)
    S["io32"] = p.sb("io32", [32, 1], F32, oh_ctx)
    S["iof"] = p.sb("iof", [128, 32], F32, oh_ctx)

    def f_ohld(eng):
        p.inc(eng.dma_start(out=S["gfm"][:, :].rearrange("p (c e) -> p c e", c=DC),
                            in_=p.H["gidx_fm"].rearrange("(c p) e -> p c e", p=128)), "gp", 16)
        p.inc(eng.dma_start(out=S["scf2"][:, :], in_=p.H["gidx_p32"]), "gp", 16)
        p.inc(eng.dma_start(out=S["io32"][:, :], in_=p.H["iota32"]), "gp", 16)
        ohh["ld"] = p.inc(eng.dma_start(out=S["iof"][:, :], in_=p.H["iotaf"]), "gp", 16)
    p.gp(f_ohld)

    def f_oh(eng):
        W(p, eng, "gp", ohh["ld"])
        vchain(p, eng, eng.memset(S["onehT"][:, :], 0.0))
        vchain(p, eng, eng.memset(S["negB"][:, :], 0.0))
        vchain(p, eng, eng.memset(S["xgs"][:, :], 0.0))
        vchain(p, eng, eng.tensor_single_scalar(S["onehT"][0:32, :], S["scf2"][:, :],
                                               S["io32"][:, :], ALU.is_equal))
        for c in range(DC):
            eng.tensor_tensor(S["onehF"][:, c * 32:(c + 1) * 32],
                              S["gfm"][:, c * 32:(c + 1) * 32], S["iof"][:, :], ALU.is_equal)
        ohh["oh"] = p.inc(eng.engine_nop(), "dve")
    p.ve(f_oh)

    if dequant:
        dq, deq_main_ctx = emit_dequant2(p, S, ohh, oh_ctx)
    else:
        dq = {"laststore": lambda: None, "lastev": lambda: None}
        deq_main_ctx = None
        prep_ctx = p.scope()
        alloc_ffn1_deq_prep(p, S, prep_ctx)
        dd = {}

        def f_ld2(eng):
            p.inc(eng.dma_start(out=S["qzs"][:, :], in_=p.H["qz_in"]), "gp", 16)
            p.inc(eng.dma_start(out=S["scf"][:, :], in_=p.H["sc_in"]), "gp", 16)
            dd["ld"] = p.inc(eng.dma_start(out=S["bhf"][:, :], in_=p.H["bias_h"]), "gp", 16)
        p.gp(f_ld2)

        def f_prep2(eng):
            W(p, eng, "gp", dd["ld"])
            for jj in range(8):
                vchain(p, eng, eng.tensor_single_scalar(S["zsh"][:, :], S["qzs"][:, :],
                                                        4 * jj, ALU.logical_shift_right))
                zq_ap = bass.AP(S["zq"], jj, [[S["zq"].ap().ap[0][0], 32], [8, 512]])
                vchain(p, eng, eng.tensor_single_scalar(zq_ap, S["zsh"][:, :], 15, ALU.bitwise_and))
            vchain(p, eng, eng.tensor_copy(S["zqf"][:, :], S["zq"][:, :]))
            vchain(p, eng, eng.tensor_scalar_add(S["zqf"][:, :], S["zqf"][:, :], 1.0))
            vchain(p, eng, eng.tensor_mul(S["zqf"][:, :], S["zqf"][:, :], S["scf"][:, :]))
            vchain(p, eng, eng.tensor_scalar_mul(S["negB"][0:32, :], S["zqf"][:, :], -1.0))
            vchain(p, eng, eng.tensor_copy(S["bh"][:, :], S["bhf"][:, :]))
            dd["prep"] = vchain(p, eng, eng.tensor_scalar_mul(S["bh"][:, :], S["bh"][:, :], 0.5))
        p.ve(f_prep2)
        dq["negb"] = lambda: dd["prep"]
        prep_ctx.close()
        oh_ctx.close()

    ln_ctx = p.scope()
    alloc_ln(p, S, ln_ctx, gate=False)
    ms = emit_ln(p, S, x_src, perm_stride=True, gate=False,
                 start_gate=(("dve", dq["negb"]),) if not dequant else ())
    xln_done = ms["xln"]
    ln_ctx.close()
    if deq_main_ctx is not None:
        deq_main_ctx.close()
    deq_ctx.close()

    xg = {}

    def f_xg(eng):
        W(p, eng, "dve", xln_done())
        W(p, eng, "dve", ohh["oh"])
        lev = dq["lastev"]()
        if lev is not None:
            W(p, eng, "dve", lev)
        i = None
        for tb in range(2):
            for c in range(DC):
                i = eng.matmul(P[6 + tb][0:32, :], S["onehF"][:, c * 32:(c + 1) * 32],
                               S["xln"][:, c * T + tb * 512:c * T + (tb + 1) * 512],
                               start=(c == 0), stop=(c == DC - 1))
        xg["mm"] = p.inc(i, "pe")
    p.pe(f_xg)

    def f_xgv(eng):
        W(p, eng, "pe", xg["mm"])
        eng.tensor_copy(S["xgs"][0:32, 0:512], P[6][0:32, :])
        xg["ev"] = p.inc(eng.tensor_copy(S["xgs"][0:32, 512:T], P[7][0:32, :]), "dve")
    p.ve(f_xgv)

    main_ctx = p.scope()
    alloc_ffn1_main(p, S, main_ctx)
    wn = S["wn"]
    wld, mm_done, ep_s, ep_v, ep_x, ep_st = {}, {}, {}, {}, {}, {}

    for n in range(DC):
        buf = n % 2

        def f_wld(eng, n=n, buf=buf):
            if n == 0:
                ls = dq["laststore"]()
                if ls is not None:
                    W(p, eng, "sp", ls)
                W(p, eng, "dve", xln_done())
            if n >= 2:
                W(p, eng, "pe", mm_done[(n - 2, 1)])
            wld[n] = p.inc(eng.dma_start(out=wn[:, buf * D:(buf + 1) * D],
                                         in_=p.H["wdq"][n, :, :]), "gp", 16)
        p.gp(f_wld)

        for tb in range(2):
            idx = n * 2 + tb
            bank = idx % 4

            def f_spx(eng, idx=idx, n=n, tb=tb):
                if idx == 0:
                    W(p, eng, "dve", xln_done())
                if idx >= 2:
                    W(p, eng, "dve", ep_v[idx - 2])
                ep_x[idx] = p.inc(eng.dma_start(
                    out=S["xep"][:, (idx % 2) * 512:((idx % 2) + 1) * 512],
                    in_=x_src[n * 128:(n + 1) * 128, tb * 512:(tb + 1) * 512]), "sp", 16)
            p.sp(f_spx)

            def f_pem(eng, n=n, tb=tb, idx=idx, bank=bank, buf=buf):
                W(p, eng, "gp", wld[n])
                if idx == 0:
                    W(p, eng, "dve", xg["ev"])
                    W(p, eng, "dve", dq["negb"]())
                if idx >= 4:
                    W(p, eng, "dve", ep_s[idx - 4])
                for k in range(DC):
                    eng.matmul(P[bank][:, :], wn[:, buf * D + k * 128:buf * D + (k + 1) * 128],
                               S["xln"][:, k * T + tb * 512:k * T + (tb + 1) * 512],
                               start=(k == 0), stop=False)
                i = eng.matmul(P[bank][:, :], S["negB"][:, n * 128:(n + 1) * 128],
                               S["xgs"][:, tb * 512:(tb + 1) * 512],
                               start=False, stop=True)
                mm_done[(n, tb)] = p.inc(i, "pe")
            p.pe(f_pem)

            def f_eps(eng, idx=idx, bank=bank, n=n):
                W(p, eng, "pe", mm_done[(n, idx % 2)])
                if idx >= 2:
                    W(p, eng, "dve", ep_v[idx - 2])
                ep_s[idx] = p.inc(eng.tensor_scalar(S["eps"][:, (idx % 2) * 512:((idx % 2) + 1) * 512],
                                                    P[bank][:, :], 0.5, S["bh"][:, n:n + 1],
                                                    ALU.mult, ALU.add), "dve")
            p.ve(f_eps)

            def f_epv(eng, idx=idx):
                W(p, eng, "dve", ep_s[idx])
                W(p, eng, "sp", ep_x[idx])
                if idx >= 2:
                    W(p, eng, "sp", ep_st[idx - 2])
                ep_v[idx] = p.inc(eng.tensor_add(
                    S["xo"][:, (idx % 2) * 512:((idx % 2) + 1) * 512],
                    S["eps"][:, (idx % 2) * 512:((idx % 2) + 1) * 512],
                    S["xep"][:, (idx % 2) * 512:((idx % 2) + 1) * 512]), "dve")
            p.ve(f_epv)

            def f_epst(eng, idx=idx, n=n, tb=tb):
                W(p, eng, "dve", ep_v[idx])
                ep_st[idx] = p.inc(eng.dma_start(
                    out=x_dst[n * 128:(n + 1) * 128, tb * 512:(tb + 1) * 512],
                    in_=S["xo"][:, (idx % 2) * 512:((idx % 2) + 1) * 512]), "sp", 16)
            p.sp(f_epst)

    main_ctx.close()
    outer_ctx.close()
    return lambda: ep_st[DC * 2 - 1]


def emit_dequant2(p, S, ohh, oh_ctx):
    P = S["p"]
    d = {}
    prep_ctx = p.scope()
    alloc_ffn1_deq_prep(p, S, prep_ctx)

    def f_ld(eng):
        p.inc(eng.dma_start(out=S["qzs"][:, :], in_=p.H["qz_in"]), "gp", 16)
        p.inc(eng.dma_start(out=S["scf"][:, :], in_=p.H["sc_in"]), "gp", 16)
        d["ld"] = p.inc(eng.dma_start(out=S["bhf"][:, :], in_=p.H["bias_h"]), "gp", 16)
    p.gp(f_ld)

    def f_prep(eng):
        W(p, eng, "gp", d["ld"])
        vchain(p, eng, eng.memset(S["scb"][:, :], 0.0))
        vchain(p, eng, eng.tensor_copy(S["scb"][0:32, :], S["scf"][:, :]))
        for jj in range(8):
            vchain(p, eng, eng.tensor_single_scalar(S["zsh"][:, :], S["qzs"][:, :],
                                                    4 * jj, ALU.logical_shift_right))
            zq_ap = bass.AP(S["zq"], jj, [[S["zq"].ap().ap[0][0], 32], [8, 512]])
            vchain(p, eng, eng.tensor_single_scalar(zq_ap, S["zsh"][:, :], 15, ALU.bitwise_and))
        vchain(p, eng, eng.tensor_copy(S["zqf"][:, :], S["zq"][:, :]))
        vchain(p, eng, eng.tensor_scalar_add(S["zqf"][:, :], S["zqf"][:, :], 1.0))
        vchain(p, eng, eng.tensor_mul(S["zqf"][:, :], S["zqf"][:, :], S["scf"][:, :]))
        vchain(p, eng, eng.tensor_scalar_mul(S["negB"][0:32, :], S["zqf"][:, :], -1.0))
        vchain(p, eng, eng.tensor_copy(S["bh"][:, :], S["bhf"][:, :]))
        d["prep"] = vchain(p, eng, eng.tensor_scalar_mul(S["bh"][:, :], S["bh"][:, :], 0.5))
    p.ve(f_prep)

    prep_ctx.close()
    oh_ctx.close()
    main_ctx = p.scope()
    alloc_ffn1_deq_main(p, S, main_ctx)
    qld, qsh, wmm, wev, wstd = {}, {}, {}, {}, {}
    for c in range(DC):
        j, r = c // 4, c % 4

        def f_qld(eng, c=c, r=r):
            if c == 0:
                W(p, eng, "dve", d["prep"])
            if c >= 1:
                W(p, eng, "dve", qsh[c - 1])
            qld[c] = p.inc(eng.dma_start(out=S["qw"][:, :],
                                         in_=p.H["qw_in"][128 * r:128 * (r + 1), :]), "sp", 16)
        p.sp(f_qld)

        def f_qsh(eng, c=c, j=j):
            W(p, eng, "sp", qld[c])
            if c >= 1:
                W(p, eng, "dve", wev[(c - 1, 7)])
            vchain(p, eng, eng.tensor_single_scalar(S["qn"][:, :], S["qw"][:, :],
                                                    4 * j, ALU.logical_shift_right))
            qsh[c] = p.inc(eng.tensor_single_scalar(S["qn2"][:, :], S["qn"][:, :], 15,
                                                    ALU.bitwise_and), "dve")
        p.ve(f_qsh)

        for ns in range(8):
            def f_amm(eng, c=c, ns=ns):
                if c == 0 and ns == 0:
                    W(p, eng, "dve", d["prep"])
                prev = wev.get((c, ns - 2)) if ns >= 2 else (wev.get((c - 1, 6 + ns)) if c >= 1 else None)
                if prev is not None:
                    W(p, eng, "dve", prev)
                bank = P[6] if ns % 2 == 0 else P[7]
                wmm[(c, ns)] = p.inc(eng.matmul(bank[:, :], S["onehT"][:, c * 128:(c + 1) * 128],
                                                S["scb"][:, ns * 512:(ns + 1) * 512],
                                                start=True, stop=True), "pe")
            p.pe(f_amm)

            def f_wev(eng, c=c, ns=ns):
                W(p, eng, "pe", wmm[(c, ns)])
                W(p, eng, "dve", qsh[c])
                if ns == 0 and c >= 1:
                    W(p, eng, "sp", wstd[c - 1])
                bank = P[6] if ns % 2 == 0 else P[7]
                par = ns % 2
                vchain(p, eng, eng.tensor_copy(S["qfb"][:, par * 512:(par + 1) * 512],
                                               S["qn2"][:, ns * 512:(ns + 1) * 512]))
                wev[(c, ns)] = p.inc(eng.tensor_mul(S["wst"][:, ns * 512:(ns + 1) * 512],
                                                    S["qfb"][:, par * 512:(par + 1) * 512],
                                                    bank[:, :]), "dve")
            p.ve(f_wev)

        def f_wst(eng, c=c):
            W(p, eng, "dve", wev[(c, 7)])
            for gq in range(8):
                i = eng.dma_start(
                    out=p.H["wdq"][gq * 4:(gq + 1) * 4, :, c * 128:(c + 1) * 128].rearrange("n p m -> p n m"),
                    in_=S["wst"][:, gq * 512:(gq + 1) * 512].rearrange("p (n m) -> p n m", n=4))
                wstd[c] = p.inc(i, "sp", 16)
        p.sp(f_wst)

    d["laststore"] = lambda: wstd[DC - 1]
    d["lastev"] = lambda: wev[(DC - 1, 7)]
    d["negb"] = lambda: d["prep"]
    return d, main_ctx


def build_ffn1_prog():
    p = Prog().open()
    x_in = p.dram_in("x_in", [D, T], F32)
    x_out = p.dram_out("x_out", [D, T], F32)
    decl_ffn1_inputs(p)
    S = alloc_common(p)
    emit_ffn1(p, S, x_in, x_out, dequant=True)
    final_wait(p)
    return p


def perm_kprime():
    idx = np.empty(D, np.int64)
    for c in range(DC):
        j, r = c // 4, c % 4
        pvec = np.arange(128)
        idx[c * 128:(c + 1) * 128] = 8 * (128 * r + pvec) + j
    return idx


def prep_ffn1_inputs(inp):
    gidx = np.asarray(inp["L1_ffn_g_idx"], np.int64)
    kp = perm_kprime()
    gperm = gidx[kp].astype(np.float32)
    bias = np.asarray(inp["L1_ffn_bias"])
    return {
        "qw_in": np.asarray(inp["L1_ffn_qweight"], np.int32),
        "qz_in": np.asarray(inp["L1_ffn_qzeros"], np.int32),
        "sc_in": np.asarray(inp["L1_ffn_scales"], np.float16),
        "gidx_p32": np.ascontiguousarray(np.broadcast_to(gperm, (32, D))),
        "gidx_fm": np.ascontiguousarray(np.broadcast_to(gperm[:, None], (D, 32))),
        "iota32": np.arange(32, dtype=np.float32).reshape(32, 1),
        "iotaf": np.ascontiguousarray(np.broadcast_to(np.arange(32, dtype=np.float32), (128, 32))),
        "bias_h": np.ascontiguousarray(np.asarray(bias, np.float16).reshape(32, 128).T),
    }


def run_ffn1(x_full, inp, trace=False):
    pr = build_ffn1_prog()
    wmaps = prep_ffn1_inputs(inp)
    consts = host_consts()
    in_maps = []
    for c in range(NCORES):
        shard = np.ascontiguousarray(np.asarray(x_full, np.float32)[c * T:(c + 1) * T, :].T)
        m = {"x_in": shard}
        m.update(wmaps)
        m.update(consts)
        in_maps.append(m)
    res = run_bass_kernel_spmd(pr.nc, in_maps, core_ids=list(range(NCORES)), trace=trace)
    pr.close()
    out = np.empty((NCORES * T, D), np.float32)
    for c in range(NCORES):
        out[c * T:(c + 1) * T, :] = res.results[c]["x_out"].T
    return out, res





def decl_qkv1_inputs(p):
    p.dram_in("wq1_blk", [DC, 128, D], F32)
    p.dram_in("wk1_blk", [8, 128, D], F32)
    p.dram_in("wv1_blk", [8, 128, D], F32)
    p.dram_in("cs_in", [128, T], F32)
    p.dram_int("qkvb", [6144, T], BF16)


def alloc_qkv1(p, S, ctx):
    S["wqb"] = p.sb("wqb", [128, 2 * D], BF16, ctx)
    S["css"] = p.sb("css", [128, T], F32, ctx)
    S["qst"] = p.sb("qst", [128, 4 * T], BF16, ctx)
    S["rta"] = p.sb("rta", [128, 512], F32, ctx)
    S["rtb"] = p.sb("rtb", [128, 512], F32, ctx)


def emit_qkv1(p, S, x_src):
    P = S["p"]
    ln_ctx = p.scope()
    alloc_ln(p, S, ln_ctx, gate=False)
    ms = emit_ln(p, S, x_src, gate=False)
    xln_done = ms["xln"]
    ln_ctx.close()
    q_ctx = p.scope()
    alloc_qkv1(p, S, q_ctx)
    qkvb = p.H["qkvb"]
    wld, mmd, rpd, std, sld = {}, {}, {}, {}, {}

    def f_cs(eng):
        W(p, eng, "dve", xln_done())
        sld["cs"] = p.inc(eng.dma_start(out=S["css"][:, :], in_=p.H["cs_in"]), "gp", 16)
    p.gp(f_cs)

    for oc in range(48):
        buf = oc % 2

        def f_w(eng, oc=oc, buf=buf):
            if oc >= 2:
                W(p, eng, "pe", mmd[(oc - 2, 1)])
            if oc < 32:
                src = p.H["wq1_blk"][oc, :, :]
            elif oc < 40:
                src = p.H["wk1_blk"][oc - 32, :, :]
            else:
                src = p.H["wv1_blk"][oc - 40, :, :]
            wld[oc] = p.inc(eng.dma_start(out=S["wqb"][:, buf * D:(buf + 1) * D], in_=src), "gp", 16)
        p.gp(f_w)

        for tb in range(2):
            idx = oc * 2 + tb
            bank = idx % 4

            def f_mm(eng, oc=oc, tb=tb, idx=idx, bank=bank, buf=buf):
                W(p, eng, "gp", wld[oc])
                if idx >= 4:
                    W(p, eng, "dve", rpd[idx - 4])
                i = None
                for c in range(DC):
                    i = eng.matmul(P[bank][:, :], S["wqb"][:, buf * D + c * 128:buf * D + (c + 1) * 128],
                                   S["xln"][:, c * T + tb * 512:c * T + (tb + 1) * 512],
                                   start=(c == 0), stop=(c == DC - 1))
                mmd[(oc, tb)] = p.inc(i, "pe")
            p.pe(f_mm)

            if oc < 40:
                def f_rope(eng, oc=oc, tb=tb, idx=idx, bank=bank):
                    W(p, eng, "pe", mmd[(oc, tb)])
                    if idx == 0:
                        W(p, eng, "gp", sld["cs"])
                    if idx >= 2:
                        W(p, eng, "sp", std[idx - 2])
                    par = idx % 2
                    cos = S["css"][0:64, tb * 512:(tb + 1) * 512]
                    sin = S["css"][64:128, tb * 512:(tb + 1) * 512]
                    dst = S["qst"][:, par * T:par * T + 512]
                    eng.tensor_tensor(S["rta"][0:64, :], P[bank][0:64, :], cos, ALU.mult)
                    vchain(p, eng, eng.tensor_tensor(S["rtb"][0:64, :], P[bank][64:128, :], sin, ALU.mult))
                    eng.tensor_sub(dst[0:64, :], S["rta"][0:64, :], S["rtb"][0:64, :])
                    eng.tensor_tensor(S["rta"][0:64, :], P[bank][0:64, :], sin, ALU.mult)
                    vchain(p, eng, eng.tensor_tensor(S["rtb"][0:64, :], P[bank][64:128, :], cos, ALU.mult))
                    rpd[idx] = p.inc(eng.tensor_add(dst[64:128, :], S["rta"][0:64, :], S["rtb"][0:64, :]), "dve")
                p.ve(f_rope)
            else:
                def f_vcp(eng, oc=oc, tb=tb, idx=idx, bank=bank):
                    W(p, eng, "pe", mmd[(oc, tb)])
                    if idx >= 2:
                        W(p, eng, "sp", std[idx - 2])
                    par = idx % 2
                    rpd[idx] = p.inc(eng.tensor_copy(S["qst"][:, par * T:par * T + 512],
                                                     P[bank][:, :]), "dve")
                p.ve(f_vcp)

            def f_st(eng, oc=oc, tb=tb, idx=idx):
                W(p, eng, "dve", rpd[idx])
                par = idx % 2
                if oc < 32:
                    row0 = oc * 128
                elif oc < 40:
                    row0 = 4096 + (oc - 32) * 128
                else:
                    row0 = 5120 + (oc - 40) * 128
                std[idx] = p.inc(eng.dma_start(
                    out=qkvb[row0:row0 + 128, tb * 512:(tb + 1) * 512],
                    in_=S["qst"][:, par * T:par * T + 512]), "sp", 16)
            p.sp(f_st)

    q_ctx.close()
    return lambda: std[95]




def decl_attn1core_inputs(p):
    p.dram_in("q_in", [512, 8 * T], BF16)
    p.dram_in("k_in", [128, 8 * T], BF16)
    p.dram_in("v_in", [128, 8 * T], BF16)
    p.dram_in("mask_d", [128, 4 * 512], BF16)
    p.dram_out("attn_out", [512, 8 * T], BF16)


def alloc_attn1core(p, S, ctx):
    S["ksl"] = p.sb("ksl", [128, 64 * 128], BF16, ctx)
    S["vfm"] = p.sb("vfm", [128, 64 * 128], BF16, ctx)
    S["vtf"] = p.sb("vtf", [128, 2 * 128], F32, ctx)
    S["vsl"] = p.sb("vsl", [128, 64 * 128], BF16, ctx)
    S["qtl"] = p.sb("qtl", [128, 2 * 512], BF16, ctx)
    S["psc"] = p.sb("psc", [128, 2 * 512], BF16, ctx)
    S["mask"] = p.sb("mask", [128, 4 * 512], BF16, ctx)
    S["aow"] = p.sb("aow", [128, 2 * 512], BF16, ctx)
    S["rbs"] = p.sb("rbs", [128, 2 * 512], F32, ctx)


def emit_attn1core(p, S, gate=(), fused=False):
    P = S["p"]
    _thunks = []

    def _mk(en):
        def deco(fn):
            _thunks.append((en, fn))
        return deco
    _add_gp, _add_sp, _add_pe, _add_sc, _add_ve = (_mk("gp"), _mk("sp"),
                                                   _mk("pe"), _mk("sc"), _mk("ve"))
    core_ctx = p.scope()
    alloc_attn1core(p, S, core_ctx)
    SC = 1.0 / np.sqrt(128.0)
    sld, qld, scd, exd, mkd, pvd, tl = MD(), MD(), MD(), MD(), MD(), MD(), MD()

    if fused:
        qall = p.H["qkv_all"]

        def f_slab(eng):
            for sgn, sgv in gate:
                W(p, eng, sgn, sgv() if callable(sgv) else sgv)
            with eng.register("ra") as ra, eng.register("rb") as rb:
                eng.reg_load(ra, S["rks"][0:1, 0:1])
                eng.reg_alu(ra, ra, 128 * T, ALU.mult)
                eng.reg_add(rb, ra, 4096 * T)
                p.inc(eng.dma_start(
                    out=S["ksl"][:, :].rearrange("p (r t) -> p r t", r=NCORES),
                    in_=bass.AP(qall.tensor, reg_off(eng, rb), [[T, 128], [6144 * T, NCORES], [1, T]])), "gp", 16)
                eng.reg_add(rb, ra, 5120 * T)
                p.inc(eng.dma_start(
                    out=S["vfm"][:, :].rearrange("p (r t) -> p r t", r=NCORES),
                    in_=bass.AP(qall.tensor, reg_off(eng, rb), [[T, 128], [6144 * T, NCORES], [1, T]])), "gp", 16)
                eng.reg_alu(ra, ra, 4, ALU.mult)
                for j in range(4):
                    eng.reg_add(rb, ra, j * 128 * T)
                    p.inc(eng.dma_start(
                        out=p.H["q_mine"][j * 128:(j + 1) * 128, :].rearrange("p (r t) -> p r t", r=NCORES),
                        in_=bass.AP(qall.tensor, reg_off(eng, rb), [[T, 128], [6144 * T, NCORES], [1, T]])), "gp", 16)
            sld["s"] = p.inc(eng.dma_start(out=S["mask"][:, :], in_=p.H["mask_d"]), "gp", 16)
        _add_gp(f_slab)
        p.H["q_in"] = p.H["q_mine"]
    else:
        def f_slab(eng):
            for sgn, sgv in gate:
                W(p, eng, sgn, sgv() if callable(sgv) else sgv)
            p.inc(eng.dma_start(out=S["ksl"][:, :], in_=p.H["k_in"]), "gp", 16)
            p.inc(eng.dma_start(out=S["vfm"][:, :], in_=p.H["v_in"]), "gp", 16)
            sld["s"] = p.inc(eng.dma_start(out=S["mask"][:, :], in_=p.H["mask_d"]), "gp", 16)
        _add_gp(f_slab)

    vt = MD()
    for sc64 in range(64):
        def f_vt1(eng, sc64=sc64):
            if sc64 == 0:
                W(p, eng, "gp", sld["s"])
            if sc64 >= 2:
                W(p, eng, "pe", vt[("t", sc64 - 2)])
            par = sc64 % 2
            vt[("c", sc64)] = p.inc(eng.tensor_copy(S["vtf"][:, par * 128:(par + 1) * 128],
                                                    S["vfm"][:, sc64 * 128:(sc64 + 1) * 128]), "dve")
        _add_ve(f_vt1)

        def f_vt2(eng, sc64=sc64):
            W(p, eng, "dve", vt[("c", sc64)])
            bank = P[6] if sc64 % 2 == 0 else P[7]
            if sc64 >= 2:
                W(p, eng, "dve", vt[("e", sc64 - 2)])
            par = sc64 % 2
            vt[("t", sc64)] = p.inc(eng.transpose(bank[:, 0:128], S["vtf"][:, par * 128:(par + 1) * 128],
                                                  S["identf"][:, :]), "pe")
        _add_pe(f_vt2)

        def f_vt3(eng, sc64=sc64):
            W(p, eng, "pe", vt[("t", sc64)])
            bank = P[6] if sc64 % 2 == 0 else P[7]
            vt[("e", sc64)] = p.inc(eng.tensor_copy(S["vsl"][:, sc64 * 128:(sc64 + 1) * 128],
                                                    bank[:, 0:128]), "dve")
        _add_ve(f_vt3)

    prev_rb = {}
    for j in range(4):
        for qb in range(16):
            jqi = j * 16 + qb
            nsc = 4 * qb + 4
            par2 = jqi % 2
            Po = P[par2]
            Pl = P[2 + par2]
            pe_scores = []
            pe_pvs = []

            def f_q(eng, j=j, qb=qb, jqi=jqi, par2=par2):
                if jqi == 0 and fused:
                    W(p, eng, "gp", sld["s"])
                if jqi >= 2:
                    W(p, eng, "pe", scd[(jqi - 2, "last")])
                src = p.H["q_in"][j * 128:(j + 1) * 128, qb * 512:(qb + 1) * 512]
                qld[jqi] = p.inc(eng.dma_start(
                    out=S["qtl"][:, par2 * 512:(par2 + 1) * 512], in_=src), "sp", 16)
            _add_sp(f_q)

            for sc in range(nsc):
                spar = sc % 2

                def f_sc(eng, jqi=jqi, sc=sc, spar=spar, par2=par2, nsc=nsc):
                    if jqi == 0 and sc == 0:
                        W(p, eng, "gp", sld["s"])
                    W(p, eng, "sp", qld[jqi])
                    if sc >= 2:
                        W(p, eng, "act", exd[(jqi, sc - 2)])
                    elif jqi >= 1:
                        W(p, eng, "act", exd[(jqi - 1, "last")])
                    i = eng.matmul(P[4 + spar][:, :], S["ksl"][:, sc * 128:(sc + 1) * 128],
                                   S["qtl"][:, par2 * 512:(par2 + 1) * 512], start=True, stop=True)
                    scd[(jqi, sc)] = p.inc(i, "pe")
                    if sc == nsc - 1:
                        scd[(jqi, "last")] = scd[(jqi, sc)]
                pe_scores.append(f_sc)

                def f_exp(eng, jqi=jqi, sc=sc, spar=spar, nsc=nsc):
                    W(p, eng, "pe", scd[(jqi, sc)])
                    if (jqi, sc - 2) in pvd:
                        W(p, eng, "pe", pvd[(jqi, sc - 2)])
                    elif sc < 2 and jqi >= 1:
                        W(p, eng, "pe", pvd[(jqi - 1, "last")])
                    i = eng.activation(S["psc"][:, spar * 512:(spar + 1) * 512],
                                       P[4 + spar][:, :], AF.Exp, bias=S["fc"][0:128, 1:2], scale=SC)
                    exd[(jqi, sc)] = p.inc(i, "act")
                    if sc == nsc - 1:
                        exd[(jqi, "last")] = exd[(jqi, sc)]
                _add_sc(f_exp)

                if sc >= 4 * qb:
                    m = sc - 4 * qb

                    def f_mk(eng, jqi=jqi, sc=sc, spar=spar, m=m):
                        W(p, eng, "act", exd[(jqi, sc)])
                        mkd[(jqi, sc)] = p.inc(eng.tensor_tensor(
                            S["psc"][:, spar * 512:(spar + 1) * 512],
                            S["psc"][:, spar * 512:(spar + 1) * 512],
                            S["mask"][:, m * 512:(m + 1) * 512], ALU.mult), "dve")
                    _add_ve(f_mk)

                def f_pv(eng, jqi=jqi, qb=qb, sc=sc, spar=spar, nsc=nsc, Po=Po, Pl=Pl):
                    if sc == 0:
                        if jqi == 0:
                            W(p, eng, "dve", vt[("e", 63)])
                        if jqi >= 2:
                            W(p, eng, "dve", tl[(jqi - 2, "ev")])
                    if (jqi, sc) in mkd:
                        W(p, eng, "dve", mkd[(jqi, sc)])
                    else:
                        W(p, eng, "act", exd[(jqi, sc)])
                    st, spp = (sc == 0), (sc == nsc - 1)
                    eng.matmul(Po[:, :], S["vsl"][:, sc * 128:(sc + 1) * 128],
                               S["psc"][:, spar * 512:(spar + 1) * 512], start=st, stop=spp)
                    i = eng.matmul(Pl[0:1, :], S["onesb"][:, :],
                                   S["psc"][:, spar * 512:(spar + 1) * 512], start=st, stop=spp)
                    pvd[(jqi, sc)] = p.inc(i, "pe")
                    if spp:
                        pvd[(jqi, "last")] = pvd[(jqi, sc)]
                pe_pvs.append(f_pv)

            _add_pe(pe_scores[0])
            if nsc > 1:
                _add_pe(pe_scores[1])
            for sc in range(nsc):
                if sc + 2 < nsc:
                    _add_pe(pe_scores[sc + 2])
                _add_pe(pe_pvs[sc])

            def f_tl1(eng, jqi=jqi, par2=par2, Pl=Pl):
                W(p, eng, "pe", pvd[(jqi, "last")])
                vchain(p, eng, eng.reciprocal(S["rbs"][0:1, par2 * 512:par2 * 512 + 512], Pl[0:1, :]))
                tl[(jqi, "rl")] = p.inc(eng.tensor_copy(S["brow"][0:1, :],
                                                        S["rbs"][0:1, par2 * 512:par2 * 512 + 512]), "dve")
            _add_ve(f_tl1)

            def f_tl2(eng, jqi=jqi):
                W(p, eng, "dve", tl[(jqi, "rl")])
                if jqi >= 1:
                    W(p, eng, "dve", tl[(jqi - 1, "ev")])
                tl[(jqi, "rb")] = p.inc(eng.matmul(P[6][:, :], S["ones128"][:, :],
                                                   S["brow"][:, :], start=True, stop=True), "pe")
            _add_pe(f_tl2)

            def f_tl3(eng, jqi=jqi, par2=par2, Po=Po):
                W(p, eng, "pe", tl[(jqi, "rb")])
                if jqi >= 2:
                    W(p, eng, "sp", tl[(jqi - 2, "st")])
                vchain(p, eng, eng.tensor_copy(S["rbs"][:, par2 * 512:(par2 + 1) * 512], P[6][:, :]))
                tl[(jqi, "ev")] = p.inc(eng.tensor_mul(
                    S["aow"][:, par2 * 512:(par2 + 1) * 512], Po[:, :],
                    S["rbs"][:, par2 * 512:(par2 + 1) * 512]), "dve")
            _add_ve(f_tl3)

            def f_tl4(eng, j=j, qb=qb, jqi=jqi, par2=par2):
                W(p, eng, "dve", tl[(jqi, "ev")])
                dst = p.H["attnb" if fused else "attn_out"]
                tl[(jqi, "st")] = p.inc(eng.dma_start(
                    out=dst[j * 128:(j + 1) * 128, qb * 512:(qb + 1) * 512],
                    in_=S["aow"][:, par2 * 512:(par2 + 1) * 512]), "sp", 16)
            _add_sp(f_tl4)

    run_twopass(p, _thunks)
    core_ctx.close()
    return lambda: tl[(63, "st")]



def decl_outproj1_inputs(p):
    p.dram_in("attn_in", [D, T], BF16)
    p.dram_in("wo1_blk", [DC, 128, D], F32)


def alloc_outproj1(p, S, ctx):
    S["asl"] = p.sb("asl", [128, DC * T], BF16, ctx)
    S["wo"] = p.sb("wo", [128, 2 * D], BF16, ctx)
    S["eps"] = p.sb("eps", [128, 2 * 512], F32, ctx)
    S["xep"] = p.sb("xep", [128, 2 * 512], F32, ctx)
    S["xo"] = p.sb("xo", [128, 2 * 512], F32, ctx)


def emit_outproj1(p, S, x_src, x_dst, gate=(), fused=False):
    P = S["p"]
    o_ctx = p.scope()
    alloc_outproj1(p, S, o_ctx)
    ald, owld, omm, ep_s, ep_v, ep_x, ep_st = {}, {}, {}, {}, {}, {}, {}

    def f_asl(eng):
        for sgn, sgv in gate:
            W(p, eng, sgn, sgv() if callable(sgv) else sgv)
        if not fused:
            ald["a"] = p.inc(eng.dma_start(
                out=S["asl"][:, :].rearrange("p (c t) -> p c t", c=DC),
                in_=p.H["attn_in"].rearrange("(c p) t -> p c t", p=128)), "gp", 16)
        else:
            aall = p.H["attn_all"]
            with eng.register("rc") as rc:
                eng.reg_load(rc, S["rks"][0:1, 0:1])
                eng.reg_alu(rc, rc, T, ALU.mult)
                with eng.register("rd") as rd:
                    for g4 in range(4):
                        eng.reg_add(rd, rc, g4 * 8 * 128 * (8 * T))
                        ald["a"] = p.inc(eng.dma_start(
                            out=S["asl"][:, g4 * 8 * T:(g4 + 1) * 8 * T].rearrange(
                                "p (cc t) -> p cc t", cc=8),
                            in_=bass.AP(aall.tensor, rd, [[8 * T, 128], [128 * 8 * T, 8], [1, T]])), "gp", 16)
    p.gp(f_asl)

    for o in range(DC):
        buf = o % 2

        def f_gpo(eng, o=o, buf=buf):
            if o >= 2:
                W(p, eng, "pe", omm[(o - 2, 1)])
            owld[o] = p.inc(eng.dma_start(out=S["wo"][:, buf * D:(buf + 1) * D],
                                          in_=p.H["wo1_blk"][o, :, :]), "gp", 16)
        p.gp(f_gpo)

        for tb in range(2):
            idx = o * 2 + tb
            bank = idx % 4

            def f_spx(eng, idx=idx, o=o, tb=tb):
                if idx >= 2:
                    W(p, eng, "dve", ep_v[idx - 2])
                ep_x[idx] = p.inc(eng.dma_start(
                    out=S["xep"][:, (idx % 2) * 512:((idx % 2) + 1) * 512],
                    in_=x_src[o * 128:(o + 1) * 128, tb * 512:(tb + 1) * 512]), "sp", 16)
            p.sp(f_spx)

            def f_peo(eng, o=o, tb=tb, idx=idx, bank=bank, buf=buf):
                W(p, eng, "gp", owld[o])
                if idx == 0:
                    W(p, eng, "gp", ald["a"])
                if idx >= 4:
                    W(p, eng, "act", ep_s[idx - 4])
                i = None
                for cc in range(DC):
                    i = eng.matmul(P[bank][:, :], S["wo"][:, buf * D + cc * 128:buf * D + (cc + 1) * 128],
                                   S["asl"][:, cc * T + tb * 512:cc * T + (tb + 1) * 512],
                                   start=(cc == 0), stop=(cc == DC - 1))
                omm[(o, tb)] = p.inc(i, "pe")
            p.pe(f_peo)

            def f_eps(eng, idx=idx, bank=bank):
                W(p, eng, "pe", omm[(idx // 2, idx % 2)])
                if idx >= 2:
                    W(p, eng, "dve", ep_v[idx - 2])
                ep_s[idx] = p.inc(eng.activation(S["eps"][:, (idx % 2) * 512:((idx % 2) + 1) * 512],
                                                 P[bank][:, :], AF.Copy), "act")
            p.sc(f_eps)

            def f_epv(eng, idx=idx):
                W(p, eng, "act", ep_s[idx])
                W(p, eng, "sp", ep_x[idx])
                if idx >= 2:
                    W(p, eng, "sp", ep_st[idx - 2])
                ep_v[idx] = p.inc(eng.tensor_add(
                    S["xo"][:, (idx % 2) * 512:((idx % 2) + 1) * 512],
                    S["eps"][:, (idx % 2) * 512:((idx % 2) + 1) * 512],
                    S["xep"][:, (idx % 2) * 512:((idx % 2) + 1) * 512]), "dve")
            p.ve(f_epv)

            def f_epst(eng, idx=idx, o=o, tb=tb):
                W(p, eng, "dve", ep_v[idx])
                ep_st[idx] = p.inc(eng.dma_start(
                    out=x_dst[o * 128:(o + 1) * 128, tb * 512:(tb + 1) * 512],
                    in_=S["xo"][:, (idx % 2) * 512:((idx % 2) + 1) * 512]), "sp", 16)
            p.sp(f_epst)

    o_ctx.close()
    return lambda: ep_st[DC * 2 - 1]





def build_qkv1_prog():
    p = Prog().open()
    x_in = p.dram_in("x_in", [D, T], F32)
    p.dram_in("wq1_blk", [DC, 128, D], F32)
    p.dram_in("wk1_blk", [8, 128, D], F32)
    p.dram_in("wv1_blk", [8, 128, D], F32)
    p.dram_in("cs_in", [128, T], F32)
    p.dram_out("qkvb", [6144, T], BF16)
    S = alloc_common(p)
    emit_qkv1(p, S, x_in)
    final_wait(p)
    return p


def build_attn1core_prog():
    p = Prog().open()
    decl_attn1core_inputs(p)
    S = alloc_common(p)
    emit_attn1core(p, S)
    final_wait(p)
    return p


def build_outproj1_prog():
    p = Prog().open()
    x_in = p.dram_in("x_in", [D, T], F32)
    x_out = p.dram_out("x_out", [D, T], F32)
    decl_outproj1_inputs(p)
    S = alloc_common(p)
    emit_outproj1(p, S, x_in, x_out)
    final_wait(p)
    return p


def host_cos_sin(core):
    half = 64
    inv = 1.0 / (10000.0 ** (np.arange(half, dtype=np.float32) / half))
    pos = np.arange(core * T, (core + 1) * T, dtype=np.float32)
    ang = pos[:, None] * inv[None, :]
    return np.ascontiguousarray(
        np.concatenate([np.cos(ang).T, np.sin(ang).T], axis=0).astype(np.float32))


def host_masks():
    m = np.zeros((128, 4, 512), dtype=BF)
    for mm_ in range(4):
        for n in range(4):
            if n > mm_:
                m[:, mm_, n * 128:(n + 1) * 128] = 1
            elif n == mm_:
                tri = (np.arange(128)[:, None] <= np.arange(128)[None, :])
                m[:, mm_, n * 128:(n + 1) * 128] = tri.astype(BF)
    return np.ascontiguousarray(m.reshape(128, 4 * 512))


def prep_attn1_weights(inp):
    return {
        "wq1_blk": wblocks_io(np.asarray(inp["L1_attn_Wq"], np.float32)),
        "wk1_blk": wblocks_io(np.asarray(inp["L1_attn_Wk"], np.float32)),
        "wv1_blk": wblocks_io(np.asarray(inp["L1_attn_Wv"], np.float32)),
        "wo1_blk": wblocks_io(np.asarray(inp["L1_attn_Wo"], np.float32)),
    }


def run_attn1(x_full, inp, trace=False):
    consts = host_consts()
    wm = prep_attn1_weights(inp)
    pr = build_qkv1_prog()
    in_maps = []
    for c in range(NCORES):
        shard = np.ascontiguousarray(np.asarray(x_full, np.float32)[c * T:(c + 1) * T, :].T)
        m = {"x_in": shard, "wq1_blk": wm["wq1_blk"], "wk1_blk": wm["wk1_blk"],
             "wv1_blk": wm["wv1_blk"], "cs_in": host_cos_sin(c)}
        m.update(consts)
        in_maps.append(m)
    res = run_bass_kernel_spmd(pr.nc, in_maps, core_ids=list(range(NCORES)))
    pr.close()
    qkv_all = np.concatenate([res.results[c]["qkvb"] for c in range(NCORES)], axis=0)

    masks = host_masks()
    pr = build_attn1core_prog()
    in_maps = []
    for c in range(NCORES):
        q_in = np.empty((512, 8 * T), dtype=BF)
        k_in = np.empty((128, 8 * T), dtype=BF)
        v_in = np.empty((128, 8 * T), dtype=BF)
        for r in range(NCORES):
            blk = qkv_all[6144 * r:6144 * (r + 1)]
            for j in range(4):
                q_in[j * 128:(j + 1) * 128, r * T:(r + 1) * T] = blk[(4 * c + j) * 128:(4 * c + j + 1) * 128]
            k_in[:, r * T:(r + 1) * T] = blk[4096 + c * 128:4096 + (c + 1) * 128]
            v_in[:, r * T:(r + 1) * T] = blk[5120 + c * 128:5120 + (c + 1) * 128]
        m = {"q_in": q_in, "k_in": k_in, "v_in": v_in, "mask_d": masks}
        m.update(consts)
        in_maps.append(m)
    res = run_bass_kernel_spmd(pr.nc, in_maps, core_ids=list(range(NCORES)))
    pr.close()
    attn_all = np.concatenate([res.results[c]["attn_out"] for c in range(NCORES)], axis=0)

    pr = build_outproj1_prog()
    in_maps = []
    for c in range(NCORES):
        shard = np.ascontiguousarray(np.asarray(x_full, np.float32)[c * T:(c + 1) * T, :].T)
        m = {"x_in": shard, "attn_in": np.ascontiguousarray(attn_all[:, c * T:(c + 1) * T]),
             "wo1_blk": wm["wo1_blk"]}
        m.update(consts)
        in_maps.append(m)
    res = run_bass_kernel_spmd(pr.nc, in_maps, core_ids=list(range(NCORES)), trace=trace)
    pr.close()
    out = np.empty((NCORES * T, D), np.float32)
    for c in range(NCORES):
        out[c * T:(c + 1) * T, :] = res.results[c]["x_out"].T
    return out, res





def build_fused_prog():
    p = Prog().open()
    x_in = p.dram_in("x_in", [D, T], F32)
    x_out = p.dram_out("x_out", [D, T], F32)
    p.dram_in("myrank", [1, 1], I32)
    decl_ffn0_inputs(p)
    decl_attn0_inputs(p)
    decl_ffn1_inputs(p)
    decl_qkv1_inputs(p)
    p.dram_in("mask_d", [128, 4 * 512], BF16)
    p.dram_in("wo1_blk", [DC, 128, D], F32)
    xres = p.dram_int("xres", [D, T], F32)
    p.dram_int("qkv_all", [NCORES * 6144, T], BF16)
    p.H["attnb"] = p.dram_int("attnb", [512, 8 * T], BF16)
    p.dram_int("attn_all", [4096, 8 * T], BF16)
    p.dram_int("q_mine", [512, 8 * T], BF16)
    S = alloc_common(p)

    emit_ffn0(p, S, x_in, xres)
    emit_barrier(p)
    emit_attn0(p, S, xres, xres)
    emit_barrier(p)
    emit_ffn0(p, S, xres, xres)
    emit_barrier(p)
    emit_ffn1(p, S, xres, xres, dequant=True)
    emit_barrier(p)
    qkv_done = emit_qkv1(p, S, xres)
    ag = {}

    def f_ag1(eng):
        W(p, eng, "sp", qkv_done())
        ag["1"] = p.inc(eng.collective_compute(
            "AllGather", ALU.bypass, replica_groups=[list(range(NCORES))],
            ins=[p.H["qkvb"].opt()], outs=[p.H["qkv_all"].opt()]), "gp", 1)
    p.gp(f_ag1)
    emit_barrier(p)
    attn_done = emit_attn1core(p, S, gate=(("gp", lambda: ag["1"]),), fused=True)

    def f_ag2(eng):
        W(p, eng, "sp", attn_done())
        ag["2"] = p.inc(eng.collective_compute(
            "AllGather", ALU.bypass, replica_groups=[list(range(NCORES))],
            ins=[p.H["attnb"].opt()], outs=[p.H["attn_all"].opt()]), "gp", 1)
    p.gp(f_ag2)
    emit_barrier(p)
    emit_outproj1(p, S, xres, xres, gate=(("gp", lambda: ag["2"]),), fused=True)
    emit_barrier(p)
    emit_ffn1(p, S, xres, x_out, dequant=False)
    final_wait(p)
    return p


_FUSED_CACHE = {}


def kernel(**inputs):
    inp = {k: np.asarray(v) for k, v in inputs.items()}
    x = np.asarray(inp["L0_attn_hidden_states"], np.float32).reshape(8192, 4096)
    if "prog" not in _FUSED_CACHE:
        _FUSED_CACHE["prog"] = build_fused_prog()
    pr = _FUSED_CACHE["prog"]
    wm = {}
    wm.update(prep_ffn0_inputs(inp))
    wm.update(prep_attn0_inputs(inp))
    wm.update(prep_ffn1_inputs(inp))
    wm.update(prep_attn1_weights(inp))
    wm["mask_d"] = host_masks()
    wm.update(host_consts())
    in_maps = []
    for c in range(NCORES):
        m = dict(wm)
        m["x_in"] = np.ascontiguousarray(x[c * T:(c + 1) * T, :].T)
        m["myrank"] = np.array([[c]], np.int32)
        m["cs_in"] = host_cos_sin(c)
        in_maps.append(m)
    res = run_bass_kernel_spmd(pr.nc, in_maps, core_ids=list(range(NCORES)))
    out = np.empty((NCORES * T, D), np.float32)
    for c in range(NCORES):
        out[c * T:(c + 1) * T, :] = res.results[c]["x_out"].T
    return out.reshape(1, 8192, 4096)



# version 11
# speedup vs baseline: 1.4268x; 1.0122x over previous
import numpy as np
import ml_dtypes
from contextlib import ExitStack
import concourse.bass as bass
import concourse.mybir as mybir
from concourse.bass_utils import run_bass_kernel_spmd

F32 = mybir.dt.float32
BF16 = mybir.dt.bfloat16
F16 = mybir.dt.float16
I32 = mybir.dt.int32
AF = mybir.ActivationFunctionType
ALU = mybir.AluOpType
AX = mybir.AxisListType.X

NCORES = 8
T = 1024
D = 4096
DC = D // 128
EI = 7168
EIC = EI // 128
NEXP = 16
INTER = 448
BF = ml_dtypes.bfloat16
BIG = 1.0e30
ATT_SHIFT = 15.0



def wblocks_oi(Wm):
    O, I = Wm.shape
    W4 = np.asarray(Wm, np.float32).reshape(O // 128, 128, I // 128, 128)
    return np.ascontiguousarray(W4.transpose(0, 3, 2, 1)).reshape(O // 128, 128, I)


def wblocks_io(Wm):
    I, O = Wm.shape
    W4 = np.asarray(Wm, np.float32).reshape(I // 128, 128, O // 128, 128)
    return np.ascontiguousarray(W4.transpose(2, 1, 0, 3)).reshape(O // 128, 128, I)


def host_consts():
    return {
        "onesb_d": np.ones((128, 1), dtype=BF),
        "onesf_d": np.ones((1, 128), dtype=np.float32),
        "identf_d": np.eye(128, dtype=np.float32),
        "ones128_d": np.ones((128, 128), dtype=np.float32),
        "fconst_d": np.broadcast_to(
            np.array([1e-5, -15.0, 1e-6, 1.0], np.float32), (128, 4)).copy(),
    }



class Prog:
    def __init__(self):
        self.nc = bass.Bass("TRN2", target_bir_lowering=False, debug=False)
        self.ctx = ExitStack()
        self.c = {}
        self.sems = {}
        self.block = None
        self.H = {}

    def open(self):
        for s in ("gp", "sp", "pe", "act", "dve"):
            self.sems[s] = self.ctx.enter_context(self.nc.semaphore(s))
            self.c[s] = 0
        self.block = self.ctx.enter_context(self.nc.Block())
        return self

    def close(self):
        self.ctx.close()

    def dram_in(self, name, shape, dtype):
        self.H[name] = self.nc.dram_tensor(name, shape, dtype, kind="ExternalInput").ap()
        return self.H[name]

    def dram_out(self, name, shape, dtype):
        self.H[name] = self.nc.dram_tensor(name, shape, dtype, kind="ExternalOutput").ap()
        return self.H[name]

    def dram_int(self, name, shape, dtype):
        self.H[name] = self.nc.dram_tensor(name, shape, dtype, kind="Internal").ap()
        return self.H[name]

    def sb(self, name, shape, dtype, ctx=None):
        self._n = getattr(self, "_n", 0) + 1
        return (ctx or self.ctx).enter_context(
            self.nc.sbuf_tensor(f"{name}_{self._n}", shape, dtype))

    def scope(self):
        return ExitStack()

    def ps(self, name, shape, dtype=F32):
        return self.ctx.enter_context(self.nc.psum_tensor(name, shape, dtype))

    def inc(self, inst, name, n=1):
        inst.then_inc(self.sems[name], n)
        self.c[name] += n
        return self.c[name]

    _collect = None

    def begin(self):
        self._collect = []

    def flush(self, grp_size=500):
        if self._collect is None:
            return
        thunks = self._collect
        self._collect = None
        run_twopass(self, thunks, grp_size)
        self._collect = []

    def _sec(self, name, emitfn, f):
        if self._collect is not None:
            self._collect.append((name, f))
        else:
            emitfn(f)

    def gp(self, f):
        self._sec("gp", self.block.gpsimd, f)

    def sp(self, f):
        self._sec("sp", self.block.sync, f)

    def pe(self, f):
        self._sec("pe", self.block.tensor, f)

    def sc(self, f):
        self._sec("sc", self.block.scalar, f)

    def ve(self, f):
        self._sec("ve", self.block.vector, f)


def W(p, eng, name, val):
    eng.wait_ge(p.sems[name], val)


def vchain(p, eng, inst):
    v = p.inc(inst, "dve")
    eng.wait_ge(p.sems["dve"], v)
    return v



def alloc_common(p):
    S = {}
    p.dram_in("onesb_d", [128, 1], BF16)
    p.dram_in("onesf_d", [1, 128], F32)
    p.dram_in("identf_d", [128, 128], F32)
    p.dram_in("fconst_d", [128, 4], F32)
    p.dram_in("ones128_d", [128, 128], F32)
    S["p"] = [p.ps(f"pb{i}", [128, 512], F32) for i in range(8)]
    S["xln"] = p.sb("xln", [128, DC * T], BF16)
    S["onesb"] = p.sb("onesb", [128, 1], BF16)
    S["onesf"] = p.sb("onesf", [1, 128], F32)
    S["identf"] = p.sb("identf", [128, 128], F32)
    S["sm1"] = p.sb("sm1", [128, 192], F32)
    S["fc"] = p.sb("fc", [128, 4], F32)
    S["ones128"] = p.sb("ones128", [128, 128], F32)
    S["brow"] = p.sb("brow", [128, 512], F32)
    S["rks"] = p.sb("rks", [1, 1], I32)

    def f(eng):
        p.inc(eng.dma_start(out=S["onesb"][:, :], in_=p.H["onesb_d"]), "sp", 16)
        p.inc(eng.dma_start(out=S["onesf"][:, :], in_=p.H["onesf_d"]), "sp", 16)
        p.inc(eng.dma_start(out=S["identf"][:, :], in_=p.H["identf_d"]), "sp", 16)
        p.inc(eng.dma_start(out=S["fc"][:, :], in_=p.H["fconst_d"]), "sp", 16)
        S["_c_ld"] = p.inc(eng.dma_start(out=S["ones128"][:, :], in_=p.H["ones128_d"]), "sp", 16)
        if "myrank" in p.H:
            S["_c_ld"] = p.inc(eng.dma_start(out=S["rks"][:, :], in_=p.H["myrank"]), "sp", 16)
    p.sp(f)

    def fz(eng):
        W(p, eng, "sp", S["_c_ld"])
        S["_броу"] = p.inc(eng.memset(S["brow"][:, :], 0.0), "dve")
    p.ve(fz)
    return S


def alloc_ln(p, S, ctx, gate=False):
    S["xst"] = p.sb("xst", [128, 2 * T], F32, ctx)
    S["row"] = p.sb("row", [1, 4 * T], F32, ctx)
    S["bca"] = p.sb("bca", [128, T], F32, ctx)
    S["bcb"] = p.sb("bcb", [128, T], F32, ctx)
    S["xbf"] = p.sb("xbf", [128, 2 * T], BF16, ctx)
    S["sq"] = p.sb("sq", [128, 2 * T], BF16, ctx)
    S["ts1"] = p.sb("ts1", [128, 2 * T], F32, ctx)
    if gate:
        S["xlo"] = p.sb("xlo", [128, 2 * T], BF16, ctx)
        S["gwf"] = p.sb("gwf", [128, DC * NEXP], F32, ctx)
        S["gwhi"] = p.sb("gwhi", [128, DC * NEXP], BF16, ctx)
        S["gwlo"] = p.sb("gwlo", [128, DC * NEXP], BF16, ctx)
        S["lg"] = p.sb("lg", [16, T], F32, ctx)
        S["lt"] = p.sb("lt", [128, 8 * 16], F32, ctx)
        S["wtm"] = p.sb("wtm", [128, 8 * 16], F32, ctx)



def emit_ln(p, S, x_src, perm_stride=False, gate=False, start_gate=()):
    P = S["p"]
    xst, xbf, sq, xln = S["xst"], S["xbf"], S["sq"], S["xln"]
    row, bca, bcb, ts1 = S["row"], S["bca"], S["bcb"], S["ts1"]
    ms = {}
    pe_done, sc_done, dma1, dma2, n_done, glog = {}, {}, {}, {}, {}, {}

    def xrow_ap(c):
        if not perm_stride:
            return x_src[c * 128:(c + 1) * 128, :]
        j, r = c // 4, c % 4
        off = (8 * 128 * r + j) * T
        return bass.AP(x_src.tensor, off, [[8 * T, 128], [1, T]])

    for c in range(DC):
        buf = c % 2

        def f_sp(eng, c=c):
            if c == 0:
                for sgn, sgv in start_gate:
                    W(p, eng, sgn, sgv() if callable(sgv) else sgv)
            if c >= 2:
                W(p, eng, "act", sc_done[c - 2])
            dma1[c] = p.inc(eng.dma_start(
                out=xst[:, (c % 2) * T:((c % 2) + 1) * T], in_=xrow_ap(c)), "sp", 16)
        p.sp(f_sp)

        def f_sc(eng, c=c, buf=buf):
            W(p, eng, "sp", dma1[c])
            if c >= 2:
                W(p, eng, "pe", pe_done[c - 2])
            src = xst[:, (c % 2) * T:((c % 2) + 1) * T]
            eng.activation(xbf[:, buf * T:(buf + 1) * T], src, AF.Copy)
            sc_done[c] = p.inc(eng.activation(sq[:, buf * T:(buf + 1) * T], src, AF.Square), "act")
        p.sc(f_sc)

        def f_pe(eng, c=c, buf=buf):
            W(p, eng, "act", sc_done[c])
            st, spp = (c == 0), (c == DC - 1)
            ob = S["onesb"][:, :]
            eng.matmul(P[0][0:1, :], ob, xbf[:, buf * T:buf * T + 512], start=st, stop=spp)
            eng.matmul(P[1][0:1, :], ob, xbf[:, buf * T + 512:(buf + 1) * T], start=st, stop=spp)
            eng.matmul(P[2][0:1, :], ob, sq[:, buf * T:buf * T + 512], start=st, stop=spp)
            i = eng.matmul(P[3][0:1, :], ob, sq[:, buf * T + 512:(buf + 1) * T], start=st, stop=spp)
            pe_done[c] = p.inc(i, "pe")
        p.pe(f_pe)

    def f_v1(eng):
        W(p, eng, "pe", pe_done[DC - 1])
        eng.tensor_scalar_mul(row[:, 0:512], P[0][0:1, :], 1.0 / D)
        eng.tensor_scalar_mul(row[:, 512:T], P[1][0:1, :], 1.0 / D)
        eng.tensor_scalar_mul(row[:, T:T + 512], P[2][0:1, :], 1.0 / D)
        vchain(p, eng, eng.tensor_scalar_mul(row[:, T + 512:2 * T], P[3][0:1, :], 1.0 / D))
        vchain(p, eng, eng.tensor_mul(row[:, 2 * T:3 * T], row[:, 0:T], row[:, 0:T]))
        ms["var"] = p.inc(eng.tensor_sub(row[:, T:2 * T], row[:, T:2 * T], row[:, 2 * T:3 * T]), "dve")
    p.ve(f_v1)

    def f_s1(eng):
        W(p, eng, "dve", ms["var"])
        ms["std"] = p.inc(eng.activation(row[:, 2 * T:3 * T], row[:, T:2 * T], AF.Sqrt,
                                         bias=S["fc"][0:1, 0:1]), "act")
    p.sc(f_s1)

    def f_v2(eng):
        W(p, eng, "act", ms["std"])
        vchain(p, eng, eng.reciprocal(row[:, 3 * T:4 * T], row[:, 2 * T:3 * T]))
        ms["mr"] = p.inc(eng.tensor_mul(row[:, 2 * T:3 * T], row[:, 0:T], row[:, 3 * T:4 * T]), "dve")
    p.ve(f_v2)

    srcs = [3 * T, 3 * T + 512, 2 * T, 2 * T + 512]
    dsts = [(bca, 0), (bca, 512), (bcb, 0), (bcb, 512)]
    for bi in range(4):
        def f_brow(eng, bi=bi):
            if bi == 0:
                W(p, eng, "dve", ms["mr"])
            else:
                W(p, eng, "pe", ms[("bcmm", bi - 1)])
            ms[("brow", bi)] = vchain(p, eng, eng.tensor_copy(
                S["brow"][0:1, :], row[0:1, srcs[bi]:srcs[bi] + 512]))
        p.ve(f_brow)

        def f_bc(eng, bi=bi):
            W(p, eng, "dve", ms[("brow", bi)])
            ms[("bcmm", bi)] = p.inc(eng.matmul(P[bi][:, :], S["ones128"][:, :],
                                                S["brow"][:, :], start=True, stop=True), "pe")
        p.pe(f_bc)

        def f_bcv(eng, bi=bi):
            W(p, eng, "pe", ms[("bcmm", bi)])
            dst, off = dsts[bi]
            v = p.inc(eng.tensor_copy(dst[:, off:off + 512], P[bi][:, :]), "dve")
            if bi == 3:
                ms["bcev"] = v
        p.ve(f_bcv)

    for c in range(DC):
        def f_sp2(eng, c=c):
            if c < 2:
                W(p, eng, "act", sc_done[DC - 2 + c])
            else:
                W(p, eng, "dve", n_done[c - 2])
            dma2[c] = p.inc(eng.dma_start(
                out=xst[:, (c % 2) * T:((c % 2) + 1) * T], in_=xrow_ap(c)), "sp", 16)
        p.sp(f_sp2)

        if not gate:
            def f_vn(eng, c=c):
                W(p, eng, "sp", dma2[c])
                if c == 0:
                    W(p, eng, "dve", ms["bcev"])
                src = xst[:, (c % 2) * T:((c % 2) + 1) * T]
                vchain(p, eng, eng.tensor_mul(ts1[:, 0:T], src, bca[:, 0:T]))
                n_done[c] = p.inc(eng.tensor_sub(xln[:, c * T:(c + 1) * T], ts1[:, 0:T], bcb[:, 0:T]), "dve")
            p.ve(f_vn)
        else:
            xlo = S["xlo"]

            def f_vng(eng, c=c):
                W(p, eng, "sp", dma2[c])
                if c == 0:
                    W(p, eng, "dve", ms["bcev"])
                if c >= 2:
                    W(p, eng, "pe", glog[c - 2])
                buf = c % 2
                src = xst[:, (c % 2) * T:((c % 2) + 1) * T]
                vchain(p, eng, eng.tensor_mul(ts1[:, 0:T], src, bca[:, 0:T]))
                vchain(p, eng, eng.tensor_sub(ts1[:, T:2 * T], ts1[:, 0:T], bcb[:, 0:T]))
                vchain(p, eng, eng.tensor_copy(xln[:, c * T:(c + 1) * T], ts1[:, T:2 * T]))
                n_done[c] = p.inc(eng.tensor_sub(xlo[:, buf * T:(buf + 1) * T],
                                                 ts1[:, T:2 * T], xln[:, c * T:(c + 1) * T]), "dve")
            p.ve(f_vng)

            def f_peg(eng, c=c):
                W(p, eng, "dve", n_done[c])
                buf = c % 2
                st, spp = (c == 0), (c == DC - 1)
                gh, gl = S["gwhi"], S["gwlo"]
                i = None
                for tb in range(2):
                    xh = xln[:, c * T + tb * 512:c * T + (tb + 1) * 512]
                    xl = xlo[:, buf * T + tb * 512:buf * T + (tb + 1) * 512]
                    eng.matmul(P[4 + tb][0:16, :], gh[:, c * 16:(c + 1) * 16], xh, start=st, stop=False)
                    eng.matmul(P[4 + tb][0:16, :], gl[:, c * 16:(c + 1) * 16], xh, start=False, stop=False)
                    i = eng.matmul(P[4 + tb][0:16, :], gh[:, c * 16:(c + 1) * 16], xl, start=False, stop=spp)
                glog[c] = p.inc(i, "pe")
            p.pe(f_peg)

    ms["xln"] = lambda: n_done[DC - 1]
    ms["glog"] = (lambda: glog[DC - 1]) if gate else None
    return ms



def emit_gating(p, S, ms):
    P = S["p"]
    lg, lt, wtm, wfm, sm1 = S["lg"], S["lt"], S["wtm"], S["wfm"], S["sm1"]
    g = {}

    def f_ev(eng):
        W(p, eng, "pe", ms["glog"]())
        eng.memset(wfm[:, :], 0.0)
        eng.tensor_copy(lg[:, 0:512], P[4][0:16, :])
        g["lg"] = p.inc(eng.tensor_copy(lg[:, 512:T], P[5][0:16, :]), "dve")
    p.ve(f_ev)

    tpmm, tpev = {}, {}
    for tc in range(8):
        def f_tp(eng, tc=tc):
            if tc == 0:
                W(p, eng, "dve", g["lg"])
            bank = P[6] if tc % 2 == 0 else P[7]
            if tc >= 2:
                W(p, eng, "dve", tpev[tc - 2])
            tpmm[tc] = p.inc(eng.transpose(bank[:, 0:16], lg[0:16, tc * 128:(tc + 1) * 128],
                                           S["identf"][0:16, 0:16]), "pe")
        p.pe(f_tp)

        def f_tpv(eng, tc=tc):
            W(p, eng, "pe", tpmm[tc])
            bank = P[6] if tc % 2 == 0 else P[7]
            tpev[tc] = p.inc(eng.tensor_copy(lt[:, tc * 16:(tc + 1) * 16], bank[:, 0:16]), "dve")
        p.ve(f_tpv)

    sexp = {}
    for tc in range(8):
        def f_tk(eng, tc=tc):
            W(p, eng, "dve", tpev[tc])
            L = lt[:, tc * 16:(tc + 1) * 16]
            m1, nm1, m2 = sm1[:, 0:1], sm1[:, 1:2], sm1[:, 2:3]
            eq, lm, keep = sm1[:, 16:32], sm1[:, 32:48], sm1[:, 48:64]
            eng.tensor_reduce(m1, L, AX, ALU.max)
            vchain(p, eng, eng.tensor_reduce(nm1, L, AX, ALU.max, negate=True))
            vchain(p, eng, eng.tensor_single_scalar(eq, L, m1, ALU.is_equal))
            vchain(p, eng, eng.scalar_tensor_tensor(lm, eq, -BIG, L, ALU.mult, ALU.add))
            vchain(p, eng, eng.tensor_reduce(m2, lm, AX, ALU.max))
            g[("keep", tc)] = p.inc(eng.tensor_single_scalar(keep, L, m2, ALU.is_ge), "dve")
        p.ve(f_tk)

        def f_te(eng, tc=tc):
            W(p, eng, "dve", g[("keep", tc)])
            sexp[tc] = p.inc(eng.activation(sm1[:, 80:96], lt[:, tc * 16:(tc + 1) * 16],
                                            AF.Exp, bias=sm1[:, 1:2]), "act")
        p.sc(f_te)

        def f_tk2(eng, tc=tc):
            W(p, eng, "act", sexp[tc])
            keep, wk = sm1[:, 48:64], sm1[:, 64:80]
            vchain(p, eng, eng.tensor_mul(wk, sm1[:, 80:96], keep))
            vchain(p, eng, eng.tensor_reduce(sm1[:, 3:4], wk, AX, ALU.add))
            vchain(p, eng, eng.reciprocal(sm1[:, 4:5], sm1[:, 3:4]))
            g[("w", tc)] = p.inc(eng.tensor_scalar_mul(wtm[:, tc * 16:(tc + 1) * 16], wk, sm1[:, 4:5]), "dve")
        p.ve(f_tk2)

    for tc in range(8):
        def f_wt(eng, tc=tc):
            W(p, eng, "dve", g[("w", tc)])
            bank = P[6] if tc % 2 == 0 else P[7]
            if tc >= 2:
                W(p, eng, "dve", g[("wf", tc - 2)])
            g[("wt", tc)] = p.inc(eng.transpose(bank[0:16, 0:128], wtm[:, tc * 16:(tc + 1) * 16],
                                                S["identf"][:, :]), "pe")
        p.pe(f_wt)

        def f_wv(eng, tc=tc):
            W(p, eng, "pe", g[("wt", tc)])
            bank = P[6] if tc % 2 == 0 else P[7]
            g[("wf", tc)] = p.inc(eng.tensor_copy(wfm[0:16, tc * 128:(tc + 1) * 128],
                                                  bank[0:16, 0:128]), "dve")
        p.ve(f_wv)

    return {"wfm": lambda: g[("wf", 7)], "sexp": lambda: sexp[7],
            "wt": lambda: g[("wt", 7)]}



def alloc_ffn0_outer(p, S, ctx):
    S["hsl"] = p.sb("hsl", [128, EIC * 512], BF16, ctx)
    S["wfm"] = p.sb("wfm", [128, T], BF16, ctx)


def alloc_ffn0_body(p, S, ctx):
    S["h1st"] = p.sb("h1st", [128, 2 * 512], BF16, ctx)
    S["wexp"] = p.sb("wexp", [128, 2 * T], BF16, ctx)
    S["expTs"] = p.sb("expTs", [128, EI], BF16, ctx)
    S["sg"] = p.sb("sg", [128, 2 * T], F32, ctx)
    S["hv"] = p.sb("hv", [128, T], F32, ctx)


def alloc_ffn0_ug(p, S, ctx):
    S["wu"] = p.sb("wu", [128, 2 * D], BF16, ctx)
    S["wg"] = p.sb("wg", [128, 2 * D], BF16, ctx)


def alloc_ffn0_down(p, S, ctx):
    S["wd"] = p.sb("wd", [128, 2 * EI], BF16, ctx)
    S["eps"] = p.sb("eps", [128, 2 * 512], F32, ctx)
    S["xep"] = p.sb("xep", [128, 2 * 512], F32, ctx)
    S["xo"] = p.sb("xo", [128, 2 * 512], F32, ctx)


def decl_ffn0_inputs(p):
    p.dram_in("u_blk", [EIC, 128, D], F32)
    p.dram_in("g_blk", [EIC, 128, D], F32)
    p.dram_in("d_blk", [DC, 128, EI], F32)
    p.dram_in("gwT_blk", [128, DC * NEXP], F32)
    p.dram_in("expT", [128, EI], BF16)
    p.dram_int("h1s", [EIC, 128, 512], BF16)


def emit_ffn0(p, S, x_src, x_dst, dbg=False):
    P = S["p"]
    u_blk, g_blk, d_blk = p.H["u_blk"], p.H["g_blk"], p.H["d_blk"]
    h1s = p.H["h1s"]

    outer_ctx = p.scope()
    alloc_ffn0_outer(p, S, outer_ctx)
    ln_ctx = p.scope()
    alloc_ln(p, S, ln_ctx, gate=True)

    gwld = {}

    def f_gw_ld(eng):
        gwld["ld"] = p.inc(eng.dma_start(out=S["gwf"][:, :], in_=p.H["gwT_blk"]), "sp", 16)
    p.sp(f_gw_ld)

    def f_gw_v(eng):
        W(p, eng, "sp", gwld["ld"])
        vchain(p, eng, eng.tensor_copy(S["gwhi"][:, :], S["gwf"][:, :]))
        gwld["hl"] = p.inc(eng.tensor_sub(S["gwlo"][:, :], S["gwf"][:, :], S["gwhi"][:, :]), "dve")
    p.ve(f_gw_v)

    ms = emit_ln(p, S, x_src, gate=True)
    gat = emit_gating(p, S, ms)
    wfm_done = gat["wfm"]
    if dbg:
        def f_dbg1(eng):
            W(p, eng, "dve", wfm_done())
            p.inc(eng.dma_start(out=p.H["xln_dbg"], in_=S["xln"][:, :]), "sp", 16)
            p.inc(eng.dma_start(out=p.H["wfm_dbg"], in_=S["wfm"][:, :]), "sp", 16)
        p.sp(f_dbg1)
    ln_ctx.close()
    body_ctx = p.scope()
    alloc_ffn0_body(p, S, body_ctx)
    ug_ctx = p.scope()
    alloc_ffn0_ug(p, S, ug_ctx)

    wu, wg, wexp, sg, hsl, h1st, hv = (S["wu"], S["wg"], S["wexp"], S["sg"],
                                       S["hsl"], S["h1st"], S["hv"])
    wld, ug_done, sil_done, hv_done, h1_store = {}, {}, {}, {}, {}
    wexp_ms = {}

    for e in range(EIC):
        buf = e % 2
        par = e % 2

        def f_gp(eng, e=e, buf=buf):
            if e == 0:
                W(p, eng, "dve", wfm_done())
                W(p, eng, "act", gat["sexp"]())
                W(p, eng, "pe", gat["wt"]())
                p.inc(eng.dma_start(out=S["expTs"][:, :], in_=p.H["expT"]), "gp", 16)
            if e >= 2:
                W(p, eng, "pe", ug_done[e - 2])
            p.inc(eng.dma_start(out=wu[:, buf * D:(buf + 1) * D], in_=u_blk[e, :, :]), "gp", 16)
            wld[e] = p.inc(eng.dma_start(out=wg[:, buf * D:(buf + 1) * D], in_=g_blk[e, :, :]), "gp", 16)
        p.gp(f_gp)

        def f_pe_a(eng, e=e, buf=buf, par=par):
            W(p, eng, "gp", wld[e])
            pu = [P[par * 4 + 0], P[par * 4 + 1]]
            pg = [P[par * 4 + 2], P[par * 4 + 3]]
            if e >= 2:
                W(p, eng, "act", sil_done[e - 2])
                W(p, eng, "dve", hv_done[e - 2])
            if e == 0:
                W(p, eng, "dve", wfm_done())
            i = None
            for tb in range(2):
                i = eng.matmul(pu[tb][:, :], S["expTs"][:, e * 128:(e + 1) * 128],
                               S["wfm"][:, tb * 512:(tb + 1) * 512], start=True, stop=True)
            wexp_ms[("mm", e)] = p.inc(i, "pe")
            for tb in range(2):
                for c in range(DC):
                    eng.matmul(pg[tb][:, :], wg[:, buf * D + c * 128:buf * D + (c + 1) * 128],
                               S["xln"][:, c * T + tb * 512:c * T + (tb + 1) * 512],
                               start=(c == 0), stop=(c == DC - 1))
        p.pe(f_pe_a)

        def f_vwx(eng, e=e, par=par):
            W(p, eng, "pe", wexp_ms[("mm", e)])
            pu = [P[par * 4 + 0], P[par * 4 + 1]]
            eng.tensor_copy(wexp[:, par * T:par * T + 512], pu[0][:, :])
            wexp_ms[("ev", e)] = p.inc(
                eng.tensor_copy(wexp[:, par * T + 512:(par + 1) * T], pu[1][:, :]), "dve")
        p.ve(f_vwx)

        def f_pe_b(eng, e=e, buf=buf, par=par):
            W(p, eng, "dve", wexp_ms[("ev", e)])
            pu = [P[par * 4 + 0], P[par * 4 + 1]]
            i = None
            for tb in range(2):
                for c in range(DC):
                    i = eng.matmul(pu[tb][:, :], wu[:, buf * D + c * 128:buf * D + (c + 1) * 128],
                                   S["xln"][:, c * T + tb * 512:c * T + (tb + 1) * 512],
                                   start=(c == 0), stop=(c == DC - 1))
            ug_done[e] = p.inc(i, "pe")
        p.pe(f_pe_b)

        def f_sil(eng, e=e, par=par):
            W(p, eng, "pe", ug_done[e])
            pg = [P[par * 4 + 2], P[par * 4 + 3]]
            eng.activation(sg[:, par * T:par * T + 512], pg[0][:, :], AF.Silu)
            sil_done[e] = p.inc(
                eng.activation(sg[:, par * T + 512:(par + 1) * T], pg[1][:, :], AF.Silu), "act")
        p.sc(f_sil)

        def f_hv(eng, e=e, par=par):
            W(p, eng, "act", sil_done[e])
            if e >= 2:
                W(p, eng, "sp", h1_store[e - 2])
            pu = [P[par * 4 + 0], P[par * 4 + 1]]
            eng.tensor_mul(hv[:, 0:512], sg[:, par * T:par * T + 512], pu[0][:, :])
            vchain(p, eng, eng.tensor_mul(hv[:, 512:T], sg[:, par * T + 512:(par + 1) * T], pu[1][:, :]))
            eng.tensor_mul(hsl[:, e * 512:(e + 1) * 512], hv[:, 0:512], wexp[:, par * T:par * T + 512])
            hv_done[e] = p.inc(
                eng.tensor_mul(h1st[:, par * 512:(par + 1) * 512], hv[:, 512:T],
                               wexp[:, par * T + 512:(par + 1) * T]), "dve")
        p.ve(f_hv)

        def f_h1(eng, e=e, par=par):
            W(p, eng, "dve", hv_done[e])
            h1_store[e] = p.inc(eng.dma_start(out=h1s[e, :, :],
                                              in_=h1st[:, par * 512:(par + 1) * 512]), "sp", 16)
        p.sp(f_h1)

    if dbg:
        def f_dbg2(eng):
            W(p, eng, "dve", hv_done[EIC - 1])
            p.inc(eng.dma_start(out=p.H["hsl_dbg"], in_=S["hsl"][:, :]), "sp", 16)
        p.sp(f_dbg2)
    ug_ctx.close()
    down_ctx = p.scope()
    alloc_ffn0_down(p, S, down_ctx)
    wd, xln = S["wd"], S["xln"]
    h1b = {}

    def f_h1back(eng):
        W(p, eng, "dve", hv_done[EIC - 1])
        W(p, eng, "sp", h1_store[EIC - 1])
        h1b["v"] = p.inc(eng.dma_start(
            out=xln[:, 0:EIC * 512].rearrange("p (e m) -> p e m", e=EIC),
            in_=h1s.rearrange("e p m -> p e m")), "sp", 16)
    p.sp(f_h1back)

    dwld, dn_done, ep_s, ep_v, ep_x, ep_st = {}, {}, {}, {}, {}, {}
    for o in range(DC):
        buf = o % 2

        def f_gp2(eng, o=o, buf=buf):
            if o < 2:
                W(p, eng, "pe", ug_done[EIC - 1])
            else:
                W(p, eng, "pe", dn_done[(o - 2, 1)])
            dwld[o] = p.inc(eng.dma_start(out=wd[:, buf * EI:(buf + 1) * EI], in_=d_blk[o, :, :]), "gp", 16)
        p.gp(f_gp2)

        for tb in range(2):
            idx = o * 2 + tb
            bank = idx % 4

            def f_spx(eng, idx=idx, o=o, tb=tb):
                if idx >= 2:
                    W(p, eng, "dve", ep_v[idx - 2])
                ep_x[idx] = p.inc(eng.dma_start(
                    out=S["xep"][:, (idx % 2) * 512:((idx % 2) + 1) * 512],
                    in_=x_src[o * 128:(o + 1) * 128, tb * 512:(tb + 1) * 512]), "sp", 16)
            p.sp(f_spx)

            def f_pe2(eng, o=o, tb=tb, idx=idx, bank=bank, buf=buf):
                W(p, eng, "gp", dwld[o])
                if idx == 0:
                    W(p, eng, "sp", h1b["v"])
                if idx >= 4:
                    W(p, eng, "act", ep_s[idx - 4])
                i = None
                for e in range(EIC):
                    rhs = (hsl[:, e * 512:(e + 1) * 512] if tb == 0
                           else xln[:, e * 512:(e + 1) * 512])
                    i = eng.matmul(P[bank][:, :], wd[:, buf * EI + e * 128:buf * EI + (e + 1) * 128],
                                   rhs, start=(e == 0), stop=(e == EIC - 1))
                dn_done[(o, tb)] = p.inc(i, "pe")
            p.pe(f_pe2)

            def f_eps(eng, idx=idx, bank=bank):
                W(p, eng, "pe", dn_done[(idx // 2, idx % 2)])
                if idx >= 2:
                    W(p, eng, "dve", ep_v[idx - 2])
                ep_s[idx] = p.inc(eng.activation(S["eps"][:, (idx % 2) * 512:((idx % 2) + 1) * 512],
                                                 P[bank][:, :], AF.Copy, scale=0.5), "act")
            p.sc(f_eps)

            def f_epv(eng, idx=idx):
                W(p, eng, "act", ep_s[idx])
                W(p, eng, "sp", ep_x[idx])
                if idx >= 2:
                    W(p, eng, "sp", ep_st[idx - 2])
                ep_v[idx] = p.inc(eng.tensor_add(
                    S["xo"][:, (idx % 2) * 512:((idx % 2) + 1) * 512],
                    S["eps"][:, (idx % 2) * 512:((idx % 2) + 1) * 512],
                    S["xep"][:, (idx % 2) * 512:((idx % 2) + 1) * 512]), "dve")
            p.ve(f_epv)

            def f_epst(eng, idx=idx, o=o, tb=tb):
                W(p, eng, "dve", ep_v[idx])
                ep_st[idx] = p.inc(eng.dma_start(
                    out=x_dst[o * 128:(o + 1) * 128, tb * 512:(tb + 1) * 512],
                    in_=S["xo"][:, (idx % 2) * 512:((idx % 2) + 1) * 512]), "sp", 16)
            p.sp(f_epst)

    down_ctx.close()
    body_ctx.close()
    outer_ctx.close()
    return lambda: ep_st[DC * 2 - 1]




class _FakeInst:
    def then_inc(self, sem, n=1):
        return self


class _FakeReg:
    def __enter__(self):
        return self

    def __exit__(self, *a):
        return False


class MD(dict):

    def __missing__(self, k):
        return 0


class _FakeEng:
    def wait_ge(self, *a, **k):
        return None

    def register(self, *a, **k):
        return _FakeReg()

    def __getattr__(self, name):
        if name in ("reg_load", "reg_alu", "reg_add", "reg_mov"):
            return lambda *a, **k: None
        return lambda *a, **k: _FakeInst()


def reg_off(eng, reg):
    return 0 if isinstance(eng, _FakeEng) else reg


def run_twopass(p, thunks, grp_size=500):
    c0 = dict(p.c)
    fake = _FakeEng()
    for _, fn in thunks:
        fn(fake)
    p.c.clear()
    p.c.update(c0)
    emitters = {"gp": p.block.gpsimd, "sp": p.block.sync, "pe": p.block.tensor,
                "sc": p.block.scalar, "ve": p.block.vector}
    for engname in ("gp", "sp", "pe", "sc", "ve"):
        fns = [f for e, f in thunks if e == engname]
        for i in range(0, len(fns), grp_size):
            grp = fns[i:i + grp_size]

            def sec(eng, grp=grp):
                for f in grp:
                    f(eng)
            emitters[engname](sec)


def emit_barrier(p):
    snap = {}
    if p._collect is None:
        snap.update(p.c)
    for secname in ("gp", "sp", "pe", "sc", "ve"):
        def f(eng, snap=snap):
            if isinstance(eng, _FakeEng):
                snap.update(p.c)
                return
            for nm in ("gp", "sp", "pe", "act", "dve"):
                eng.wait_ge(p.sems[nm], snap.get(nm, 0))
        getattr(p, secname)(f)


def final_wait(p):
    snap = {}
    if p._collect is None:
        snap.update(p.c)

    def f(eng):
        if isinstance(eng, _FakeEng):
            snap.update(p.c)
            return
        eng.wait_ge(p.sems["sp"], snap["sp"])
        eng.wait_ge(p.sems["gp"], snap["gp"])
    p.sp(f)

    def f2(eng):
        if isinstance(eng, _FakeEng):
            return
        eng.wait_ge(p.sems["sp"], snap["sp"])
    p.gp(f2)



def build_ffn0_prog(dbg=False):
    p = Prog().open()
    x_in = p.dram_in("x_in", [D, T], F32)
    x_out = p.dram_out("x_out", [D, T], F32)
    if dbg:
        p.dram_out("xln_dbg", [128, DC * T], BF16)
        p.dram_out("wfm_dbg", [16, T], BF16)
        p.dram_out("hsl_dbg", [128, EIC * 512], BF16)
    decl_ffn0_inputs(p)
    S = alloc_common(p)
    emit_ffn0(p, S, x_in, x_out, dbg=dbg)
    final_wait(p)
    return p


def prep_ffn0_inputs(inp):
    gw = np.asarray(inp["L0_ffn_gate_w"], np.float32)
    gwT_blk = np.ascontiguousarray(
        gw.reshape(16, DC, 128).transpose(2, 1, 0)).reshape(128, DC * 16)
    expT = np.zeros((128, EI), dtype=BF)
    for e2 in range(NEXP):
        expT[e2, e2 * INTER:(e2 + 1) * INTER] = 1
    return {
        "u_blk": wblocks_oi(inp["L0_ffn_u"]),
        "g_blk": wblocks_oi(inp["L0_ffn_g"]),
        "d_blk": wblocks_oi(inp["L0_ffn_d"]),
        "gwT_blk": gwT_blk,
        "expT": expT,
    }


def run_ffn0(x_full, inp, trace=False, dbg=False):
    pr = build_ffn0_prog(dbg=dbg)
    wmaps = prep_ffn0_inputs(inp)
    consts = host_consts()
    in_maps = []
    for c in range(NCORES):
        shard = np.ascontiguousarray(np.asarray(x_full, np.float32)[c * T:(c + 1) * T, :].T)
        m = {"x_in": shard}
        m.update(wmaps)
        m.update(consts)
        in_maps.append(m)
    res = run_bass_kernel_spmd(pr.nc, in_maps, core_ids=list(range(NCORES)), trace=trace)
    pr.close()
    out = np.empty((NCORES * T, D), np.float32)
    for c in range(NCORES):
        out[c * T:(c + 1) * T, :] = res.results[c]["x_out"].T
    return out, res





def decl_attn0_inputs(p):
    p.dram_in("wqsel_blk", [8, 128, D], F32)
    p.dram_in("bq_sel", [128, 8], F32)
    p.dram_in("m_in", [8, 128, 128], F32)
    p.dram_in("z_in", [128, 8], F32)
    p.dram_in("wo_blk", [DC, 128, D], F32)


def alloc_attn0_outer(p, S, ctx):
    S["rfm"] = p.sb("rfm", [128, 8 * T], BF16, ctx)


def alloc_attn0_qr(p, S, ctx):
    S["wq"] = p.sb("wq", [128, 2 * D], BF16, ctx)
    S["sigq"] = p.sb("sigq", [128, 8 * T], BF16, ctx)
    S["bqs"] = p.sb("bqs", [128, 8], F32, ctx)
    S["sia"] = p.sb("sia", [128, 2 * 512], F32, ctx)
    S["sib"] = p.sb("sib", [128, 2 * 512], F32, ctx)
    S["sic"] = p.sb("sic", [128, 2 * 512], F32, ctx)
    S["maug"] = p.sb("maug", [128, 8 * 132], BF16, ctx)
    S["mst"] = p.sb("mst", [128, 128], F32, ctx)
    S["zst"] = p.sb("zst", [128, 8], F32, ctx)
    S["rtm"] = p.sb("rtm", [128, 2 * 128], F32, ctx)


def alloc_attn0_o(p, S, ctx):
    S["wo"] = p.sb("wo", [128, 2 * D], BF16, ctx)
    S["eps"] = p.sb("eps", [128, 2 * 512], F32, ctx)
    S["xep"] = p.sb("xep", [128, 2 * 512], F32, ctx)
    S["xo"] = p.sb("xo", [128, 2 * 512], F32, ctx)


def emit_attn0(p, S, x_src, x_dst):
    P = S["p"]
    outer_ctx = p.scope()
    alloc_attn0_outer(p, S, outer_ctx)
    ln_ctx = p.scope()
    alloc_ln(p, S, ln_ctx, gate=False)
    ms = emit_ln(p, S, x_src, gate=False)
    xln_done = ms["xln"]
    ln_ctx.close()

    q_ctx = p.scope()
    alloc_attn0_qr(p, S, q_ctx)
    wq_blk, wo_blk = p.H["wqsel_blk"], p.H["wo_blk"]
    sigq, wq = S["sigq"], S["wq"]
    qwld, qmm, qv1, qs, qsig = {}, {}, {}, {}, {}

    for h in range(8):
        buf = h % 2

        def f_gp(eng, h=h, buf=buf):
            if h == 0:
                W(p, eng, "dve", xln_done())
                p.inc(eng.dma_start(out=S["bqs"][:, :], in_=p.H["bq_sel"]), "gp", 16)
            if h >= 2:
                W(p, eng, "pe", qmm[(h - 2, 1)])
            qwld[h] = p.inc(eng.dma_start(out=wq[:, buf * D:(buf + 1) * D],
                                          in_=wq_blk[h, :, :]), "gp", 16)
        p.gp(f_gp)

        for tb in range(2):
            idx = h * 2 + tb
            bank = idx % 4

            def f_pe(eng, h=h, tb=tb, idx=idx, bank=bank, buf=buf):
                W(p, eng, "gp", qwld[h])
                if idx >= 4:
                    W(p, eng, "dve", qv1[idx - 4])
                i = None
                for c in range(DC):
                    i = eng.matmul(P[bank][:, :], wq[:, buf * D + c * 128:buf * D + (c + 1) * 128],
                                   S["xln"][:, c * T + tb * 512:c * T + (tb + 1) * 512],
                                   start=(c == 0), stop=(c == DC - 1))
                qmm[(h, tb)] = p.inc(i, "pe")
            p.pe(f_pe)

            def f_v1(eng, h=h, tb=tb, idx=idx, bank=bank):
                W(p, eng, "pe", qmm[(h, tb)])
                if idx >= 2:
                    W(p, eng, "dve", qsig[idx - 2])
                par = idx % 2
                bq = S["bqs"][:, h:h + 1]
                eng.tensor_scalar(S["sia"][:, par * 512:(par + 1) * 512], P[bank][:, :],
                                  bq, 0.0, ALU.add, ALU.min)
                qv1[idx] = p.inc(eng.tensor_scalar(S["sib"][:, par * 512:(par + 1) * 512],
                                                   P[bank][:, :], bq, 0.0, ALU.add, ALU.max), "dve")
            p.ve(f_v1)

            def f_s(eng, idx=idx):
                W(p, eng, "dve", qv1[idx])
                par = idx % 2
                qs[idx] = p.inc(eng.activation(S["sic"][:, par * 512:(par + 1) * 512],
                                               S["sia"][:, par * 512:(par + 1) * 512], AF.Exp), "act")
            p.sc(f_s)

            def f_v2(eng, h=h, tb=tb, idx=idx):
                W(p, eng, "act", qs[idx])
                par = idx % 2
                qsig[idx] = p.inc(eng.tensor_add(
                    sigq[:, h * T + tb * 512:h * T + (tb + 1) * 512],
                    S["sic"][:, par * 512:(par + 1) * 512],
                    S["sib"][:, par * 512:(par + 1) * 512]), "dve")
            p.ve(f_v2)

    q_last = lambda: qsig[15]
    maug, rfm, rtm = S["maug"], S["rfm"], S["rtm"]
    mld, mcast = {}, {}

    def f_mld(eng):
        mld[0] = p.inc(eng.dma_start(out=S["zst"][:, :], in_=p.H["z_in"]), "gp", 16)
    p.gp(f_mld)

    for h in range(8):
        def f_mh(eng, h=h):
            if h >= 1:
                W(p, eng, "act", mcast[h - 1])
            mld[h + 1] = p.inc(eng.dma_start(out=S["mst"][:, :], in_=p.H["m_in"][h, :, :]), "gp", 16)
        p.gp(f_mh)

        def f_mc(eng, h=h):
            W(p, eng, "gp", mld[h + 1])
            if h == 0:
                W(p, eng, "gp", mld[0])
            eng.activation(maug[:, h * 132:h * 132 + 128], S["mst"][:, :], AF.Copy)
            mcast[h] = p.inc(eng.activation(maug[:, h * 132 + 128:h * 132 + 129],
                                            S["zst"][:, h:h + 1], AF.Copy), "act")
        p.sc(f_mc)

    rmm, rv, rtp, rfv = {}, {}, {}, {}
    for h in range(8):
        for tc in range(8):
            idx = h * 8 + tc
            bank = 4 + (idx % 2)

            def f_rmm(eng, h=h, tc=tc, idx=idx, bank=bank):
                if idx == 0:
                    W(p, eng, "act", mcast[7])
                    W(p, eng, "dve", q_last())
                if idx >= 2:
                    W(p, eng, "dve", rv[idx - 2])
                rmm[idx] = p.inc(eng.matmul(
                    P[bank][:, 0:129],
                    sigq[:, h * T + tc * 128:h * T + (tc + 1) * 128],
                    maug[:, h * 132:h * 132 + 129], start=True, stop=True), "pe")
            p.pe(f_rmm)

            def f_rv(eng, idx=idx, bank=bank):
                W(p, eng, "pe", rmm[idx])
                par = idx % 2
                den = S["sm1"][:, 128:129]
                vchain(p, eng, eng.tensor_scalar_add(den, P[bank][:, 128:129], 1e-6))
                vchain(p, eng, eng.reciprocal(S["sm1"][:, 129:130], den))
                rv[idx] = p.inc(eng.tensor_scalar_mul(
                    rtm[:, par * 128:(par + 1) * 128], P[bank][:, 0:128],
                    S["sm1"][:, 129:130]), "dve")
            p.ve(f_rv)

            def f_rtp(eng, idx=idx, h=h, tc=tc):
                W(p, eng, "dve", rv[idx])
                par = idx % 2
                bank2 = P[6] if idx % 2 == 0 else P[7]
                if idx >= 2:
                    W(p, eng, "dve", rfv[idx - 2])
                rtp[idx] = p.inc(eng.transpose(bank2[:, 0:128], rtm[:, par * 128:(par + 1) * 128],
                                               S["identf"][:, :]), "pe")
            p.pe(f_rtp)

            def f_rfv(eng, idx=idx, h=h, tc=tc):
                W(p, eng, "pe", rtp[idx])
                bank2 = P[6] if idx % 2 == 0 else P[7]
                rfv[idx] = p.inc(eng.tensor_copy(
                    rfm[:, h * T + tc * 128:h * T + (tc + 1) * 128], bank2[:, 0:128]), "dve")
            p.ve(f_rfv)

    r_last = lambda: rfv[63]

    q_ctx.close()
    o_ctx = p.scope()
    alloc_attn0_o(p, S, o_ctx)
    wo = S["wo"]
    owld, omm, ep_s, ep_v, ep_x, ep_st = {}, {}, {}, {}, {}, {}

    for o in range(DC):
        buf = o % 2

        def f_gpo(eng, o=o, buf=buf):
            if o == 0:
                W(p, eng, "dve", r_last())
            if o >= 2:
                W(p, eng, "pe", omm[(o - 2, 1)])
            owld[o] = p.inc(eng.dma_start(out=wo[:, buf * D:(buf + 1) * D],
                                          in_=wo_blk[o, :, :]), "gp", 16)
        p.gp(f_gpo)

        for tb in range(2):
            idx = o * 2 + tb
            bank = idx % 4

            def f_spx(eng, idx=idx, o=o, tb=tb):
                if idx == 0:
                    W(p, eng, "dve", r_last())
                if idx >= 2:
                    W(p, eng, "dve", ep_v[idx - 2])
                ep_x[idx] = p.inc(eng.dma_start(
                    out=S["xep"][:, (idx % 2) * 512:((idx % 2) + 1) * 512],
                    in_=x_src[o * 128:(o + 1) * 128, tb * 512:(tb + 1) * 512]), "sp", 16)
            p.sp(f_spx)

            def f_peo(eng, o=o, tb=tb, idx=idx, bank=bank, buf=buf):
                W(p, eng, "gp", owld[o])
                if idx < 2:
                    W(p, eng, "dve", r_last())
                if idx >= 4:
                    W(p, eng, "act", ep_s[idx - 4])
                i = None
                for cc in range(DC):
                    i = eng.matmul(P[bank][:, :], wo[:, buf * D + cc * 128:buf * D + (cc + 1) * 128],
                                   rfm[:, (cc // 4) * T + tb * 512:(cc // 4) * T + (tb + 1) * 512],
                                   start=(cc == 0), stop=(cc == DC - 1))
                omm[(o, tb)] = p.inc(i, "pe")
            p.pe(f_peo)

            def f_eps(eng, idx=idx, bank=bank):
                W(p, eng, "pe", omm[(idx // 2, idx % 2)])
                if idx >= 2:
                    W(p, eng, "dve", ep_v[idx - 2])
                ep_s[idx] = p.inc(eng.activation(S["eps"][:, (idx % 2) * 512:((idx % 2) + 1) * 512],
                                                 P[bank][:, :], AF.Copy), "act")
            p.sc(f_eps)

            def f_epv(eng, idx=idx):
                W(p, eng, "act", ep_s[idx])
                W(p, eng, "sp", ep_x[idx])
                if idx >= 2:
                    W(p, eng, "sp", ep_st[idx - 2])
                ep_v[idx] = p.inc(eng.tensor_add(
                    S["xo"][:, (idx % 2) * 512:((idx % 2) + 1) * 512],
                    S["eps"][:, (idx % 2) * 512:((idx % 2) + 1) * 512],
                    S["xep"][:, (idx % 2) * 512:((idx % 2) + 1) * 512]), "dve")
            p.ve(f_epv)

            def f_epst(eng, idx=idx, o=o, tb=tb):
                W(p, eng, "dve", ep_v[idx])
                ep_st[idx] = p.inc(eng.dma_start(
                    out=x_dst[o * 128:(o + 1) * 128, tb * 512:(tb + 1) * 512],
                    in_=S["xo"][:, (idx % 2) * 512:((idx % 2) + 1) * 512]), "sp", 16)
            p.sp(f_epst)

    o_ctx.close()
    outer_ctx.close()
    return lambda: ep_st[DC * 2 - 1]


def build_attn0_prog():
    p = Prog().open()
    x_in = p.dram_in("x_in", [D, T], F32)
    x_out = p.dram_out("x_out", [D, T], F32)
    decl_attn0_inputs(p)
    S = alloc_common(p)
    emit_attn0(p, S, x_in, x_out)
    final_wait(p)
    return p


def prep_attn0_inputs(inp):
    Wq = np.asarray(inp["L0_attn_Wq"], np.float32)
    bq = np.asarray(inp["L0_attn_bq"], np.float32)
    sel = np.concatenate([np.arange(4 * k * 128, (4 * k + 1) * 128) for k in range(8)])
    Wq_sel = Wq[sel]
    bq_sel = bq[sel]
    return {
        "wqsel_blk": wblocks_oi(Wq_sel),
        "bq_sel": np.ascontiguousarray(bq_sel.reshape(8, 128).T),
        "m_in": np.asarray(inp["L0_attn_M"], np.float32),
        "z_in": np.ascontiguousarray(np.asarray(inp["L0_attn_z"], np.float32).T),
        "wo_blk": wblocks_oi(np.asarray(inp["L0_attn_Wo"], np.float32)),
    }


def run_attn0(x_full, inp, trace=False):
    pr = build_attn0_prog()
    wmaps = prep_attn0_inputs(inp)
    consts = host_consts()
    in_maps = []
    for c in range(NCORES):
        shard = np.ascontiguousarray(np.asarray(x_full, np.float32)[c * T:(c + 1) * T, :].T)
        m = {"x_in": shard}
        m.update(wmaps)
        m.update(consts)
        in_maps.append(m)
    res = run_bass_kernel_spmd(pr.nc, in_maps, core_ids=list(range(NCORES)), trace=trace)
    pr.close()
    out = np.empty((NCORES * T, D), np.float32)
    for c in range(NCORES):
        out[c * T:(c + 1) * T, :] = res.results[c]["x_out"].T
    return out, res





def decl_ffn1_inputs(p):
    p.dram_in("qw_in", [512, D], I32)
    p.dram_in("qz_in", [32, 512], I32)
    p.dram_in("sc_in", [32, D], F16)
    p.dram_in("gidx_p32", [32, D], F32)
    p.dram_in("gidx_fm", [D, 32], F32)
    p.dram_in("iota32", [32, 1], F32)
    p.dram_in("iotaf", [128, 32], F32)
    p.dram_in("bias_h", [128, 32], F16)
    p.dram_int("wdq", [32, 128, DC * 128], BF16)


def alloc_ffn1_outer(p, S, ctx):
    S["negB"] = p.sb("negB", [128, D], BF16, ctx)
    S["onehT"] = p.sb("onehT", [128, D], BF16, ctx)
    S["onehF"] = p.sb("onehF", [128, DC * 32], BF16, ctx)
    S["xgs"] = p.sb("xgs", [128, T], BF16, ctx)
    S["bh"] = p.sb("bh", [128, 32], F32, ctx)


def alloc_ffn1_deq(p, S, ctx):
    S["scb"] = p.sb("scb", [128, D], BF16, ctx)


def alloc_ffn1_deq_prep(p, S, ctx):
    S["scf"] = p.sb("scf", [32, D], F32, ctx)
    S["zq"] = p.sb("zq", [32, D], I32, ctx)
    S["zqf"] = p.sb("zqf", [32, D], F32, ctx)
    S["qzs"] = p.sb("qzs", [32, 512], I32, ctx)
    S["zsh"] = p.sb("zsh", [32, 512], I32, ctx)
    S["bhf"] = p.sb("bhf", [128, 32], F16, ctx)


def alloc_ffn1_deq_main(p, S, ctx):
    S["qw"] = p.sb("qw", [128, D], I32, ctx)
    S["qn"] = p.sb("qn", [128, D], I32, ctx)
    S["qn2"] = p.sb("qn2", [128, D], I32, ctx)
    S["qfb"] = p.sb("qfb", [128, 2 * 512], BF16, ctx)
    S["wst"] = p.sb("wst", [128, D], BF16, ctx)


def alloc_ffn1_main(p, S, ctx):
    S["wn"] = p.sb("wn", [128, 2 * D], BF16, ctx)
    S["eps"] = p.sb("eps", [128, 2 * 512], F32, ctx)
    S["xep"] = p.sb("xep", [128, 2 * 512], F32, ctx)
    S["xo"] = p.sb("xo", [128, 2 * 512], F32, ctx)


def emit_ffn1(p, S, x_src, x_dst, dequant):
    P = S["p"]
    outer_ctx = p.scope()
    alloc_ffn1_outer(p, S, outer_ctx)

    ohh = {}
    deq_ctx = p.scope()
    if dequant:
        alloc_ffn1_deq(p, S, deq_ctx)
    oh_ctx = p.scope()
    S["gfm"] = p.sb("gfm", [128, DC * 32], F32, oh_ctx)
    S["scf2"] = p.sb("scf2", [32, D], F32, oh_ctx)
    S["io32"] = p.sb("io32", [32, 1], F32, oh_ctx)
    S["iof"] = p.sb("iof", [128, 32], F32, oh_ctx)

    def f_ohld(eng):
        p.inc(eng.dma_start(out=S["gfm"][:, :].rearrange("p (c e) -> p c e", c=DC),
                            in_=p.H["gidx_fm"].rearrange("(c p) e -> p c e", p=128)), "gp", 16)
        p.inc(eng.dma_start(out=S["scf2"][:, :], in_=p.H["gidx_p32"]), "gp", 16)
        p.inc(eng.dma_start(out=S["io32"][:, :], in_=p.H["iota32"]), "gp", 16)
        ohh["ld"] = p.inc(eng.dma_start(out=S["iof"][:, :], in_=p.H["iotaf"]), "gp", 16)
    p.gp(f_ohld)

    def f_oh(eng):
        W(p, eng, "gp", ohh["ld"])
        vchain(p, eng, eng.memset(S["onehT"][:, :], 0.0))
        vchain(p, eng, eng.memset(S["negB"][:, :], 0.0))
        vchain(p, eng, eng.memset(S["xgs"][:, :], 0.0))
        vchain(p, eng, eng.tensor_single_scalar(S["onehT"][0:32, :], S["scf2"][:, :],
                                               S["io32"][:, :], ALU.is_equal))
        for c in range(DC):
            eng.tensor_tensor(S["onehF"][:, c * 32:(c + 1) * 32],
                              S["gfm"][:, c * 32:(c + 1) * 32], S["iof"][:, :], ALU.is_equal)
        ohh["oh"] = p.inc(eng.engine_nop(), "dve")
    p.ve(f_oh)

    if dequant:
        dq, deq_main_ctx = emit_dequant2(p, S, ohh, oh_ctx)
    else:
        dq = {"laststore": lambda: None, "lastev": lambda: None}
        deq_main_ctx = None
        prep_ctx = p.scope()
        alloc_ffn1_deq_prep(p, S, prep_ctx)
        dd = {}

        def f_ld2(eng):
            p.inc(eng.dma_start(out=S["qzs"][:, :], in_=p.H["qz_in"]), "gp", 16)
            p.inc(eng.dma_start(out=S["scf"][:, :], in_=p.H["sc_in"]), "gp", 16)
            dd["ld"] = p.inc(eng.dma_start(out=S["bhf"][:, :], in_=p.H["bias_h"]), "gp", 16)
        p.gp(f_ld2)

        def f_prep2(eng):
            W(p, eng, "gp", dd["ld"])
            for jj in range(8):
                vchain(p, eng, eng.tensor_single_scalar(S["zsh"][:, :], S["qzs"][:, :],
                                                        4 * jj, ALU.logical_shift_right))
                zq_ap = bass.AP(S["zq"], jj, [[S["zq"].ap().ap[0][0], 32], [8, 512]])
                vchain(p, eng, eng.tensor_single_scalar(zq_ap, S["zsh"][:, :], 15, ALU.bitwise_and))
            vchain(p, eng, eng.tensor_copy(S["zqf"][:, :], S["zq"][:, :]))
            vchain(p, eng, eng.tensor_scalar_add(S["zqf"][:, :], S["zqf"][:, :], 1.0))
            vchain(p, eng, eng.tensor_mul(S["zqf"][:, :], S["zqf"][:, :], S["scf"][:, :]))
            vchain(p, eng, eng.tensor_scalar_mul(S["negB"][0:32, :], S["zqf"][:, :], -1.0))
            vchain(p, eng, eng.tensor_copy(S["bh"][:, :], S["bhf"][:, :]))
            dd["prep"] = vchain(p, eng, eng.tensor_scalar_mul(S["bh"][:, :], S["bh"][:, :], 0.5))
        p.ve(f_prep2)
        dq["negb"] = lambda: dd["prep"]
        prep_ctx.close()
        oh_ctx.close()

    ln_ctx = p.scope()
    alloc_ln(p, S, ln_ctx, gate=False)
    ms = emit_ln(p, S, x_src, perm_stride=True, gate=False,
                 start_gate=(("dve", dq["negb"]),) if not dequant else ())
    xln_done = ms["xln"]
    ln_ctx.close()
    if deq_main_ctx is not None:
        deq_main_ctx.close()
    deq_ctx.close()

    xg = {}

    def f_xg(eng):
        W(p, eng, "dve", xln_done())
        W(p, eng, "dve", ohh["oh"])
        lev = dq["lastev"]()
        if lev is not None:
            W(p, eng, "dve", lev)
        i = None
        for tb in range(2):
            for c in range(DC):
                i = eng.matmul(P[6 + tb][0:32, :], S["onehF"][:, c * 32:(c + 1) * 32],
                               S["xln"][:, c * T + tb * 512:c * T + (tb + 1) * 512],
                               start=(c == 0), stop=(c == DC - 1))
        xg["mm"] = p.inc(i, "pe")
    p.pe(f_xg)

    def f_xgv(eng):
        W(p, eng, "pe", xg["mm"])
        eng.tensor_copy(S["xgs"][0:32, 0:512], P[6][0:32, :])
        xg["ev"] = p.inc(eng.tensor_copy(S["xgs"][0:32, 512:T], P[7][0:32, :]), "dve")
    p.ve(f_xgv)

    main_ctx = p.scope()
    alloc_ffn1_main(p, S, main_ctx)
    wn = S["wn"]
    wld, mm_done, ep_s, ep_v, ep_x, ep_st = {}, {}, {}, {}, {}, {}

    for n in range(DC):
        buf = n % 2

        def f_wld(eng, n=n, buf=buf):
            if n == 0:
                ls = dq["laststore"]()
                if ls is not None:
                    W(p, eng, "sp", ls)
                W(p, eng, "dve", xln_done())
            if n >= 2:
                W(p, eng, "pe", mm_done[(n - 2, 1)])
            wld[n] = p.inc(eng.dma_start(out=wn[:, buf * D:(buf + 1) * D],
                                         in_=p.H["wdq"][n, :, :]), "gp", 16)
        p.gp(f_wld)

        for tb in range(2):
            idx = n * 2 + tb
            bank = idx % 4

            def f_spx(eng, idx=idx, n=n, tb=tb):
                if idx == 0:
                    W(p, eng, "dve", xln_done())
                if idx >= 2:
                    W(p, eng, "dve", ep_v[idx - 2])
                ep_x[idx] = p.inc(eng.dma_start(
                    out=S["xep"][:, (idx % 2) * 512:((idx % 2) + 1) * 512],
                    in_=x_src[n * 128:(n + 1) * 128, tb * 512:(tb + 1) * 512]), "sp", 16)
            p.sp(f_spx)

            def f_pem(eng, n=n, tb=tb, idx=idx, bank=bank, buf=buf):
                W(p, eng, "gp", wld[n])
                if idx == 0:
                    W(p, eng, "dve", xg["ev"])
                    W(p, eng, "dve", dq["negb"]())
                if idx >= 4:
                    W(p, eng, "dve", ep_s[idx - 4])
                for k in range(DC):
                    eng.matmul(P[bank][:, :], wn[:, buf * D + k * 128:buf * D + (k + 1) * 128],
                               S["xln"][:, k * T + tb * 512:k * T + (tb + 1) * 512],
                               start=(k == 0), stop=False)
                i = eng.matmul(P[bank][:, :], S["negB"][:, n * 128:(n + 1) * 128],
                               S["xgs"][:, tb * 512:(tb + 1) * 512],
                               start=False, stop=True)
                mm_done[(n, tb)] = p.inc(i, "pe")
            p.pe(f_pem)

            def f_eps(eng, idx=idx, bank=bank, n=n):
                W(p, eng, "pe", mm_done[(n, idx % 2)])
                if idx >= 2:
                    W(p, eng, "dve", ep_v[idx - 2])
                ep_s[idx] = p.inc(eng.tensor_scalar(S["eps"][:, (idx % 2) * 512:((idx % 2) + 1) * 512],
                                                    P[bank][:, :], 0.5, S["bh"][:, n:n + 1],
                                                    ALU.mult, ALU.add), "dve")
            p.ve(f_eps)

            def f_epv(eng, idx=idx):
                W(p, eng, "dve", ep_s[idx])
                W(p, eng, "sp", ep_x[idx])
                if idx >= 2:
                    W(p, eng, "sp", ep_st[idx - 2])
                ep_v[idx] = p.inc(eng.tensor_add(
                    S["xo"][:, (idx % 2) * 512:((idx % 2) + 1) * 512],
                    S["eps"][:, (idx % 2) * 512:((idx % 2) + 1) * 512],
                    S["xep"][:, (idx % 2) * 512:((idx % 2) + 1) * 512]), "dve")
            p.ve(f_epv)

            def f_epst(eng, idx=idx, n=n, tb=tb):
                W(p, eng, "dve", ep_v[idx])
                ep_st[idx] = p.inc(eng.dma_start(
                    out=x_dst[n * 128:(n + 1) * 128, tb * 512:(tb + 1) * 512],
                    in_=S["xo"][:, (idx % 2) * 512:((idx % 2) + 1) * 512]), "sp", 16)
            p.sp(f_epst)

    main_ctx.close()
    outer_ctx.close()
    return lambda: ep_st[DC * 2 - 1]


def emit_dequant2(p, S, ohh, oh_ctx):
    P = S["p"]
    d = {}
    prep_ctx = p.scope()
    alloc_ffn1_deq_prep(p, S, prep_ctx)

    def f_ld(eng):
        p.inc(eng.dma_start(out=S["qzs"][:, :], in_=p.H["qz_in"]), "gp", 16)
        p.inc(eng.dma_start(out=S["scf"][:, :], in_=p.H["sc_in"]), "gp", 16)
        d["ld"] = p.inc(eng.dma_start(out=S["bhf"][:, :], in_=p.H["bias_h"]), "gp", 16)
    p.gp(f_ld)

    def f_prep(eng):
        W(p, eng, "gp", d["ld"])
        vchain(p, eng, eng.memset(S["scb"][:, :], 0.0))
        vchain(p, eng, eng.tensor_copy(S["scb"][0:32, :], S["scf"][:, :]))
        for jj in range(8):
            vchain(p, eng, eng.tensor_single_scalar(S["zsh"][:, :], S["qzs"][:, :],
                                                    4 * jj, ALU.logical_shift_right))
            zq_ap = bass.AP(S["zq"], jj, [[S["zq"].ap().ap[0][0], 32], [8, 512]])
            vchain(p, eng, eng.tensor_single_scalar(zq_ap, S["zsh"][:, :], 15, ALU.bitwise_and))
        vchain(p, eng, eng.tensor_copy(S["zqf"][:, :], S["zq"][:, :]))
        vchain(p, eng, eng.tensor_scalar_add(S["zqf"][:, :], S["zqf"][:, :], 1.0))
        vchain(p, eng, eng.tensor_mul(S["zqf"][:, :], S["zqf"][:, :], S["scf"][:, :]))
        vchain(p, eng, eng.tensor_scalar_mul(S["negB"][0:32, :], S["zqf"][:, :], -1.0))
        vchain(p, eng, eng.tensor_copy(S["bh"][:, :], S["bhf"][:, :]))
        d["prep"] = vchain(p, eng, eng.tensor_scalar_mul(S["bh"][:, :], S["bh"][:, :], 0.5))
    p.ve(f_prep)

    prep_ctx.close()
    oh_ctx.close()
    main_ctx = p.scope()
    alloc_ffn1_deq_main(p, S, main_ctx)
    qld, qsh, wmm, wev, wstd = {}, {}, {}, {}, {}
    for c in range(DC):
        j, r = c // 4, c % 4

        def f_qld(eng, c=c, r=r):
            if c == 0:
                W(p, eng, "dve", d["prep"])
            if c >= 1:
                W(p, eng, "dve", qsh[c - 1])
            qld[c] = p.inc(eng.dma_start(out=S["qw"][:, :],
                                         in_=p.H["qw_in"][128 * r:128 * (r + 1), :]), "sp", 16)
        p.sp(f_qld)

        def f_qsh(eng, c=c, j=j):
            W(p, eng, "sp", qld[c])
            if c >= 1:
                W(p, eng, "dve", wev[(c - 1, 7)])
            vchain(p, eng, eng.tensor_single_scalar(S["qn"][:, :], S["qw"][:, :],
                                                    4 * j, ALU.logical_shift_right))
            qsh[c] = p.inc(eng.tensor_single_scalar(S["qn2"][:, :], S["qn"][:, :], 15,
                                                    ALU.bitwise_and), "dve")
        p.ve(f_qsh)

        for ns in range(8):
            def f_amm(eng, c=c, ns=ns):
                if c == 0 and ns == 0:
                    W(p, eng, "dve", d["prep"])
                prev = wev.get((c, ns - 2)) if ns >= 2 else (wev.get((c - 1, 6 + ns)) if c >= 1 else None)
                if prev is not None:
                    W(p, eng, "dve", prev)
                bank = P[6] if ns % 2 == 0 else P[7]
                wmm[(c, ns)] = p.inc(eng.matmul(bank[:, :], S["onehT"][:, c * 128:(c + 1) * 128],
                                                S["scb"][:, ns * 512:(ns + 1) * 512],
                                                start=True, stop=True), "pe")
            p.pe(f_amm)

            def f_wev(eng, c=c, ns=ns):
                W(p, eng, "pe", wmm[(c, ns)])
                W(p, eng, "dve", qsh[c])
                if ns == 0 and c >= 1:
                    W(p, eng, "sp", wstd[c - 1])
                bank = P[6] if ns % 2 == 0 else P[7]
                par = ns % 2
                vchain(p, eng, eng.tensor_copy(S["qfb"][:, par * 512:(par + 1) * 512],
                                               S["qn2"][:, ns * 512:(ns + 1) * 512]))
                wev[(c, ns)] = p.inc(eng.tensor_mul(S["wst"][:, ns * 512:(ns + 1) * 512],
                                                    S["qfb"][:, par * 512:(par + 1) * 512],
                                                    bank[:, :]), "dve")
            p.ve(f_wev)

        def f_wst(eng, c=c):
            W(p, eng, "dve", wev[(c, 7)])
            for gq in range(8):
                i = eng.dma_start(
                    out=p.H["wdq"][gq * 4:(gq + 1) * 4, :, c * 128:(c + 1) * 128].rearrange("n p m -> p n m"),
                    in_=S["wst"][:, gq * 512:(gq + 1) * 512].rearrange("p (n m) -> p n m", n=4))
                wstd[c] = p.inc(i, "sp", 16)
        p.sp(f_wst)

    d["laststore"] = lambda: wstd[DC - 1]
    d["lastev"] = lambda: wev[(DC - 1, 7)]
    d["negb"] = lambda: d["prep"]
    return d, main_ctx


def build_ffn1_prog():
    p = Prog().open()
    x_in = p.dram_in("x_in", [D, T], F32)
    x_out = p.dram_out("x_out", [D, T], F32)
    decl_ffn1_inputs(p)
    S = alloc_common(p)
    emit_ffn1(p, S, x_in, x_out, dequant=True)
    final_wait(p)
    return p


def perm_kprime():
    idx = np.empty(D, np.int64)
    for c in range(DC):
        j, r = c // 4, c % 4
        pvec = np.arange(128)
        idx[c * 128:(c + 1) * 128] = 8 * (128 * r + pvec) + j
    return idx


def prep_ffn1_inputs(inp):
    gidx = np.asarray(inp["L1_ffn_g_idx"], np.int64)
    kp = perm_kprime()
    gperm = gidx[kp].astype(np.float32)
    bias = np.asarray(inp["L1_ffn_bias"])
    return {
        "qw_in": np.asarray(inp["L1_ffn_qweight"], np.int32),
        "qz_in": np.asarray(inp["L1_ffn_qzeros"], np.int32),
        "sc_in": np.asarray(inp["L1_ffn_scales"], np.float16),
        "gidx_p32": np.ascontiguousarray(np.broadcast_to(gperm, (32, D))),
        "gidx_fm": np.ascontiguousarray(np.broadcast_to(gperm[:, None], (D, 32))),
        "iota32": np.arange(32, dtype=np.float32).reshape(32, 1),
        "iotaf": np.ascontiguousarray(np.broadcast_to(np.arange(32, dtype=np.float32), (128, 32))),
        "bias_h": np.ascontiguousarray(np.asarray(bias, np.float16).reshape(32, 128).T),
    }


def run_ffn1(x_full, inp, trace=False):
    pr = build_ffn1_prog()
    wmaps = prep_ffn1_inputs(inp)
    consts = host_consts()
    in_maps = []
    for c in range(NCORES):
        shard = np.ascontiguousarray(np.asarray(x_full, np.float32)[c * T:(c + 1) * T, :].T)
        m = {"x_in": shard}
        m.update(wmaps)
        m.update(consts)
        in_maps.append(m)
    res = run_bass_kernel_spmd(pr.nc, in_maps, core_ids=list(range(NCORES)), trace=trace)
    pr.close()
    out = np.empty((NCORES * T, D), np.float32)
    for c in range(NCORES):
        out[c * T:(c + 1) * T, :] = res.results[c]["x_out"].T
    return out, res





def decl_qkv1_inputs(p):
    p.dram_in("wq1_blk", [DC, 128, D], F32)
    p.dram_in("wk1_blk", [8, 128, D], F32)
    p.dram_in("wv1_blk", [8, 128, D], F32)
    p.dram_in("cs_in", [128, T], F32)
    p.dram_int("qkvb", [6144, T], BF16)


def alloc_qkv1(p, S, ctx):
    S["wqb"] = p.sb("wqb", [128, 2 * D], BF16, ctx)
    S["css"] = p.sb("css", [128, T], F32, ctx)
    S["qst"] = p.sb("qst", [128, 4 * T], BF16, ctx)
    S["rta"] = p.sb("rta", [128, 512], F32, ctx)
    S["rtb"] = p.sb("rtb", [128, 512], F32, ctx)


def emit_qkv1(p, S, x_src):
    P = S["p"]
    ln_ctx = p.scope()
    alloc_ln(p, S, ln_ctx, gate=False)
    ms = emit_ln(p, S, x_src, gate=False)
    xln_done = ms["xln"]
    ln_ctx.close()
    q_ctx = p.scope()
    alloc_qkv1(p, S, q_ctx)
    qkvb = p.H["qkvb"]
    wld, mmd, rpd, std, sld = {}, {}, {}, {}, {}

    def f_cs(eng):
        W(p, eng, "dve", xln_done())
        sld["cs"] = p.inc(eng.dma_start(out=S["css"][:, :], in_=p.H["cs_in"]), "gp", 16)
    p.gp(f_cs)

    for oc in range(48):
        buf = oc % 2

        def f_w(eng, oc=oc, buf=buf):
            if oc >= 2:
                W(p, eng, "pe", mmd[(oc - 2, 1)])
            if oc < 32:
                src = p.H["wq1_blk"][oc, :, :]
            elif oc < 40:
                src = p.H["wk1_blk"][oc - 32, :, :]
            else:
                src = p.H["wv1_blk"][oc - 40, :, :]
            wld[oc] = p.inc(eng.dma_start(out=S["wqb"][:, buf * D:(buf + 1) * D], in_=src), "gp", 16)
        p.gp(f_w)

        for tb in range(2):
            idx = oc * 2 + tb
            bank = idx % 4

            def f_mm(eng, oc=oc, tb=tb, idx=idx, bank=bank, buf=buf):
                W(p, eng, "gp", wld[oc])
                if idx >= 4:
                    W(p, eng, "dve", rpd[idx - 4])
                i = None
                for c in range(DC):
                    i = eng.matmul(P[bank][:, :], S["wqb"][:, buf * D + c * 128:buf * D + (c + 1) * 128],
                                   S["xln"][:, c * T + tb * 512:c * T + (tb + 1) * 512],
                                   start=(c == 0), stop=(c == DC - 1))
                mmd[(oc, tb)] = p.inc(i, "pe")
            p.pe(f_mm)

            if oc < 40:
                def f_rope(eng, oc=oc, tb=tb, idx=idx, bank=bank):
                    W(p, eng, "pe", mmd[(oc, tb)])
                    if idx == 0:
                        W(p, eng, "gp", sld["cs"])
                    if idx >= 2:
                        W(p, eng, "sp", std[idx - 2])
                    par = idx % 2
                    cos = S["css"][0:64, tb * 512:(tb + 1) * 512]
                    sin = S["css"][64:128, tb * 512:(tb + 1) * 512]
                    dst = S["qst"][:, par * T:par * T + 512]
                    eng.tensor_tensor(S["rta"][0:64, :], P[bank][0:64, :], cos, ALU.mult)
                    vchain(p, eng, eng.tensor_tensor(S["rtb"][0:64, :], P[bank][64:128, :], sin, ALU.mult))
                    eng.tensor_sub(dst[0:64, :], S["rta"][0:64, :], S["rtb"][0:64, :])
                    eng.tensor_tensor(S["rta"][0:64, :], P[bank][0:64, :], sin, ALU.mult)
                    vchain(p, eng, eng.tensor_tensor(S["rtb"][0:64, :], P[bank][64:128, :], cos, ALU.mult))
                    rpd[idx] = p.inc(eng.tensor_add(dst[64:128, :], S["rta"][0:64, :], S["rtb"][0:64, :]), "dve")
                p.ve(f_rope)
            else:
                def f_vcp(eng, oc=oc, tb=tb, idx=idx, bank=bank):
                    W(p, eng, "pe", mmd[(oc, tb)])
                    if idx >= 2:
                        W(p, eng, "sp", std[idx - 2])
                    par = idx % 2
                    rpd[idx] = p.inc(eng.tensor_copy(S["qst"][:, par * T:par * T + 512],
                                                     P[bank][:, :]), "dve")
                p.ve(f_vcp)

            def f_st(eng, oc=oc, tb=tb, idx=idx):
                W(p, eng, "dve", rpd[idx])
                par = idx % 2
                if oc < 32:
                    row0 = oc * 128
                elif oc < 40:
                    row0 = 4096 + (oc - 32) * 128
                else:
                    row0 = 5120 + (oc - 40) * 128
                std[idx] = p.inc(eng.dma_start(
                    out=qkvb[row0:row0 + 128, tb * 512:(tb + 1) * 512],
                    in_=S["qst"][:, par * T:par * T + 512]), "sp", 16)
            p.sp(f_st)

    q_ctx.close()
    return lambda: std[95]




def decl_attn1core_inputs(p):
    p.dram_in("q_in", [512, 8 * T], BF16)
    p.dram_in("k_in", [128, 8 * T], BF16)
    p.dram_in("v_in", [128, 8 * T], BF16)
    p.dram_in("mask_d", [128, 4 * 512], BF16)
    p.dram_out("attn_out", [512, 8 * T], BF16)


def alloc_attn1core(p, S, ctx):
    S["ksl"] = p.sb("ksl", [128, 64 * 128], BF16, ctx)
    S["vfm"] = p.sb("vfm", [128, 64 * 128], BF16, ctx)
    S["vtf"] = p.sb("vtf", [128, 2 * 128], F32, ctx)
    S["vsl"] = p.sb("vsl", [128, 64 * 128], BF16, ctx)
    S["qtl"] = p.sb("qtl", [128, 2 * 512], BF16, ctx)
    S["psc"] = p.sb("psc", [128, 2 * 512], BF16, ctx)
    S["mask"] = p.sb("mask", [128, 4 * 512], BF16, ctx)
    S["aow"] = p.sb("aow", [128, 2 * 512], BF16, ctx)
    S["rbs"] = p.sb("rbs", [128, 2 * 512], F32, ctx)


def emit_attn1core(p, S, gate=(), fused=False):
    P = S["p"]
    _thunks = []

    def _mk(en):
        def deco(fn):
            _thunks.append((en, fn))
        return deco
    _add_gp, _add_sp, _add_pe, _add_sc, _add_ve = (_mk("gp"), _mk("sp"),
                                                   _mk("pe"), _mk("sc"), _mk("ve"))
    core_ctx = p.scope()
    alloc_attn1core(p, S, core_ctx)
    SC = 1.0 / np.sqrt(128.0)
    sld, qld, scd, exd, mkd, pvd, tl = MD(), MD(), MD(), MD(), MD(), MD(), MD()

    if fused:
        qall = p.H["qkv_all"]

        def f_slab(eng):
            for sgn, sgv in gate:
                W(p, eng, sgn, sgv() if callable(sgv) else sgv)
            with eng.register("ra") as ra, eng.register("rb") as rb:
                eng.reg_load(ra, S["rks"][0:1, 0:1])
                eng.reg_alu(ra, ra, 128 * T, ALU.mult)
                eng.reg_add(rb, ra, 4096 * T)
                p.inc(eng.dma_start(
                    out=S["ksl"][:, :].rearrange("p (r t) -> p r t", r=NCORES),
                    in_=bass.AP(qall.tensor, reg_off(eng, rb), [[T, 128], [6144 * T, NCORES], [1, T]])), "gp", 16)
                eng.reg_add(rb, ra, 5120 * T)
                p.inc(eng.dma_start(
                    out=S["vfm"][:, :].rearrange("p (r t) -> p r t", r=NCORES),
                    in_=bass.AP(qall.tensor, reg_off(eng, rb), [[T, 128], [6144 * T, NCORES], [1, T]])), "gp", 16)
                eng.reg_alu(ra, ra, 4, ALU.mult)
                for j in range(4):
                    eng.reg_add(rb, ra, j * 128 * T)
                    p.inc(eng.dma_start(
                        out=p.H["q_mine"][j * 128:(j + 1) * 128, :].rearrange("p (r t) -> p r t", r=NCORES),
                        in_=bass.AP(qall.tensor, reg_off(eng, rb), [[T, 128], [6144 * T, NCORES], [1, T]])), "gp", 16)
            sld["s"] = p.inc(eng.dma_start(out=S["mask"][:, :], in_=p.H["mask_d"]), "gp", 16)
        _add_gp(f_slab)
        p.H["q_in"] = p.H["q_mine"]
    else:
        def f_slab(eng):
            for sgn, sgv in gate:
                W(p, eng, sgn, sgv() if callable(sgv) else sgv)
            p.inc(eng.dma_start(out=S["ksl"][:, :], in_=p.H["k_in"]), "gp", 16)
            p.inc(eng.dma_start(out=S["vfm"][:, :], in_=p.H["v_in"]), "gp", 16)
            sld["s"] = p.inc(eng.dma_start(out=S["mask"][:, :], in_=p.H["mask_d"]), "gp", 16)
        _add_gp(f_slab)

    vt = MD()
    for sc64 in range(64):
        def f_vt1(eng, sc64=sc64):
            if sc64 == 0:
                W(p, eng, "gp", sld["s"])
            if sc64 >= 2:
                W(p, eng, "pe", vt[("t", sc64 - 2)])
            par = sc64 % 2
            vt[("c", sc64)] = p.inc(eng.tensor_copy(S["vtf"][:, par * 128:(par + 1) * 128],
                                                    S["vfm"][:, sc64 * 128:(sc64 + 1) * 128]), "dve")
        _add_ve(f_vt1)

        def f_vt2(eng, sc64=sc64):
            W(p, eng, "dve", vt[("c", sc64)])
            bank = P[6] if sc64 % 2 == 0 else P[7]
            if sc64 >= 2:
                W(p, eng, "dve", vt[("e", sc64 - 2)])
            par = sc64 % 2
            vt[("t", sc64)] = p.inc(eng.transpose(bank[:, 0:128], S["vtf"][:, par * 128:(par + 1) * 128],
                                                  S["identf"][:, :]), "pe")
        _add_pe(f_vt2)

        def f_vt3(eng, sc64=sc64):
            W(p, eng, "pe", vt[("t", sc64)])
            bank = P[6] if sc64 % 2 == 0 else P[7]
            vt[("e", sc64)] = p.inc(eng.tensor_copy(S["vsl"][:, sc64 * 128:(sc64 + 1) * 128],
                                                    bank[:, 0:128]), "dve")
        _add_ve(f_vt3)

    prev_rb = {}
    for j in range(4):
        for qb in range(16):
            jqi = j * 16 + qb
            nsc = 4 * qb + 4
            par2 = jqi % 2
            Po = P[par2]
            Pl = P[2 + par2]
            pe_scores = []
            pe_pvs = []

            def f_q(eng, j=j, qb=qb, jqi=jqi, par2=par2):
                if jqi == 0 and fused:
                    W(p, eng, "gp", sld["s"])
                if jqi >= 2:
                    W(p, eng, "pe", scd[(jqi - 2, "last")])
                src = p.H["q_in"][j * 128:(j + 1) * 128, qb * 512:(qb + 1) * 512]
                qld[jqi] = p.inc(eng.dma_start(
                    out=S["qtl"][:, par2 * 512:(par2 + 1) * 512], in_=src), "sp", 16)
            _add_sp(f_q)

            for sc in range(nsc):
                spar = sc % 2

                def f_sc(eng, jqi=jqi, sc=sc, spar=spar, par2=par2, nsc=nsc):
                    if jqi == 0 and sc == 0:
                        W(p, eng, "gp", sld["s"])
                    W(p, eng, "sp", qld[jqi])
                    if sc >= 2:
                        W(p, eng, "act", exd[(jqi, sc - 2)])
                    elif jqi >= 1:
                        W(p, eng, "act", exd[(jqi - 1, "last")])
                    i = eng.matmul(P[4 + spar][:, :], S["ksl"][:, sc * 128:(sc + 1) * 128],
                                   S["qtl"][:, par2 * 512:(par2 + 1) * 512], start=True, stop=True)
                    scd[(jqi, sc)] = p.inc(i, "pe")
                    if sc == nsc - 1:
                        scd[(jqi, "last")] = scd[(jqi, sc)]
                pe_scores.append(f_sc)

                def f_exp(eng, jqi=jqi, sc=sc, spar=spar, nsc=nsc):
                    W(p, eng, "pe", scd[(jqi, sc)])
                    if (jqi, sc - 2) in pvd:
                        W(p, eng, "pe", pvd[(jqi, sc - 2)])
                    elif sc < 2 and jqi >= 1:
                        W(p, eng, "pe", pvd[(jqi - 1, "last")])
                    i = eng.activation(S["psc"][:, spar * 512:(spar + 1) * 512],
                                       P[4 + spar][:, :], AF.Exp, bias=S["fc"][0:128, 1:2], scale=SC)
                    exd[(jqi, sc)] = p.inc(i, "act")
                    if sc == nsc - 1:
                        exd[(jqi, "last")] = exd[(jqi, sc)]
                _add_sc(f_exp)

                if sc >= 4 * qb:
                    m = sc - 4 * qb

                    def f_mk(eng, jqi=jqi, sc=sc, spar=spar, m=m):
                        W(p, eng, "act", exd[(jqi, sc)])
                        mkd[(jqi, sc)] = p.inc(eng.tensor_tensor(
                            S["psc"][:, spar * 512:(spar + 1) * 512],
                            S["psc"][:, spar * 512:(spar + 1) * 512],
                            S["mask"][:, m * 512:(m + 1) * 512], ALU.mult), "dve")
                    _add_ve(f_mk)

                def f_pv(eng, jqi=jqi, qb=qb, sc=sc, spar=spar, nsc=nsc, Po=Po, Pl=Pl):
                    if sc == 0:
                        if jqi == 0:
                            W(p, eng, "dve", vt[("e", 63)])
                        if jqi >= 2:
                            W(p, eng, "dve", tl[(jqi - 2, "ev")])
                    if (jqi, sc) in mkd:
                        W(p, eng, "dve", mkd[(jqi, sc)])
                    else:
                        W(p, eng, "act", exd[(jqi, sc)])
                    st, spp = (sc == 0), (sc == nsc - 1)
                    eng.matmul(Po[:, :], S["vsl"][:, sc * 128:(sc + 1) * 128],
                               S["psc"][:, spar * 512:(spar + 1) * 512], start=st, stop=spp)
                    i = eng.matmul(Pl[0:1, :], S["onesb"][:, :],
                                   S["psc"][:, spar * 512:(spar + 1) * 512], start=st, stop=spp)
                    pvd[(jqi, sc)] = p.inc(i, "pe")
                    if spp:
                        pvd[(jqi, "last")] = pvd[(jqi, sc)]
                pe_pvs.append(f_pv)

            _add_pe(pe_scores[0])
            if nsc > 1:
                _add_pe(pe_scores[1])
            for sc in range(nsc):
                if sc + 2 < nsc:
                    _add_pe(pe_scores[sc + 2])
                _add_pe(pe_pvs[sc])

            def f_tl1(eng, jqi=jqi, par2=par2, Pl=Pl):
                W(p, eng, "pe", pvd[(jqi, "last")])
                vchain(p, eng, eng.reciprocal(S["rbs"][0:1, par2 * 512:par2 * 512 + 512], Pl[0:1, :]))
                tl[(jqi, "rl")] = p.inc(eng.tensor_copy(S["brow"][0:1, :],
                                                        S["rbs"][0:1, par2 * 512:par2 * 512 + 512]), "dve")
            _add_ve(f_tl1)

            def f_tl2(eng, jqi=jqi):
                W(p, eng, "dve", tl[(jqi, "rl")])
                if jqi >= 1:
                    W(p, eng, "dve", tl[(jqi - 1, "ev")])
                tl[(jqi, "rb")] = p.inc(eng.matmul(P[6][:, :], S["ones128"][:, :],
                                                   S["brow"][:, :], start=True, stop=True), "pe")
            _add_pe(f_tl2)

            def f_tl3(eng, jqi=jqi, par2=par2, Po=Po):
                W(p, eng, "pe", tl[(jqi, "rb")])
                if jqi >= 2:
                    W(p, eng, "sp", tl[(jqi - 2, "st")])
                vchain(p, eng, eng.tensor_copy(S["rbs"][:, par2 * 512:(par2 + 1) * 512], P[6][:, :]))
                tl[(jqi, "ev")] = p.inc(eng.tensor_mul(
                    S["aow"][:, par2 * 512:(par2 + 1) * 512], Po[:, :],
                    S["rbs"][:, par2 * 512:(par2 + 1) * 512]), "dve")
            _add_ve(f_tl3)

            def f_tl4(eng, j=j, qb=qb, jqi=jqi, par2=par2):
                W(p, eng, "dve", tl[(jqi, "ev")])
                dst = p.H["attnb" if fused else "attn_out"]
                tl[(jqi, "st")] = p.inc(eng.dma_start(
                    out=dst[j * 128:(j + 1) * 128, qb * 512:(qb + 1) * 512],
                    in_=S["aow"][:, par2 * 512:(par2 + 1) * 512]), "sp", 16)
            _add_sp(f_tl4)

    run_twopass(p, _thunks)
    core_ctx.close()
    return lambda: tl[(63, "st")]



def decl_outproj1_inputs(p):
    p.dram_in("attn_in", [D, T], BF16)
    p.dram_in("wo1_blk", [DC, 128, D], F32)


def alloc_outproj1(p, S, ctx):
    S["asl"] = p.sb("asl", [128, DC * T], BF16, ctx)
    S["wo"] = p.sb("wo", [128, 2 * D], BF16, ctx)
    S["eps"] = p.sb("eps", [128, 2 * 512], F32, ctx)
    S["xep"] = p.sb("xep", [128, 2 * 512], F32, ctx)
    S["xo"] = p.sb("xo", [128, 2 * 512], F32, ctx)


def emit_outproj1(p, S, x_src, x_dst, gate=(), fused=False):
    P = S["p"]
    o_ctx = p.scope()
    alloc_outproj1(p, S, o_ctx)
    ald, owld, omm, ep_s, ep_v, ep_x, ep_st = {}, {}, {}, {}, {}, {}, {}

    def f_asl(eng):
        for sgn, sgv in gate:
            W(p, eng, sgn, sgv() if callable(sgv) else sgv)
        if not fused:
            ald["a"] = p.inc(eng.dma_start(
                out=S["asl"][:, :].rearrange("p (c t) -> p c t", c=DC),
                in_=p.H["attn_in"].rearrange("(c p) t -> p c t", p=128)), "gp", 16)
        else:
            aall = p.H["attn_all"]
            with eng.register("rc") as rc:
                eng.reg_load(rc, S["rks"][0:1, 0:1])
                eng.reg_alu(rc, rc, T, ALU.mult)
                with eng.register("rd") as rd:
                    for g4 in range(4):
                        eng.reg_add(rd, rc, g4 * 8 * 128 * (8 * T))
                        ald["a"] = p.inc(eng.dma_start(
                            out=S["asl"][:, g4 * 8 * T:(g4 + 1) * 8 * T].rearrange(
                                "p (cc t) -> p cc t", cc=8),
                            in_=bass.AP(aall.tensor, reg_off(eng, rd),
                                        [[8 * T, 128], [128 * 8 * T, 8], [1, T]])), "gp", 16)
    p.gp(f_asl)

    for o in range(DC):
        buf = o % 2

        def f_gpo(eng, o=o, buf=buf):
            if o >= 2:
                W(p, eng, "pe", omm[(o - 2, 1)])
            owld[o] = p.inc(eng.dma_start(out=S["wo"][:, buf * D:(buf + 1) * D],
                                          in_=p.H["wo1_blk"][o, :, :]), "gp", 16)
        p.gp(f_gpo)

        for tb in range(2):
            idx = o * 2 + tb
            bank = idx % 4

            def f_spx(eng, idx=idx, o=o, tb=tb):
                if idx >= 2:
                    W(p, eng, "dve", ep_v[idx - 2])
                ep_x[idx] = p.inc(eng.dma_start(
                    out=S["xep"][:, (idx % 2) * 512:((idx % 2) + 1) * 512],
                    in_=x_src[o * 128:(o + 1) * 128, tb * 512:(tb + 1) * 512]), "sp", 16)
            p.sp(f_spx)

            def f_peo(eng, o=o, tb=tb, idx=idx, bank=bank, buf=buf):
                W(p, eng, "gp", owld[o])
                if idx == 0:
                    W(p, eng, "gp", ald["a"])
                if idx >= 4:
                    W(p, eng, "act", ep_s[idx - 4])
                i = None
                for cc in range(DC):
                    i = eng.matmul(P[bank][:, :], S["wo"][:, buf * D + cc * 128:buf * D + (cc + 1) * 128],
                                   S["asl"][:, cc * T + tb * 512:cc * T + (tb + 1) * 512],
                                   start=(cc == 0), stop=(cc == DC - 1))
                omm[(o, tb)] = p.inc(i, "pe")
            p.pe(f_peo)

            def f_eps(eng, idx=idx, bank=bank):
                W(p, eng, "pe", omm[(idx // 2, idx % 2)])
                if idx >= 2:
                    W(p, eng, "dve", ep_v[idx - 2])
                ep_s[idx] = p.inc(eng.activation(S["eps"][:, (idx % 2) * 512:((idx % 2) + 1) * 512],
                                                 P[bank][:, :], AF.Copy), "act")
            p.sc(f_eps)

            def f_epv(eng, idx=idx):
                W(p, eng, "act", ep_s[idx])
                W(p, eng, "sp", ep_x[idx])
                if idx >= 2:
                    W(p, eng, "sp", ep_st[idx - 2])
                ep_v[idx] = p.inc(eng.tensor_add(
                    S["xo"][:, (idx % 2) * 512:((idx % 2) + 1) * 512],
                    S["eps"][:, (idx % 2) * 512:((idx % 2) + 1) * 512],
                    S["xep"][:, (idx % 2) * 512:((idx % 2) + 1) * 512]), "dve")
            p.ve(f_epv)

            def f_epst(eng, idx=idx, o=o, tb=tb):
                W(p, eng, "dve", ep_v[idx])
                ep_st[idx] = p.inc(eng.dma_start(
                    out=x_dst[o * 128:(o + 1) * 128, tb * 512:(tb + 1) * 512],
                    in_=S["xo"][:, (idx % 2) * 512:((idx % 2) + 1) * 512]), "sp", 16)
            p.sp(f_epst)

    o_ctx.close()
    return lambda: ep_st[DC * 2 - 1]





def build_qkv1_prog():
    p = Prog().open()
    x_in = p.dram_in("x_in", [D, T], F32)
    p.dram_in("wq1_blk", [DC, 128, D], F32)
    p.dram_in("wk1_blk", [8, 128, D], F32)
    p.dram_in("wv1_blk", [8, 128, D], F32)
    p.dram_in("cs_in", [128, T], F32)
    p.dram_out("qkvb", [6144, T], BF16)
    S = alloc_common(p)
    emit_qkv1(p, S, x_in)
    final_wait(p)
    return p


def build_attn1core_prog():
    p = Prog().open()
    decl_attn1core_inputs(p)
    S = alloc_common(p)
    emit_attn1core(p, S)
    final_wait(p)
    return p


def build_outproj1_prog():
    p = Prog().open()
    x_in = p.dram_in("x_in", [D, T], F32)
    x_out = p.dram_out("x_out", [D, T], F32)
    decl_outproj1_inputs(p)
    S = alloc_common(p)
    emit_outproj1(p, S, x_in, x_out)
    final_wait(p)
    return p


def host_cos_sin(core):
    half = 64
    inv = 1.0 / (10000.0 ** (np.arange(half, dtype=np.float32) / half))
    pos = np.arange(core * T, (core + 1) * T, dtype=np.float32)
    ang = pos[:, None] * inv[None, :]
    return np.ascontiguousarray(
        np.concatenate([np.cos(ang).T, np.sin(ang).T], axis=0).astype(np.float32))


def host_masks():
    m = np.zeros((128, 4, 512), dtype=BF)
    for mm_ in range(4):
        for n in range(4):
            if n > mm_:
                m[:, mm_, n * 128:(n + 1) * 128] = 1
            elif n == mm_:
                tri = (np.arange(128)[:, None] <= np.arange(128)[None, :])
                m[:, mm_, n * 128:(n + 1) * 128] = tri.astype(BF)
    return np.ascontiguousarray(m.reshape(128, 4 * 512))


def prep_attn1_weights(inp):
    return {
        "wq1_blk": wblocks_io(np.asarray(inp["L1_attn_Wq"], np.float32)),
        "wk1_blk": wblocks_io(np.asarray(inp["L1_attn_Wk"], np.float32)),
        "wv1_blk": wblocks_io(np.asarray(inp["L1_attn_Wv"], np.float32)),
        "wo1_blk": wblocks_io(np.asarray(inp["L1_attn_Wo"], np.float32)),
    }


def run_attn1(x_full, inp, trace=False):
    consts = host_consts()
    wm = prep_attn1_weights(inp)
    pr = build_qkv1_prog()
    in_maps = []
    for c in range(NCORES):
        shard = np.ascontiguousarray(np.asarray(x_full, np.float32)[c * T:(c + 1) * T, :].T)
        m = {"x_in": shard, "wq1_blk": wm["wq1_blk"], "wk1_blk": wm["wk1_blk"],
             "wv1_blk": wm["wv1_blk"], "cs_in": host_cos_sin(c)}
        m.update(consts)
        in_maps.append(m)
    res = run_bass_kernel_spmd(pr.nc, in_maps, core_ids=list(range(NCORES)))
    pr.close()
    qkv_all = np.concatenate([res.results[c]["qkvb"] for c in range(NCORES)], axis=0)

    masks = host_masks()
    pr = build_attn1core_prog()
    in_maps = []
    for c in range(NCORES):
        q_in = np.empty((512, 8 * T), dtype=BF)
        k_in = np.empty((128, 8 * T), dtype=BF)
        v_in = np.empty((128, 8 * T), dtype=BF)
        for r in range(NCORES):
            blk = qkv_all[6144 * r:6144 * (r + 1)]
            for j in range(4):
                q_in[j * 128:(j + 1) * 128, r * T:(r + 1) * T] = blk[(4 * c + j) * 128:(4 * c + j + 1) * 128]
            k_in[:, r * T:(r + 1) * T] = blk[4096 + c * 128:4096 + (c + 1) * 128]
            v_in[:, r * T:(r + 1) * T] = blk[5120 + c * 128:5120 + (c + 1) * 128]
        m = {"q_in": q_in, "k_in": k_in, "v_in": v_in, "mask_d": masks}
        m.update(consts)
        in_maps.append(m)
    res = run_bass_kernel_spmd(pr.nc, in_maps, core_ids=list(range(NCORES)))
    pr.close()
    attn_all = np.concatenate([res.results[c]["attn_out"] for c in range(NCORES)], axis=0)

    pr = build_outproj1_prog()
    in_maps = []
    for c in range(NCORES):
        shard = np.ascontiguousarray(np.asarray(x_full, np.float32)[c * T:(c + 1) * T, :].T)
        m = {"x_in": shard, "attn_in": np.ascontiguousarray(attn_all[:, c * T:(c + 1) * T]),
             "wo1_blk": wm["wo1_blk"]}
        m.update(consts)
        in_maps.append(m)
    res = run_bass_kernel_spmd(pr.nc, in_maps, core_ids=list(range(NCORES)), trace=trace)
    pr.close()
    out = np.empty((NCORES * T, D), np.float32)
    for c in range(NCORES):
        out[c * T:(c + 1) * T, :] = res.results[c]["x_out"].T
    return out, res





def build_fused_prog():
    p = Prog().open()
    x_in = p.dram_in("x_in", [D, T], F32)
    x_out = p.dram_out("x_out", [D, T], F32)
    p.dram_in("myrank", [1, 1], I32)
    decl_ffn0_inputs(p)
    decl_attn0_inputs(p)
    decl_ffn1_inputs(p)
    decl_qkv1_inputs(p)
    p.dram_in("mask_d", [128, 4 * 512], BF16)
    p.dram_in("wo1_blk", [DC, 128, D], F32)
    xres = p.dram_int("xres", [D, T], F32)
    p.dram_int("qkv_all", [NCORES * 6144, T], BF16)
    p.H["attnb"] = p.dram_int("attnb", [512, 8 * T], BF16)
    p.dram_int("attn_all", [4096, 8 * T], BF16)
    p.dram_int("q_mine", [512, 8 * T], BF16)
    p.begin()
    S = alloc_common(p)

    with p.nc.named_scope("s1_ffn0"):
        emit_ffn0(p, S, x_in, xres)
        emit_barrier(p)
        p.flush()
    with p.nc.named_scope("s2_attn0"):
        emit_attn0(p, S, xres, xres)
        emit_barrier(p)
        p.flush()
    with p.nc.named_scope("s3_ffn0"):
        emit_ffn0(p, S, xres, xres)
        emit_barrier(p)
        p.flush()
    with p.nc.named_scope("s4_ffn1"):
        emit_ffn1(p, S, xres, xres, dequant=True)
        emit_barrier(p)
        p.flush()
    with p.nc.named_scope("s5a_qkv"):
        qkv_done = emit_qkv1(p, S, xres)
    ag = {}

    def f_ag1(eng):
        W(p, eng, "sp", qkv_done())
        ag["1"] = p.inc(eng.collective_compute(
            "AllGather", ALU.bypass, replica_groups=[list(range(NCORES))],
            ins=[p.H["qkvb"].opt()], outs=[p.H["qkv_all"].opt()]), "gp", 1)
    p.gp(f_ag1)
    emit_barrier(p)
    p.flush()
    with p.nc.named_scope("s5b_core"):
        attn_done = emit_attn1core(p, S, gate=(("gp", lambda: ag["1"]),), fused=True)

    def f_ag2(eng):
        W(p, eng, "sp", attn_done())
        ag["2"] = p.inc(eng.collective_compute(
            "AllGather", ALU.bypass, replica_groups=[list(range(NCORES))],
            ins=[p.H["attnb"].opt()], outs=[p.H["attn_all"].opt()]), "gp", 1)
    p.gp(f_ag2)
    emit_barrier(p)
    p.flush()
    with p.nc.named_scope("s5c_oproj"):
        emit_outproj1(p, S, xres, xres, gate=(("gp", lambda: ag["2"]),), fused=True)
        emit_barrier(p)
        p.flush()
    with p.nc.named_scope("s6_ffn1"):
        emit_ffn1(p, S, xres, x_out, dequant=False)
    final_wait(p)
    p.flush()
    return p


_FUSED_CACHE = {}


def kernel(**inputs):
    inp = {k: np.asarray(v) for k, v in inputs.items()}
    x = np.asarray(inp["L0_attn_hidden_states"], np.float32).reshape(8192, 4096)
    if "prog" not in _FUSED_CACHE:
        _FUSED_CACHE["prog"] = build_fused_prog()
    pr = _FUSED_CACHE["prog"]
    wm = {}
    wm.update(prep_ffn0_inputs(inp))
    wm.update(prep_attn0_inputs(inp))
    wm.update(prep_ffn1_inputs(inp))
    wm.update(prep_attn1_weights(inp))
    wm["mask_d"] = host_masks()
    wm.update(host_consts())
    in_maps = []
    for c in range(NCORES):
        m = dict(wm)
        m["x_in"] = np.ascontiguousarray(x[c * T:(c + 1) * T, :].T)
        m["myrank"] = np.array([[c]], np.int32)
        m["cs_in"] = host_cos_sin(c)
        in_maps.append(m)
    res = run_bass_kernel_spmd(pr.nc, in_maps, core_ids=list(range(NCORES)))
    out = np.empty((NCORES * T, D), np.float32)
    for c in range(NCORES):
        out[c * T:(c + 1) * T, :] = res.results[c]["x_out"].T
    return out.reshape(1, 8192, 4096)

